# version 07
# baseline (speedup 1.0000x reference)
import numpy as np
import ml_dtypes
from contextlib import ExitStack

import concourse.bass as bass
import concourse.mybir as mybir
from concourse.bass_utils import run_bass_kernel_spmd

F32 = mybir.dt.float32
BF16 = mybir.dt.bfloat16
ALU = mybir.AluOpType
AF = mybir.ActivationFunctionType

NCORES = 8
C = 2048
NCC = 16
H = 32
HS = 64
EXP = 8192
NET = EXP // 128
LC = 64
TAU = 512
TAUC = 256
NU_CLAMP = 11.0
GN_EPS = 1e-5 * 64.0
LN_EPS = 1e-5
DMIX = 32
DDEC = 64

ENGINES = ("sync", "gpsimd", "vector", "scalar", "tensor")



def _vec_fm(v):
    return np.ascontiguousarray(np.asarray(v, np.float32).reshape(NCC, 128).T)


def _host_constants():
    t = np.arange(TAU)
    consts = {
        "m8": np.broadcast_to((t[:TAUC] % 8 != 0).astype(np.float32), (128, TAUC)).copy(),
        "m32": np.broadcast_to((t[:TAUC] % 32 != 0).astype(np.float32), (128, TAUC)).copy(),
        "m64": np.broadcast_to((t[:TAUC] % LC != 0).astype(np.float32), (128, TAUC)).copy(),
    }
    s = np.arange(LC)[:, None]
    tt = np.arange(LC)[None, :]
    bs, bt = s // 8, tt // 8
    pats = {
        "MA": (bs == bt) & (s < tt),
        "MB": (bt - bs) == 1,
        "MC": (bt - bs) == 2,
        "MD": (bt - bs) == 3,
        "ME": ((bt - bs) >= 4) & (s < 32) & (tt >= 32),
        "MF": (s == tt),
    }
    valid = np.zeros((LC, LC), bool)
    for name, m in pats.items():
        if name == "MA":
            consts[name] = np.tile(m.astype(np.float32), (2, 8)).astype(ml_dtypes.bfloat16)
        else:
            consts[name] = np.tile(m.astype(np.uint8), (2, 8))
        valid |= m
    consts["ident"] = np.eye(128).astype(ml_dtypes.bfloat16)
    go = np.zeros((128, NCC, 32), np.float32)
    for cc in range(NCC):
        go[0:64, cc, 2 * cc] = 1.0
        go[64:128, cc, 2 * cc + 1] = 1.0
    consts["gn_ones"] = go
    consts["gn_ones_bf"] = go.astype(ml_dtypes.bfloat16)
    gs = np.zeros((32, NCC, 128), np.float32)
    for cc in range(NCC):
        gs[2 * cc, cc, 0:64] = 1.0
        gs[2 * cc + 1, cc, 64:128] = 1.0
    consts["gn_sel"] = gs.astype(ml_dtypes.bfloat16)
    consts["ones_col"] = np.ones((128, 1), np.float32)
    consts["ones_col_bf"] = np.ones((128, 1), np.float32).astype(ml_dtypes.bfloat16)
    consts["ones_row"] = np.ones((1, 128), np.float32)
    return consts


def prep_inputs(inp):
    x = np.asarray(inp["L0_attn_hidden"], np.float32).reshape(-1, C)
    t_total = x.shape[0]
    t_core = t_total // NCORES
    consts = _host_constants()

    def T(a):
        return np.ascontiguousarray(np.asarray(a, np.float32).T)

    wu_t = T(inp["L0_ffn_W_up"])
    wgate_t = T(inp["L0_ffn_W_gate"])
    wd_t = T(inp["L0_ffn_W_down"])

    shared = {
        "wr_t": T(inp["L0_attn_W_r"]), "wk_t": T(inp["L0_attn_W_k"]),
        "wv_t": T(inp["L0_attn_W_v"]), "wg_t": T(inp["L0_attn_W_g"]),
        "wo_t": T(inp["L0_attn_W_o"]),
        "wu_t": wu_t, "wgate_t": wgate_t, "wd_t": wd_t,
        "w1": np.ascontiguousarray(np.asarray(inp["L0_attn_time_maa_w1"], np.float32)),
        "w2": np.ascontiguousarray(np.asarray(inp["L0_attn_time_maa_w2"], np.float32)),
        "w2p": np.ascontiguousarray(
            np.asarray(inp["L0_attn_time_maa_w2"], np.float32)[0:4]
            .reshape(4, DMIX, NCC, 128).transpose(0, 1, 2, 3).reshape(128, NCC, 128)),
        "dw1": np.ascontiguousarray(np.asarray(inp["L0_attn_time_decay_w1"], np.float32)),
        "dw2": np.ascontiguousarray(
            np.asarray(inp["L0_attn_time_decay_w2"], np.float32)
            .reshape(DDEC, 8, 2, 128).transpose(2, 0, 1, 3).reshape(128, 8, 128)),
        "vecs": np.stack([
            _vec_fm(np.reshape(np.asarray(inp["L0_attn_time_maa_w"], np.float32), -1)),
            _vec_fm(np.reshape(np.asarray(inp["L0_attn_time_maa_k"], np.float32), -1)),
            _vec_fm(np.reshape(np.asarray(inp["L0_attn_time_maa_v"], np.float32), -1)),
            _vec_fm(np.reshape(np.asarray(inp["L0_attn_time_maa_r"], np.float32), -1)),
            _vec_fm(np.reshape(np.asarray(inp["L0_attn_time_maa_g"], np.float32), -1)),
            _vec_fm(np.reshape(np.asarray(inp["L0_attn_time_maa_x"], np.float32), -1)),
            _vec_fm(np.reshape(np.asarray(inp["L0_attn_time_decay"], np.float32), -1)),
            _vec_fm(np.reshape(np.asarray(inp["L0_attn_time_faaaa"], np.float32), -1)),
            _vec_fm(np.reshape(np.asarray(inp["L0_attn_ln_gamma"], np.float32), -1)),
            _vec_fm(np.reshape(np.asarray(inp["L0_attn_ln_beta"], np.float32), -1)),
        ], axis=0),
    }
    shared.update(consts)

    in_maps = []
    rows_ug = C // NCORES
    rows_d = EXP // NCORES
    for r in range(NCORES):
        d = dict(shared)
        blk = x[r * t_core:(r + 1) * t_core]
        halo = x[r * t_core - 1] if r > 0 else np.zeros(C, np.float32)
        fm = np.concatenate([halo[:, None], blk.T], axis=1)
        d["hid_fm"] = np.ascontiguousarray(fm.reshape(NCC, 128, t_core + 1))
        rm = np.zeros((128, 8), np.float32)
        rm[:, :r] = 1.0
        d["rankmask"] = rm
        d["qs_up"] = np.ascontiguousarray(wu_t[r * rows_ug:(r + 1) * rows_ug])
        d["qs_gate"] = np.ascontiguousarray(wgate_t[r * rows_ug:(r + 1) * rows_ug])
        d["qs_down"] = np.ascontiguousarray(wd_t[r * rows_d:(r + 1) * rows_d])
        in_maps.append(d)
    return in_maps, t_core



class SemCtr:
    def __init__(self, h):
        self.h = h
        self.v = 0

    def bump(self, n=1):
        self.v += n
        return self.v


def build_program(t_core, upto=None, dbg=()):
    nc = bass.Bass("TRN2", target_bir_lowering=False, debug=False)
    ntau = t_core // TAU
    ntc = t_core // TAUC
    nch_tc = TAUC // LC

    D = {}

    def dram(name, shape, dtype, kind="Internal"):
        if kind == "Internal" and name in dbg:
            kind = "ExternalOutput"
        D[name] = nc.dram_tensor(name, shape, dtype, kind=kind).ap()
        return D[name]

    dram("hid_fm", [NCC, 128, t_core + 1], F32, "ExternalInput")
    for n, shp in [("wr_t", [C, C]), ("wk_t", [C, C]), ("wv_t", [C, C]),
                   ("wg_t", [C, C]), ("wo_t", [C, C]),
                   ("wu_t", [C, EXP]), ("wgate_t", [C, EXP]), ("wd_t", [EXP, C]),
                   ("w1", [C, 160]), ("w2", [5, DMIX, C]), ("w2p", [128, NCC, 128]),
                   ("dw1", [C, DDEC]), ("dw2", [128, 8, 128]),
                   ("vecs", [10, 128, NCC]),
                   ("m8", [128, TAUC]), ("m32", [128, TAUC]), ("m64", [128, TAUC]),
                   ("gn_ones", [128, NCC, 32]),
                   ("ones_col", [128, 1]), ("ones_row", [1, 128]),
                   ("rankmask", [128, 8]),
                   ("qs_up", [C // NCORES, EXP]), ("qs_gate", [C // NCORES, EXP]),
                   ("qs_down", [EXP // NCORES, C])]:
        dram(n, shp, F32, "ExternalInput")
    dram("MA", [128, 512], BF16, "ExternalInput")
    for n in ["MB", "MC", "MD", "ME", "MF"]:
        dram(n, [128, 512], mybir.dt.uint8, "ExternalInput")
    dram("ident", [128, 128], BF16, "ExternalInput")
    dram("gn_sel", [32, NCC, 128], BF16, "ExternalInput")
    dram("gn_ones_bf", [128, NCC, 32], BF16, "ExternalInput")
    dram("ones_col_bf", [128, 1], BF16, "ExternalInput")

    dram("out_fm", [NCC, 128, t_core], F32, "ExternalOutput")

    dram("xf_sp", [5, NCC, 128, t_core], BF16)
    dram("r_sp", [NCC, 128, t_core], BF16)
    dram("k_sp", [NCC, 128, t_core], BF16)
    dram("g_sp", [NCC, 128, t_core], BF16)
    dram("vtm_sp", [t_core // 128, 128, C], BF16)
    dram("vtm2_sp", [t_core // 128, 128, C], BF16)
    dram("nu_sp", [NCC, 128, t_core], F32)
    dram("o_sp", [NCC, 128, t_core], F32)
    dram("ogn_sp", [NCC, 128, t_core], BF16)
    dram("x1f_sp", [NCC, 128, t_core], F32)
    dram("x1b_sp", [NCC, 128, t_core], BF16)
    dram("h_sp", [NET, 128, t_core], BF16)
    dram("wdq_sp", [NCC, 128, EXP // 128, 128], BF16)
    dram("sc_in", [1, 4], F32)
    dram("sc_out", [NCORES, 1, 4], F32)
    dram("cc_in", [128, 1040], F32)
    dram("cc_out", [NCORES, 128, 1040], F32)

    ctx = ExitStack()

    def sbuf(name, shape, dtype):
        return ctx.enter_context(nc.sbuf_tensor(name, shape, dtype))

    def psumt(name, shape):
        return ctx.enter_context(nc.psum_tensor(name, shape, F32))

    def mksem(name):
        return SemCtr(ctx.enter_context(nc.semaphore(name)))

    SB = {}
    SB["vecs"] = sbuf("sb_vecs", [128, 10, NCC], F32)
    for n in ["m8", "m32", "m64"]:
        SB[n] = sbuf("sb_" + n, [128, TAUC], F32)
    SB["MA"] = sbuf("sb_MA", [128, 512], BF16)
    for n in ["MB", "MC", "MD", "ME", "MF"]:
        SB[n] = sbuf("sb_" + n, [128, 512], mybir.dt.uint8)
    SB["ident"] = sbuf("sb_ident", [128, 128], BF16)
    SB["gn_ones_bf"] = sbuf("sb_gnonesb", [128, NCC, 32], BF16)
    SB["gn_sel"] = sbuf("sb_gnsel", [DMIX, NCC, 128], BF16)
    SB["ones_col"] = sbuf("sb_onescol", [128, 1], F32)
    SB["ones_col_bf"] = sbuf("sb_onescolb", [128, 1], BF16)
    SB["ones_row"] = sbuf("sb_onesrow", [1, 128], F32)
    SB["rankmask"] = sbuf("sb_rankmask", [128, 8], F32)
    SB["w1"] = sbuf("sb_w1", [128, NCC, 160], BF16)
    SB["w2"] = sbuf("sb_w2", [128, NCC, 128], BF16)
    SB["w2g"] = sbuf("sb_w2g", [DMIX, NCC, 128], BF16)
    SB["dw1"] = sbuf("sb_dw1", [128, NCC, DDEC], BF16)
    SB["dw2"] = sbuf("sb_dw2", [128, NCC // 2, 128], BF16)
    SB["S"] = sbuf("sb_S", [128, NCC, HS], F32)
    SB["Sb"] = sbuf("sb_Sb", [128, NCC, HS], BF16)
    SB["Sinb"] = sbuf("sb_Sinb", [128, NCC, HS], BF16)
    SB["hs"] = sbuf("sb_hs", [128, 8], F32)
    SB["Gfin"] = sbuf("sb_Gfin", [128, NCC], F32)
    SB["Pcl"] = sbuf("sb_Pcl", [128, 512], BF16)
    SB["scl"] = sbuf("sb_scl", [1, 40], F32)
    SB["eps"] = sbuf("sb_eps", [128, 2], F32)

    BW = sbuf("ar_BW", [128, 32768], BF16)
    B2 = sbuf("ar_B2", [128, 16384], BF16)
    B3 = sbuf("ar_B3", [128, 8192], BF16)
    F1 = sbuf("ar_F1", [128, 12288], F32)
    F3 = sbuf("ar_F3", [128, 2048], F32)

    PB = [psumt(f"pb{i}", [128, 512]) for i in range(8)]

    s_ds = mksem("s_ds")
    s_dx = mksem("s_dx")
    s_dg = mksem("s_dg")
    s_v = mksem("s_v")
    s_sc = mksem("s_sc")
    s_t = mksem("s_t")
    s_g = mksem("s_g")
    ALLSEMS = [s_ds, s_dx, s_dg, s_v, s_sc, s_t, s_g]

    steps = {e: [] for e in ENGINES}

    def em(engine, fn):
        steps[engine].append(fn)

    def fence():
        tgt = [(s.h, s.v) for s in ALLSEMS if s.v > 0]
        for e in ENGINES:
            def f(eng, tgt=tuple(tgt)):
                for h, v in tgt:
                    eng.wait_ge(h, v)
            em(e, f)

    VEC = {"maa_w": 0, "maa_k": 1, "maa_v": 2, "maa_r": 3, "maa_g": 4,
           "maa_x": 5, "td": 6, "u": 7, "ln_g": 8, "ln_b": 9}

    def vcol(name, cc):
        return SB["vecs"][:, VEC[name], cc:cc + 1]

    def spl(name, a, b, f=None):
        if f is None:
            return D[name][:, :, a:b].transpose([1, 0, 2])
        return D[name][f, :, :, a:b].transpose([1, 0, 2])

    def s3(ap, t=None):
        n = ap.shape[1]
        t = t if t is not None else n // NCC
        return ap.rearrange("p (c t) -> p c t", t=t)

    def ph_setup():
        def f(s):
            s.dma_start(out=SB["vecs"][:], in_=D["vecs"].rearrange("v p c -> p v c")).then_inc(s_ds.h, 16)
            for n in ["m8", "m32", "m64", "MA", "MB", "MC", "MD", "ME", "MF",
                      "ident", "ones_col", "ones_row", "rankmask", "gn_sel"]:
                s.dma_start(out=SB[n][:], in_=D[n]).then_inc(s_ds.h, 16)
            s.dma_start(out=SB["gn_ones_bf"][:], in_=D["gn_ones_bf"]).then_inc(s_ds.h, 16)
            s.dma_start(out=SB["ones_col_bf"][:], in_=D["ones_col_bf"]).then_inc(s_ds.h, 16)
        em("sync", f)
        s_ds.bump(16 * 17)

        def g(g):
            g.dma_start(out=SB["w1"][:], in_=D["w1"].rearrange("(cc p) f -> p cc f", p=128)).then_inc(s_dg.h, 16)
            g.dma_start(out=SB["w2"][:], in_=D["w2p"]).then_inc(s_dg.h, 16)
            g.dma_start(out=SB["w2g"][:], in_=D["w2"][4].rearrange("d (cc m) -> d cc m", m=128)).then_inc(s_dg.h, 16)
            g.dma_start(out=SB["dw1"][:], in_=D["dw1"].rearrange("(cc p) f -> p cc f", p=128)).then_inc(s_dg.h, 16)
            g.dma_start(out=SB["dw2"][:], in_=D["dw2"]).then_inc(s_dg.h, 16)
        em("gpsimd", g)
        s_dg.bump(16 * 5)

        def v(v):
            v.memset(SB["S"][:], 0.0)
            v.memset(SB["Sb"][:], 0.0)
            v.memset(SB["Sinb"][:], 0.0)
            v.memset(SB["Gfin"][:], 1.0)
            v.memset(SB["scl"][:], 0.0)
            v.memset(F3[:, 2044:2048], 0.0)
            v.memset(SB["eps"][:, 0:1], GN_EPS)
            v.memset(SB["eps"][:, 1:2], LN_EPS)
            v.drain()
            v.nop().then_inc(s_v.h, 1)
        em("vector", v)
        s_v.bump(1)

    def ph_scale():
        slabs = [("qs_up", 0, 2, 8192), ("qs_gate", 1, 2, 8192), ("qs_down", 2, 8, 2048)]
        for wname, wi, nsl, cols in slabs:
            for sl in range(nsl):
                prev_sc = s_sc.v

                def fl(s, wname=wname, sl=sl, cols=cols, prev_sc=prev_sc):
                    if prev_sc > 0:
                        s.wait_ge(s_sc.h, prev_sc)
                    s.dma_start(out=F1[:, 0:cols],
                                in_=D[wname][128 * sl:128 * (sl + 1), :]).then_inc(s_ds.h, 16)
                em("sync", fl)
                ld_t = s_ds.bump(16)

                prev_fv = s_v.v

                def fs(sc, wi=wi, sl=sl, cols=cols, ld_t=ld_t, prev_fv=prev_fv):
                    sc.wait_ge(s_ds.h, ld_t)
                    if prev_fv > 0:
                        sc.wait_ge(s_v.h, prev_fv)
                    sc.activation(out=F1[:, 0:cols], in_=F1[:, 0:cols], func=AF.Abs,
                                  accum_out=F3[:, 4 + wi:5 + wi]).then_inc(s_sc.h, 1)
                em("scalar", fs)
                sc_t = s_sc.bump(1)

                def fv(v, wi=wi, sl=sl, sc_t=sc_t):
                    v.wait_ge(s_sc.h, sc_t)
                    if sl == 0:
                        v.tensor_copy(out=F3[:, wi:wi + 1], in_=F3[:, 4 + wi:5 + wi])
                    else:
                        v.tensor_add(out=F3[:, wi:wi + 1], in0=F3[:, wi:wi + 1],
                                     in1=F3[:, 4 + wi:5 + wi])
                    v.drain()
                    v.nop().then_inc(s_v.h, 1)
                em("vector", fv)
                s_v.bump(1)
        vt = s_v.v

        def ft(t, vt=vt):
            t.wait_ge(s_v.h, vt)
            t.matmul(PB[0][0:1, 0:3], SB["ones_col"][:], F3[:, 0:3],
                     start=True, stop=True).then_inc(s_t.h, 1)
        em("tensor", ft)
        tt = s_t.bump(1)

        def fv2(v, tt=tt):
            v.wait_ge(s_t.h, tt)
            v.tensor_copy(out=SB["scl"][0:1, 0:3], in_=PB[0][0:1, 0:3])
            v.drain()
            v.nop().then_inc(s_v.h, 1)
        em("vector", fv2)
        vt2 = s_v.bump(1)

        dg0 = s_dg.v

        GWAIT1 = s_g.v + 1

        def fg(g, vt2=vt2, dg0=dg0, GWAIT1=GWAIT1):
            g.wait_ge(s_v.h, vt2)
            g.dma_start(out=D["sc_in"], in_=SB["scl"][0:1, 0:4]).then_inc(s_dg.h, 16)
            g.wait_ge(s_dg.h, dg0 + 16)
            g.collective_compute(
                "AllGather", ALU.bypass, replica_groups=[list(range(NCORES))],
                ins=[D["sc_in"]], outs=[D["sc_out"]],
            ).then_inc(s_g.h, 1)
            g.wait_ge(s_g.h, GWAIT1)
            g.dma_start(out=SB["scl"][0:1, 8:40].rearrange("p (r f) -> p r f", r=NCORES),
                        in_=D["sc_out"].transpose([1, 0, 2])).then_inc(s_dg.h, 16)
        em("gpsimd", fg)
        s_dg.bump(32)
        s_g.bump(1)
        dg_t = s_dg.v

        def fv3(v, dg_t=dg_t):
            v.wait_ge(s_dg.h, dg_t)
            for wi, numel in [(0, C * EXP), (1, C * EXP), (2, C * EXP)]:
                src = bass.AP(tensor=SB["scl"], offset=8 + wi, ap=[[40, 1], [4, 8]])
                v.tensor_reduce(out=SB["scl"][0:1, 4 + wi:5 + wi], in_=src,
                                axis=mybir.AxisListType.X, op=ALU.add)
                v.drain()
            v.tensor_scalar(out=SB["scl"][0:1, 0:3], in0=SB["scl"][0:1, 4:7],
                            scalar1=0.5 / float(C * EXP), scalar2=0.5e-8,
                            op0=ALU.mult, op1=ALU.add)
            v.drain()
            v.tensor_scalar_mul(out=SB["scl"][0:1, 4:7], in0=SB["scl"][0:1, 0:3],
                                scalar1=-1.0)
            v.drain()
            v.nop().then_inc(s_v.h, 1)
        em("vector", fv3)
        vt3 = s_v.bump(1)

        def ft2(t, vt3=vt3):
            t.wait_ge(s_v.h, vt3)
            t.matmul(PB[1][:, 0:8], SB["ones_row"][:], SB["scl"][0:1, 0:8],
                     start=True, stop=True).then_inc(s_t.h, 1)
        em("tensor", ft2)
        tt2 = s_t.bump(1)

        def fv4(v, tt2=tt2):
            v.wait_ge(s_t.h, tt2)
            v.tensor_copy(out=SB["hs"][:, 0:8], in_=PB[1][:, 0:8])
            v.drain()
            v.nop().then_inc(s_g.h, 1)
        em("vector", fv4)
        s_g.bump(1)

    def ph_A():
        hid = F1
        prev_st = 0
        for tau in range(ntau):
            prev_v = s_v.v

            def fl(s, tau=tau, prev_v=prev_v):
                if tau > 0:
                    s.wait_ge(s_v.h, prev_v)
                for cc in range(NCC):
                    s.dma_start(out=hid[:, 513 * cc:513 * (cc + 1)],
                                in_=D["hid_fm"][cc, :, TAU * tau:TAU * tau + 513]).then_inc(s_ds.h, 16)
            em("sync", fl)
            ld_t = s_ds.bump(16 * NCC)

            def fv(v, ld_t=ld_t):
                v.wait_ge(s_ds.h, ld_t)
                for cc in range(NCC):
                    v.tensor_sub(out=B2[:, 512 * cc:512 * (cc + 1)],
                                 in0=hid[:, 513 * cc:513 * cc + 512],
                                 in1=hid[:, 513 * cc + 1:513 * cc + 513])
                v.drain()
                for cc in range(NCC):
                    v.scalar_tensor_tensor(
                        out=B2[:, 8192 + 512 * cc:8192 + 512 * (cc + 1)],
                        in0=B2[:, 512 * cc:512 * (cc + 1)], scalar=vcol("maa_x", cc),
                        in1=hid[:, 513 * cc + 1:513 * cc + 513], op0=ALU.mult, op1=ALU.add)
                v.drain()
                v.nop().then_inc(s_v.h, 1)
            em("vector", fv)
            v_t = s_v.bump(1)

            def ft(t, v_t=v_t):
                t.wait_ge(s_v.h, v_t)
                for cc in range(NCC):
                    t.matmul(PB[0][:, :], SB["w1"][:, cc, 0:128],
                             B2[:, 8192 + 512 * cc:8192 + 512 * (cc + 1)],
                             start=(cc == 0), stop=(cc == 15))
                for cc in range(NCC):
                    ins = t.matmul(PB[1][0:32, :], SB["w1"][:, cc, 128:160],
                                   B2[:, 8192 + 512 * cc:8192 + 512 * (cc + 1)],
                                   start=(cc == 0), stop=(cc == 15))
                ins.then_inc(s_t.h, 1)
            em("tensor", ft)
            t_t = s_t.bump(1)

            def fs(sc, t_t=t_t):
                sc.wait_ge(s_t.h, t_t)
                sc.activation(out=BW[:, 0:512], in_=PB[0][:, :], func=AF.Tanh)
                sc.activation(out=BW[0:32, 512:1024], in_=PB[1][0:32, :],
                              func=AF.Tanh).then_inc(s_sc.h, 1)
            em("scalar", fs)
            sc_t = s_sc.bump(1)

            vmix0 = s_v.v
            for f in range(5):
                for cc in range(NCC):
                    idx = f * NCC + cc
                    bank = PB[2 + (idx % 2)]
                    wait_v = vmix0 + idx - 1 if idx >= 2 else None

                    def ftm(t, f=f, cc=cc, bank=bank, sc_t=sc_t, wait_v=wait_v):
                        if f == 0 and cc == 0:
                            t.wait_ge(s_sc.h, sc_t)
                        if wait_v is not None:
                            t.wait_ge(s_v.h, wait_v)
                        zsl = (BW[32 * f:32 * (f + 1), 0:512] if f < 4
                               else BW[0:32, 512:1024])
                        w2sl = (SB["w2"][32 * f:32 * (f + 1), cc, :] if f < 4
                                else SB["w2g"][:, cc, :])
                        tp = (32 * f, 0) if f < 4 else (0, 0)
                        t.matmul(bank[:, :], w2sl, zsl,
                                 start=True, stop=True,
                                 tile_position=tp).then_inc(s_t.h, 1)
                    em("tensor", ftm)
                    mm_t = s_t.bump(1)

                    def fvm(v, f=f, cc=cc, bank=bank, mm_t=mm_t):
                        v.wait_ge(s_t.h, mm_t)
                        v.scalar_tensor_tensor(
                            out=F3[:, 0:512], in0=bank[:, :],
                            scalar=vcol("maa_" + "wkvrg"[f], cc),
                            in1=B2[:, 512 * cc:512 * (cc + 1)],
                            op0=ALU.add, op1=ALU.mult)
                        v.drain()
                        v.tensor_add(out=B3[:, 512 * cc:512 * (cc + 1)],
                                     in0=F3[:, 0:512],
                                     in1=hid[:, 513 * cc + 1:513 * cc + 513])
                        v.drain()
                        v.nop().then_inc(s_v.h, 1)
                    em("vector", fvm)
                    s_v.bump(1)
                v_done = s_v.v

                def fst(s, f=f, tau=tau, v_done=v_done):
                    s.wait_ge(s_v.h, v_done)
                    s.dma_start(out=spl("xf_sp", TAU * tau, TAU * (tau + 1), f),
                                in_=s3(B3[:, 0:8192])).then_inc(s_ds.h, 16)
                em("sync", fst)
                st_done = s_ds.bump(16)

                if f < 4:
                    def fvw(v, st_done=st_done):
                        v.wait_ge(s_ds.h, st_done)
                    em("vector", fvw)
                else:
                    prev_st = st_done

    def ph_B():
        for wname, fidx, spname, act in [
                ("wr_t", 3, "r_sp", None), ("wk_t", 1, "k_sp", None),
                ("wg_t", 4, "g_sp", "silu")]:
            prev_t = s_t.v

            def flw(g, wname=wname, prev_t=prev_t):
                g.wait_ge(s_t.h, prev_t)
                for cc in range(NCC):
                    g.dma_start(out=BW[:, 2048 * cc:2048 * (cc + 1)],
                                in_=D[wname][128 * cc:128 * (cc + 1), :]).then_inc(s_dg.h, 16)
            em("gpsimd", flw)
            w_t = s_dg.bump(16 * NCC)

            for tau in range(ntau):
                prev_mm = s_t.v

                def flx(s, fidx=fidx, tau=tau, prev_mm=prev_mm):
                    if tau > 0:
                        s.wait_ge(s_t.h, prev_mm)
                    s.dma_start(out=s3(B2[:, 0:8192]),
                                in_=spl("xf_sp", TAU * tau, TAU * (tau + 1), fidx)).then_inc(s_ds.h, 16)
                em("sync", flx)
                x_t = s_ds.bump(16)

                sc0 = s_sc.v
                prev_fso = s_ds.v
                for mt in range(NCC):
                    bank = PB[mt % 4]
                    wait_sc = sc0 + mt - 3 if mt >= 4 else None

                    def ftp(t, mt=mt, bank=bank, w_t=w_t, x_t=x_t, wait_sc=wait_sc):
                        if mt == 0:
                            t.wait_ge(s_dg.h, w_t)
                            t.wait_ge(s_ds.h, x_t)
                        if wait_sc is not None:
                            t.wait_ge(s_sc.h, wait_sc)
                        for cc in range(NCC):
                            ins = t.matmul(bank[:, :],
                                           BW[:, 2048 * cc + 128 * mt:2048 * cc + 128 * (mt + 1)],
                                           B2[:, 512 * cc:512 * (cc + 1)],
                                           start=(cc == 0), stop=(cc == 15))
                        ins.then_inc(s_t.h, 1)
                    em("tensor", ftp)
                    mm_t = s_t.bump(1)

                    def fse(sc, mt=mt, bank=bank, mm_t=mm_t, act=act, x_t=x_t):
                        sc.wait_ge(s_t.h, mm_t)
                        if mt == 0:
                            sc.wait_ge(s_ds.h, x_t)
                        fn = AF.Silu if act == "silu" else AF.Copy
                        sc.activation(out=B2[:, 8192 + 512 * mt:8192 + 512 * (mt + 1)],
                                      in_=bank[:, :], func=fn).then_inc(s_sc.h, 1)
                    em("scalar", fse)
                    s_sc.bump(1)
                e_t = s_sc.v

                def fso(s, spname=spname, tau=tau, e_t=e_t):
                    s.wait_ge(s_sc.h, e_t)
                    s.dma_start(out=spl(spname, TAU * tau, TAU * (tau + 1)),
                                in_=s3(B2[:, 8192:16384])).then_inc(s_ds.h, 16)
                em("sync", fso)
                s_ds.bump(16)

        prev_t = s_t.v

        def flw2(g, prev_t=prev_t):
            g.wait_ge(s_t.h, prev_t)
            for cc in range(NCC):
                g.dma_start(out=BW[:, 2048 * cc:2048 * (cc + 1)],
                            in_=D["wv_t"][128 * cc:128 * (cc + 1), :]).then_inc(s_dg.h, 16)
        em("gpsimd", flw2)
        w_t = s_dg.bump(16 * NCC)

        for tau in range(ntau):
            prev_mm = s_t.v

            def flx(s, tau=tau, prev_mm=prev_mm):
                if tau > 0:
                    s.wait_ge(s_t.h, prev_mm)
                s.dma_start(out=s3(B2[:, 0:8192]),
                            in_=spl("xf_sp", TAU * tau, TAU * (tau + 1), 2)).then_inc(s_ds.h, 16)
            em("sync", flx)
            x_t = s_ds.bump(16)

            v0 = s_v.v
            prev_fso2 = s_ds.v
            for tt in range(4):
                for ac in range(4):
                    idx = tt * 4 + ac
                    bank = PB[idx % 4]
                    wait_v = v0 + idx - 3 if idx >= 4 else None

                    def ftv(t, tt=tt, ac=ac, bank=bank, w_t=w_t, x_t=x_t, wait_v=wait_v):
                        if tt == 0 and ac == 0:
                            t.wait_ge(s_dg.h, w_t)
                            t.wait_ge(s_ds.h, x_t)
                        if wait_v is not None:
                            t.wait_ge(s_v.h, wait_v)
                        for cc in range(NCC):
                            ins = t.matmul(bank[:, :],
                                           B2[:, 512 * cc + 128 * tt:512 * cc + 128 * (tt + 1)],
                                           BW[:, 2048 * cc + 512 * ac:2048 * cc + 512 * (ac + 1)],
                                           start=(cc == 0), stop=(cc == 15))
                        ins.then_inc(s_t.h, 1)
                    em("tensor", ftv)
                    mm_t = s_t.bump(1)

                    def fve(v, tt=tt, ac=ac, bank=bank, mm_t=mm_t, x_t=x_t):
                        v.wait_ge(s_t.h, mm_t)
                        if tt == 0 and ac == 0:
                            v.wait_ge(s_ds.h, x_t)
                        v.tensor_copy(out=B3[:, 2048 * tt + 512 * ac:2048 * tt + 512 * (ac + 1)],
                                      in_=bank[:, :])
                        v.drain()
                        v.nop().then_inc(s_v.h, 1)
                    em("vector", fve)
                    s_v.bump(1)
            v_done = s_v.v

            def fso2(s, tau=tau, v_done=v_done):
                s.wait_ge(s_v.h, v_done)
                for tt in range(4):
                    s.dma_start(out=D["vtm_sp"][4 * tau + tt],
                                in_=B3[:, 2048 * tt:2048 * (tt + 1)]).then_inc(s_ds.h, 16)
                for tt in range(4):
                    s.dma_start(out=D["vtm2_sp"][4 * tau + tt][0:64, :],
                                in_=B3[64:128, 2048 * tt:2048 * (tt + 1)]).then_inc(s_ds.h, 16)
                    s.dma_start(out=D["vtm2_sp"][4 * tau + tt][64:128, :],
                                in_=B3[0:64, 2048 * tt:2048 * (tt + 1)]).then_inc(s_ds.h, 16)
            em("sync", fso2)
            s_ds.bump(16 * 12)

        for tau in range(ntau):
            prev_t = s_t.v
            prev_b3 = s_ds.v
            prev_fsn = s_ds.v

            def flx(s, tau=tau, prev_t=prev_t):
                if tau > 0:
                    s.wait_ge(s_t.h, prev_t)
                s.dma_start(out=s3(B2[:, 0:8192]),
                            in_=spl("xf_sp", TAU * tau, TAU * (tau + 1), 0)).then_inc(s_ds.h, 16)
            em("sync", flx)
            x_t = s_ds.bump(16)

            def ftz(t, x_t=x_t):
                t.wait_ge(s_ds.h, x_t)
                for cc in range(NCC):
                    t.matmul(PB[4][0:64, :], SB["dw1"][:, cc, :],
                             B2[:, 512 * cc:512 * (cc + 1)],
                             start=(cc == 0), stop=(cc == 15))
                for cc in range(NCC):
                    ins = t.matmul(PB[4][64:128, :], SB["dw1"][:, cc, :],
                                   B2[:, 512 * cc:512 * (cc + 1)],
                                   start=(cc == 0), stop=(cc == 15),
                                   tile_position=(0, 64))
                ins.then_inc(s_t.h, 1)
            em("tensor", ftz)
            tz = s_t.bump(1)

            def fsz(sc, tz=tz, x_t=x_t):
                sc.wait_ge(s_t.h, tz)
                sc.wait_ge(s_ds.h, x_t)
                sc.activation(out=B3[:, 0:512], in_=PB[4][:, :],
                              func=AF.Tanh).then_inc(s_sc.h, 1)
            em("scalar", fsz)
            scz = s_sc.bump(1)

            v0 = s_v.v
            for cc in range(NCC):
                bank = PB[5 + (cc % 2)]
                wait_v = v0 + cc - 1 if cc >= 2 else None

                def ftd(t, cc=cc, bank=bank, scz=scz, wait_v=wait_v):
                    if cc == 0:
                        t.wait_ge(s_sc.h, scz)
                    if wait_v is not None:
                        t.wait_ge(s_v.h, wait_v)
                    zrow = 64 * (cc % 2)
                    t.matmul(bank[:, :], SB["dw2"][zrow:zrow + 64, cc // 2, :],
                             B3[zrow:zrow + 64, 0:512],
                             start=True, stop=True, tile_position=(zrow, 0)).then_inc(s_t.h, 1)
                em("tensor", ftd)
                td = s_t.bump(1)

                def fsd(sc, cc=cc, bank=bank, td=td, x_t=x_t):
                    sc.wait_ge(s_t.h, td)
                    if cc == 0:
                        sc.wait_ge(s_ds.h, x_t)
                    sc.activation(out=F1[:, 512 * cc:512 * (cc + 1)], in_=bank[:, :],
                                  func=AF.Exp, bias=vcol("td", cc)).then_inc(s_sc.h, 1)
                em("scalar", fsd)
                sd = s_sc.bump(1)

                def fvd(v, cc=cc, sd=sd):
                    v.wait_ge(s_sc.h, sd)
                    v.tensor_scalar_min(out=F1[:, 512 * cc:512 * (cc + 1)],
                                        in0=F1[:, 512 * cc:512 * (cc + 1)],
                                        scalar1=NU_CLAMP)
                    v.drain()
                    v.nop().then_inc(s_v.h, 1)
                em("vector", fvd)
                s_v.bump(1)
            vnu = s_v.v

            def fsn(s, tau=tau, vnu=vnu):
                s.wait_ge(s_v.h, vnu)
                s.dma_start(out=spl("nu_sp", TAU * tau, TAU * (tau + 1)),
                            in_=s3(F1[:, 0:8192])).then_inc(s_ds.h, 16)
            em("sync", fsn)
            s_ds.bump(16)

    FK8, FKA, FKF2, FKF3, FK32, FR32, FR8, FK0 = range(8)

    def bwsl(slot, cc, a=0, b=TAUC):
        return BW[:, 4096 * slot + 256 * cc + a:4096 * slot + 256 * cc + b]

    def ph_C():
        prev_oso = 0
        for tc in range(ntc):
            prev_t = s_t.v
            prev_v = s_v.v

            def fl(s, tc=tc, prev_t=prev_t, prev_v=prev_v):
                if tc > 0:
                    s.wait_ge(s_t.h, prev_t)
                    s.wait_ge(s_v.h, prev_v)
                s.dma_start(out=s3(B2[:, 0:4096]),
                            in_=spl("k_sp", TAUC * tc, TAUC * (tc + 1))).then_inc(s_ds.h, 16)
                s.dma_start(out=s3(B2[:, 4096:8192]),
                            in_=spl("r_sp", TAUC * tc, TAUC * (tc + 1))).then_inc(s_ds.h, 16)
                s.dma_start(out=s3(F1[:, 0:4096]),
                            in_=spl("nu_sp", TAUC * tc, TAUC * (tc + 1))).then_inc(s_ds.h, 16)
                s.dma_start(out=B3[:, 0:2048], in_=D["vtm_sp"][2 * tc]).then_inc(s_ds.h, 16)
                s.dma_start(out=B3[:, 2048:4096], in_=D["vtm_sp"][2 * tc + 1]).then_inc(s_ds.h, 16)
                for b in range(2):
                    s.dma_start(out=B2[:, 8192 + 2048 * b:8192 + 2048 * (b + 1)],
                                in_=D["vtm2_sp"][2 * tc + b]).then_inc(s_ds.h, 16)
            em("sync", fl)
            ld_t = s_ds.bump(16 * 7)

            moff = (TAUC * tc) % TAU

            for cc in range(NCC):
                nu = F1[:, 256 * cc:256 * (cc + 1)]
                scn = F1[:, 4096 + 256 * cc:4096 + 256 * (cc + 1)]
                k = B2[:, 256 * cc:256 * (cc + 1)]
                r = B2[:, 4096 + 256 * cc:4096 + 256 * (cc + 1)]

                def fv(v, nu=nu, scn=scn, ld_t=ld_t, moff=moff, first=(cc == 0)):
                    if first:
                        v.wait_ge(s_ds.h, ld_t)
                    v.tensor_tensor_scan(out=scn, data0=SB["m8"][:, 0:TAUC],
                                         data1=nu, initial=0.0, op0=ALU.mult, op1=ALU.add)
                    v.drain()
                    v.nop().then_inc(s_v.h, 1)
                em("vector", fv)
                vb8 = s_v.bump(1)

                def fs(sc, cc=cc, scn=scn, vb8=vb8):
                    sc.wait_ge(s_v.h, vb8)
                    sc.activation(out=bwsl(FK8, cc), in_=scn, func=AF.Exp)
                    sc.activation(out=bwsl(FR8, cc, 1, TAUC), in_=scn[:, 0:TAUC - 1],
                                  func=AF.Exp, scale=-1.0)
                    g8 = bass.AP(tensor=F1, offset=4096 + 256 * cc + 7,
                                 ap=[[12288, 128], [8, 32]])
                    sc.activation(out=F3[:, 34 * cc:34 * cc + 32], in_=g8,
                                  func=AF.Exp, scale=-1.0).then_inc(s_sc.h, 1)
                em("scalar", fs)
                s8 = s_sc.bump(1)

                def fv2(v, cc=cc, k=k, r=r, s8=s8):
                    v.wait_ge(s_sc.h, s8)
                    fix = bass.AP(tensor=BW, offset=4096 * FR8 + 256 * cc,
                                  ap=[[32768, 128], [8, 32]])
                    v.memset(fix, 1.0)
                    v.memset(F3[:, 34 * cc + 32:34 * cc + 34], 0.0)
                    v.drain()
                    v.tensor_mul(out=bwsl(FK8, cc), in0=bwsl(FK8, cc), in1=k)
                    v.tensor_mul(out=bwsl(FR8, cc), in0=bwsl(FR8, cc), in1=r)
                    v.drain()
                    e8b = bass.AP(tensor=F3, offset=34 * cc,
                                  ap=[[2048, 128], [1, 32], [0, 8]])
                    v.tensor_tensor(out=bwsl(FKA, cc), in0=bwsl(FK8, cc), in1=e8b,
                                    op=ALU.mult)
                    v.drain()
                    e8b1 = bass.AP(tensor=F3, offset=34 * cc + 1,
                                   ap=[[2048, 128], [1, 32], [0, 8]])
                    v.tensor_tensor(out=bwsl(FKF2, cc), in0=bwsl(FKA, cc), in1=e8b1,
                                    op=ALU.mult)
                    v.drain()
                    e8b2 = bass.AP(tensor=F3, offset=34 * cc + 2,
                                   ap=[[2048, 128], [1, 32], [0, 8]])
                    v.tensor_tensor(out=bwsl(FKF3, cc), in0=bwsl(FKF2, cc), in1=e8b2,
                                    op=ALU.mult)
                    v.drain()
                    v.nop().then_inc(s_v.h, 1)
                em("vector", fv2)
                v8 = s_v.bump(1)

                def fv3(v, nu=nu, scn=scn, moff=moff, v8=v8):
                    v.tensor_tensor_scan(out=scn, data0=SB["m32"][:, 0:TAUC],
                                         data1=nu, initial=0.0, op0=ALU.mult, op1=ALU.add)
                    v.drain()
                    v.nop().then_inc(s_v.h, 1)
                em("vector", fv3)
                vb32 = s_v.bump(1)

                def fs2(sc, cc=cc, scn=scn, vb32=vb32):
                    sc.wait_ge(s_v.h, vb32)
                    sc.activation(out=bwsl(FR32, cc, 1, TAUC), in_=scn[:, 0:TAUC - 1],
                                  func=AF.Exp, scale=-1.0)
                    g32 = bass.AP(tensor=F1, offset=4096 + 256 * cc + 31,
                                  ap=[[12288, 128], [32, 8]])
                    sc.activation(out=F3[:, 544 + 8 * cc:544 + 8 * cc + 8], in_=g32,
                                  func=AF.Copy).then_inc(s_sc.h, 1)
                em("scalar", fs2)
                s32 = s_sc.bump(1)

                def fv4(v, cc=cc, k=k, r=r, s32=s32, scn=scn):
                    v.wait_ge(s_sc.h, s32)
                    fix = bass.AP(tensor=BW, offset=4096 * FR32 + 256 * cc,
                                  ap=[[32768, 128], [32, 8]])
                    v.memset(fix, 1.0)
                    v.drain()
                    v.tensor_mul(out=bwsl(FR32, cc), in0=bwsl(FR32, cc), in1=r)
                    b32b = bass.AP(tensor=F3, offset=544 + 8 * cc,
                                   ap=[[2048, 128], [1, 8], [0, 32]])
                    v.tensor_tensor(out=F3[:, 1024:1280], in0=scn, in1=b32b,
                                    op=ALU.subtract)
                    v.drain()
                    v.nop().then_inc(s_v.h, 1)
                em("vector", fv4)
                v32a = s_v.bump(1)

                def fs2b(sc, cc=cc, v32a=v32a):
                    sc.wait_ge(s_v.h, v32a)
                    sc.activation(out=bwsl(FK32, cc), in_=F3[:, 1024:1280],
                                  func=AF.Exp).then_inc(s_sc.h, 1)
                em("scalar", fs2b)
                s32b = s_sc.bump(1)

                def fv4b(v, cc=cc, k=k, s32b=s32b):
                    v.wait_ge(s_sc.h, s32b)
                    v.tensor_mul(out=bwsl(FK32, cc), in0=bwsl(FK32, cc), in1=k)
                    v.drain()
                    v.nop().then_inc(s_v.h, 1)
                em("vector", fv4b)
                v32 = s_v.bump(1)

                def fv5(v, nu=nu, scn=scn, moff=moff, v32=v32):
                    v.tensor_tensor_scan(out=scn, data0=SB["m64"][:, 0:TAUC],
                                         data1=nu, initial=0.0, op0=ALU.mult, op1=ALU.add)
                    v.drain()
                    v.nop().then_inc(s_v.h, 1)
                em("vector", fv5)
                vb64 = s_v.bump(1)

                def fs3(sc, cc=cc, scn=scn, vb64=vb64):
                    sc.wait_ge(s_v.h, vb64)
                    g64 = bass.AP(tensor=F1, offset=4096 + 256 * cc + 63,
                                  ap=[[12288, 128], [64, 4]])
                    sc.activation(out=F3[:, 736 + 4 * cc:736 + 4 * cc + 4], in_=g64,
                                  func=AF.Copy)
                    sc.activation(out=F3[:, 672 + 4 * cc:672 + 4 * cc + 4], in_=g64,
                                  func=AF.Exp, scale=-1.0).then_inc(s_sc.h, 1)
                em("scalar", fs3)
                s64 = s_sc.bump(1)

                def fv6(v, cc=cc, scn=scn, k=k, s64=s64):
                    v.wait_ge(s_sc.h, s64)
                    b64b = bass.AP(tensor=F3, offset=736 + 4 * cc,
                                   ap=[[2048, 128], [1, 4], [0, 64]])
                    v.tensor_tensor(out=F3[:, 1024:1280], in0=scn, in1=b64b,
                                    op=ALU.subtract)
                    v.drain()
                    v.tensor_scalar_mul(out=bwsl(FK0, cc), in0=k, scalar1=vcol("u", cc))
                    v.drain()
                    v.nop().then_inc(s_v.h, 1)
                em("vector", fv6)
                v64a = s_v.bump(1)

                def fs3b(sc, cc=cc, v64a=v64a):
                    sc.wait_ge(s_v.h, v64a)
                    sc.activation(out=B2[:, 12288 + 256 * cc:12288 + 256 * (cc + 1)],
                                  in_=F3[:, 1024:1280], func=AF.Exp).then_inc(s_sc.h, 1)
                em("scalar", fs3b)
                s64b = s_sc.bump(1)

                def fv6b(v, cc=cc, k=k, s64b=s64b):
                    v.wait_ge(s_sc.h, s64b)
                    v.tensor_mul(out=B2[:, 12288 + 256 * cc:12288 + 256 * (cc + 1)],
                                 in0=B2[:, 12288 + 256 * cc:12288 + 256 * (cc + 1)],
                                 in1=k)
                    v.drain()
                    v.nop().then_inc(s_v.h, 1)
                em("vector", fv6b)
                vK64 = s_v.bump(1)

                def ftk(t, cc=cc, vK64=vK64):
                    t.wait_ge(s_v.h, vK64)
                    for b in range(2):
                        ins = t.transpose(
                            PB[0][:, 64 * b:64 * (b + 1)].bitcast(BF16),
                            B2[:, 12288 + 256 * cc + 128 * b:12288 + 256 * cc + 128 * (b + 1)],
                            SB["ident"][:])
                    ins.then_inc(s_t.h, 1)
                em("tensor", ftk)
                tk = s_t.bump(1)

                def fv7(v, cc=cc, tk=tk):
                    v.wait_ge(s_t.h, tk)
                    v.tensor_copy(out=B3[:, 4096 + 256 * cc:4096 + 256 * (cc + 1)],
                                  in_=PB[0][:, 0:128].bitcast(BF16))
                    v.drain()
                    v.nop().then_inc(s_v.h, 1)
                em("vector", fv7)
                vtk = s_v.bump(1)

                def fs4(sc, cc=cc, scn=scn, vtk=vtk):
                    sc.wait_ge(s_v.h, vtk)
                    sc.activation(out=B2[:, 12288 + 256 * cc + 1:12288 + 256 * (cc + 1)],
                                  in_=scn[:, 0:TAUC - 1], func=AF.Exp,
                                  scale=-1.0).then_inc(s_sc.h, 1)
                em("scalar", fs4)
                sR = s_sc.bump(1)

                def fv8(v, cc=cc, r=r, sR=sR):
                    v.wait_ge(s_sc.h, sR)
                    fix = bass.AP(tensor=B2, offset=12288 + 256 * cc,
                                  ap=[[16384, 128], [64, 4]])
                    v.memset(fix, 1.0)
                    v.drain()
                    v.tensor_mul(out=B2[:, 12288 + 256 * cc:12288 + 256 * (cc + 1)],
                                 in0=B2[:, 12288 + 256 * cc:12288 + 256 * (cc + 1)],
                                 in1=r)
                    v.drain()
                    v.nop().then_inc(s_v.h, 1)
                em("vector", fv8)
                s_v.bump(1)
            fac_done = s_v.v

            prev_fso_o = prev_oso
            prev_rel = None
            for ch in range(nch_tc):
                t0 = LC * ch
                for half in range(2):
                    wait_val = fac_done if prev_rel is None else prev_rel

                    def ftp(t, tc=tc, t0=t0, half=half, wait_val=wait_val):
                        t.wait_ge(s_v.h, wait_val)
                        for ti, cc in enumerate(range(8 * half, 8 * (half + 1))):
                            for hh in range(2):
                                po, fo = 64 * hh, 64 * ti
                                tp = (64 * hh, 64 * hh)
                                for bank, slot in [(PB[0], FK8), (PB[1], FKA),
                                                   (PB[2], FKF2), (PB[3], FKF3),
                                                   (PB[4], FK32)]:
                                    rslot = FR32 if slot == FK32 else FR8
                                    t.matmul(bank[po:po + 64, fo:fo + 64],
                                             bwsl(slot, cc, t0, t0 + LC)[64 * hh:64 * hh + 64, :],
                                             bwsl(rslot, cc, t0, t0 + LC)[64 * hh:64 * hh + 64, :],
                                             start=True, stop=True, tile_position=tp)
                                ins = t.matmul(
                                    PB[5][po:po + 64, fo:fo + 64],
                                    bwsl(FK0, cc, t0, t0 + LC)[64 * hh:64 * hh + 64, :],
                                    B2[64 * hh:64 * hh + 64, 4096 + 256 * cc + t0:4096 + 256 * cc + t0 + LC],
                                    start=True, stop=True, tile_position=tp)
                        ins.then_inc(s_t.h, 1)
                    em("tensor", ftp)
                    tP = s_t.bump(1)

                    def fvc(v, tP=tP):
                        v.wait_ge(s_t.h, tP)
                        v.tensor_tensor(out=SB["Pcl"][:], in0=PB[0][:, :], in1=SB["MA"][:],
                                        op=ALU.mult)
                        v.drain()
                        v.select(out=SB["Pcl"][:], mask=SB["MB"][:], on_true=PB[1][:, :],
                                 on_false=SB["Pcl"][:], add_drain=True)
                        v.drain()
                        v.select(out=SB["Pcl"][:], mask=SB["MC"][:], on_true=PB[2][:, :],
                                 on_false=SB["Pcl"][:], add_drain=True)
                        v.drain()
                        v.select(out=SB["Pcl"][:], mask=SB["MD"][:], on_true=PB[3][:, :],
                                 on_false=SB["Pcl"][:], add_drain=True)
                        v.drain()
                        v.select(out=SB["Pcl"][:], mask=SB["ME"][:], on_true=PB[4][:, :],
                                 on_false=SB["Pcl"][:], add_drain=True)
                        v.drain()
                        v.select(out=SB["Pcl"][:], mask=SB["MF"][:], on_true=PB[5][:, :],
                                 on_false=SB["Pcl"][:], add_drain=True)
                        v.drain()
                        v.nop().then_inc(s_v.h, 1)
                    em("vector", fvc)
                    vP = s_v.bump(1)

                    def fti(t, t0=t0, half=half, vP=vP):
                        t.wait_ge(s_v.h, vP)
                        vb = t0 // 128
                        vrow = 64 * ((t0 // 64) % 2)
                        for ti, cc in enumerate(range(8 * half, 8 * (half + 1))):
                            for hh in range(2):
                                po, fo = 64 * hh, 64 * ti
                                hd = 2 * cc + hh
                                t.matmul(PB[6][po:po + 64, fo:fo + 64],
                                         SB["Sb"][64 * hh:64 * hh + 64, cc, :],
                                         B2[64 * hh:64 * hh + 64, 12288 + 256 * cc + t0:12288 + 256 * cc + t0 + LC],
                                         start=True, stop=False, tile_position=(64 * hh, po))
                                if po == vrow:
                                    vsrc = B3[po:po + 64, 2048 * vb + 64 * hd:2048 * vb + 64 * hd + 64]
                                else:
                                    vsrc = B2[po:po + 64, 8192 + 2048 * vb + 64 * hd:8192 + 2048 * vb + 64 * hd + 64]
                                t.matmul(PB[6][po:po + 64, fo:fo + 64],
                                         vsrc,
                                         SB["Pcl"][po:po + 64, fo:fo + 64],
                                         start=False, stop=True, tile_position=(po, po))
                                ins = t.matmul(
                                    PB[7][po:po + 64, fo:fo + 64],
                                    B3[vrow:vrow + 64, 4096 + 256 * cc + 128 * vb + 64 * hh:4096 + 256 * cc + 128 * vb + 64 * hh + 64],
                                    B3[vrow:vrow + 64, 2048 * vb + 64 * hd:2048 * vb + 64 * hd + 64],
                                    start=True, stop=True, tile_position=(vrow, po))
                        ins.then_inc(s_t.h, 1)
                    em("tensor", fti)
                    tI = s_t.bump(1)

                    def fve(v, t0=t0, half=half, ch=ch, tI=tI):
                        v.wait_ge(s_t.h, tI)
                        oap = bass.AP(tensor=F1, offset=8192 + 256 * 8 * half + t0,
                                      ap=[[12288, 128], [256, 8], [1, 64]])
                        v.tensor_copy(out=oap, in_=PB[6][:, :])
                        v.drain()
                        sap = bass.AP(tensor=SB["S"], offset=64 * 8 * half,
                                      ap=[[1024, 128], [64, 8], [1, 64]])
                        d64 = bass.AP(tensor=F3, offset=672 + 4 * 8 * half + ch,
                                      ap=[[2048, 128], [4, 8], [0, 64]])
                        v.tensor_tensor(out=F3[:, 1024:1536], in0=sap, in1=d64, op=ALU.mult)
                        v.drain()
                        v.tensor_add(out=sap, in0=F3[:, 1024:1536], in1=PB[7][:, :])
                        v.drain()
                        sbap = bass.AP(tensor=SB["Sb"], offset=64 * 8 * half,
                                       ap=[[1024, 128], [64, 8], [1, 64]])
                        v.tensor_copy(out=sbap, in_=sap)
                        v.drain()
                        v.nop().then_inc(s_v.h, 1)
                    em("vector", fve)
                    prev_rel = s_v.bump(1)
            vO = s_v.v

            def fso(s, tc=tc, vO=vO):
                s.wait_ge(s_v.h, vO)
                s.dma_start(out=spl("o_sp", TAUC * tc, TAUC * (tc + 1)),
                            in_=s3(F1[:, 8192:12288])).then_inc(s_ds.h, 16)
            em("sync", fso)
            prev_oso = s_ds.bump(16)

            def fvg(v, vO=vO):
                for ch in range(nch_tc):
                    d64 = bass.AP(tensor=F3, offset=672 + ch, ap=[[2048, 128], [4, NCC]])
                    v.tensor_tensor(out=SB["Gfin"][:], in0=SB["Gfin"][:], in1=d64,
                                    op=ALU.mult)
                    v.drain()
                v.nop().then_inc(s_v.h, 1)
            em("vector", fvg)
            s_v.bump(1)

    def ph_AG():
        vS = s_v.v
        dg0 = s_dg.v

        def fg(g, vS=vS, dg0=dg0):
            g.wait_ge(s_v.h, vS)
            g.dma_start(out=D["cc_in"][:, 0:1024],
                        in_=SB["S"][:].rearrange("p c m -> p (c m)")).then_inc(s_dg.h, 16)
            g.dma_start(out=D["cc_in"][:, 1024:1040], in_=SB["Gfin"][:]).then_inc(s_dg.h, 16)
            g.wait_ge(s_dg.h, dg0 + 32)
            g.collective_compute(
                "AllGather", ALU.bypass, replica_groups=[list(range(NCORES))],
                ins=[D["cc_in"]], outs=[D["cc_out"]],
            ).then_inc(s_g.h, 1)
        em("gpsimd", fg)
        s_dg.bump(32)
        gT = s_g.bump(1)

        def fl(s, gT=gT):
            s.wait_ge(s_g.h, gT)
            s.dma_start(out=F1[:, 0:8320].rearrange("p (r f) -> p r f", r=NCORES),
                        in_=D["cc_out"].transpose([1, 0, 2])).then_inc(s_ds.h, 16)
        em("sync", fl)
        ld = s_ds.bump(16)

        def fv(v, ld=ld):
            v.wait_ge(s_ds.h, ld)
            v.memset(F1[:, 8320:9344], 0.0)
            v.memset(F3[:, 0:NCC], 1.0)
            v.drain()
            for j in range(NCORES - 1, -1, -1):
                v.tensor_scalar_mul(out=F3[:, 16:32], in0=F3[:, 0:NCC],
                                    scalar1=SB["rankmask"][:, j:j + 1])
                v.drain()
                facb = bass.AP(tensor=F3, offset=16, ap=[[2048, 128], [1, NCC], [0, HS]])
                v.tensor_tensor(out=F3[:, 1024:2048],
                                in0=F1[:, 1040 * j:1040 * j + 1024], in1=facb, op=ALU.mult)
                v.drain()
                v.tensor_add(out=F1[:, 8320:9344], in0=F1[:, 8320:9344],
                             in1=F3[:, 1024:2048])
                v.drain()
                v.tensor_single_scalar(out=F3[:, 32:48],
                                       in_=F1[:, 1040 * j + 1024:1040 * j + 1040],
                                       scalar=1.0, op=ALU.subtract)
                v.drain()
                v.tensor_scalar(out=F3[:, 32:48], in0=F3[:, 32:48],
                                scalar1=SB["rankmask"][:, j:j + 1], scalar2=1.0,
                                op0=ALU.mult, op1=ALU.add)
                v.drain()
                v.tensor_mul(out=F3[:, 0:NCC], in0=F3[:, 0:NCC], in1=F3[:, 32:48])
                v.drain()
            v.tensor_copy(out=SB["Sinb"][:].rearrange("p c m -> p (c m)"),
                          in_=F1[:, 8320:9344])
            v.drain()
            v.memset(F3[:, 2044:2048], 0.0)
            v.drain()
            v.nop().then_inc(s_v.h, 1)
        em("vector", fv)
        vcomb = s_v.bump(1)

        if "ag_dbg" in dbg:
            D["ag_dbg"] = nc.dram_tensor("ag_dbg", [NCORES, 128, 1040], F32,
                                         kind="ExternalOutput").ap()
            D["sin_dbg"] = nc.dram_tensor("sin_dbg", [128, NCC, HS], BF16,
                                          kind="ExternalOutput").ap()

            def fdbg(s, vcomb=vcomb):
                s.wait_ge(s_v.h, vcomb)
                s.dma_start(out=D["ag_dbg"], in_=D["cc_out"]).then_inc(s_ds.h, 16)
                s.dma_start(out=D["sin_dbg"], in_=SB["Sinb"][:]).then_inc(s_ds.h, 16)
            em("sync", fdbg)
            s_ds.bump(32)

    def ph_C2():
        prev_fsg = 0
        for tc in range(ntc):
            prev_t = s_t.v
            prev_v = s_v.v

            def fl(s, tc=tc, prev_t=prev_t, prev_v=prev_v):
                if tc > 0:
                    s.wait_ge(s_t.h, prev_t)
                    s.wait_ge(s_v.h, prev_v)
                s.dma_start(out=s3(F1[:, 8192:12288]),
                            in_=spl("o_sp", TAUC * tc, TAUC * (tc + 1))).then_inc(s_ds.h, 16)
                s.dma_start(out=s3(F1[:, 0:4096]),
                            in_=spl("nu_sp", TAUC * tc, TAUC * (tc + 1))).then_inc(s_ds.h, 16)
                s.dma_start(out=s3(B2[:, 4096:8192]),
                            in_=spl("r_sp", TAUC * tc, TAUC * (tc + 1))).then_inc(s_ds.h, 16)
                s.dma_start(out=s3(B2[:, 0:4096]),
                            in_=spl("g_sp", TAUC * tc, TAUC * (tc + 1))).then_inc(s_ds.h, 16)
            em("sync", fl)
            ld = s_ds.bump(16 * 4)

            moff = (TAUC * tc) % TAU
            for cc in range(NCC):
                nu = F1[:, 256 * cc:256 * (cc + 1)]
                scn = F1[:, 4096 + 256 * cc:4096 + 256 * (cc + 1)]
                r = B2[:, 4096 + 256 * cc:4096 + 256 * (cc + 1)]

                def fv(v, nu=nu, scn=scn, moff=moff, ld=ld, first=(cc == 0)):
                    if first:
                        v.wait_ge(s_ds.h, ld)
                    v.tensor_tensor_scan(out=scn, data0=SB["m64"][:, 0:TAUC],
                                         data1=nu, initial=0.0, op0=ALU.mult, op1=ALU.add)
                    v.drain()
                    v.nop().then_inc(s_v.h, 1)
                em("vector", fv)
                vb = s_v.bump(1)

                def fs(sc, cc=cc, scn=scn, vb=vb):
                    sc.wait_ge(s_v.h, vb)
                    sc.activation(out=B3[:, 256 * cc + 1:256 * (cc + 1)],
                                  in_=scn[:, 0:TAUC - 1], func=AF.Exp, scale=-1.0)
                    g64 = bass.AP(tensor=F1, offset=4096 + 256 * cc + 63,
                                  ap=[[12288, 128], [64, 4]])
                    sc.activation(out=F3[:, 672 + 4 * cc:672 + 4 * cc + 4], in_=g64,
                                  func=AF.Exp, scale=-1.0).then_inc(s_sc.h, 1)
                em("scalar", fs)
                sR = s_sc.bump(1)

                def fv2(v, cc=cc, r=r, sR=sR, tc=tc):
                    v.wait_ge(s_sc.h, sR)
                    fix = bass.AP(tensor=B3, offset=256 * cc, ap=[[8192, 128], [64, 4]])
                    v.memset(fix, 1.0)
                    if tc == 0:
                        v.memset(F3[:, 1792 + cc:1792 + cc + 1], 1.0)
                    v.drain()
                    v.tensor_tensor_scan(out=F3[:, 832 + 4 * cc:832 + 4 * cc + 4],
                                         data0=F3[:, 672 + 4 * cc:672 + 4 * cc + 4],
                                         data1=F3[:, 2044:2048],
                                         initial=F3[:, 1792 + cc:1792 + cc + 1],
                                         op0=ALU.mult, op1=ALU.add)
                    v.drain()
                    v.tensor_copy(out=F3[:, 896 + 4 * cc + 1:896 + 4 * cc + 4],
                                  in_=F3[:, 832 + 4 * cc:832 + 4 * cc + 3])
                    v.tensor_copy(out=F3[:, 896 + 4 * cc:896 + 4 * cc + 1],
                                  in_=F3[:, 1792 + cc:1792 + cc + 1])
                    v.drain()
                    v.tensor_copy(out=F3[:, 1792 + cc:1792 + cc + 1],
                                  in_=F3[:, 832 + 4 * cc + 3:832 + 4 * cc + 4])
                    v.drain()
                    v.tensor_mul(out=B3[:, 256 * cc:256 * (cc + 1)],
                                 in0=B3[:, 256 * cc:256 * (cc + 1)], in1=r)
                    v.drain()
                    gpb = bass.AP(tensor=F3, offset=896 + 4 * cc,
                                  ap=[[2048, 128], [1, 4], [0, 64]])
                    v.tensor_tensor(out=B3[:, 256 * cc:256 * (cc + 1)],
                                    in0=B3[:, 256 * cc:256 * (cc + 1)], in1=gpb,
                                    op=ALU.mult)
                    v.drain()
                    v.nop().then_inc(s_v.h, 1)
                em("vector", fv2)
                s_v.bump(1)
            vR = s_v.v

            vc0 = s_v.v
            for cc in range(NCC):
                wait_v = vc0 + cc - 1 if cc >= 2 else None

                def ftc(t, cc=cc, vR=vR, wait_v=wait_v):
                    if cc == 0:
                        t.wait_ge(s_v.h, vR)
                    if wait_v is not None:
                        t.wait_ge(s_v.h, wait_v)
                    for hh in range(2):
                        ins = t.matmul(PB[cc % 2][64 * hh:64 * hh + 64, 0:256],
                                       SB["Sinb"][64 * hh:64 * hh + 64, cc, :],
                                       B3[64 * hh:64 * hh + 64, 256 * cc:256 * (cc + 1)],
                                       start=True, stop=True, tile_position=(64 * hh, 64 * hh))
                    ins.then_inc(s_t.h, 1)
                em("tensor", ftc)
                tC = s_t.bump(1)

                def fvc(v, cc=cc, tC=tC):
                    v.wait_ge(s_t.h, tC)
                    v.tensor_add(out=F1[:, 8192 + 256 * cc:8192 + 256 * (cc + 1)],
                                 in0=F1[:, 8192 + 256 * cc:8192 + 256 * (cc + 1)],
                                 in1=PB[cc % 2][:, 0:256])
                    v.drain()
                    v.nop().then_inc(s_v.h, 1)
                em("vector", fvc)
                s_v.bump(1)
            vO = s_v.v

            def fsq(sc, vO=vO):
                sc.wait_ge(s_v.h, vO)
                for cc in range(NCC):
                    sc.activation(out=B2[:, 8192 + 256 * cc:8192 + 256 * (cc + 1)],
                                  in_=F1[:, 8192 + 256 * cc:8192 + 256 * (cc + 1)],
                                  func=AF.Square)
                for cc in range(NCC):
                    ins = sc.activation(out=B2[:, 4096 + 256 * cc:4096 + 256 * (cc + 1)],
                                        in_=F1[:, 8192 + 256 * cc:8192 + 256 * (cc + 1)],
                                        func=AF.Copy)
                ins.then_inc(s_sc.h, 1)
            em("scalar", fsq)
            sq = s_sc.bump(1)

            def fts(t, sq=sq):
                t.wait_ge(s_sc.h, sq)
                for cc in range(NCC):
                    t.matmul(PB[2][0:32, 0:256], SB["gn_ones_bf"][:, cc, :],
                             B2[:, 4096 + 256 * cc:4096 + 256 * (cc + 1)],
                             start=(cc == 0), stop=(cc == 15))
                for cc in range(NCC):
                    ins = t.matmul(PB[3][0:32, 0:256], SB["gn_ones_bf"][:, cc, :],
                                   B2[:, 8192 + 256 * cc:8192 + 256 * (cc + 1)],
                                   start=(cc == 0), stop=(cc == 15))
                ins.then_inc(s_t.h, 1)
            em("tensor", fts)
            tS = s_t.bump(1)

            def fvs(v, tS=tS):
                v.wait_ge(s_t.h, tS)
                v.tensor_scalar_mul(out=F3[0:32, 1024:1280], in0=PB[2][0:32, 0:256],
                                    scalar1=1.0 / HS)
                v.drain()
                v.tensor_scalar_mul(out=F3[0:32, 1280:1536], in0=PB[3][0:32, 0:256],
                                    scalar1=1.0 / HS)
                v.drain()
                v.tensor_mul(out=F3[0:32, 1536:1792], in0=F3[0:32, 1024:1280],
                             in1=F3[0:32, 1024:1280])
                v.drain()
                v.tensor_sub(out=F3[0:32, 1280:1536], in0=F3[0:32, 1280:1536],
                             in1=F3[0:32, 1536:1792])
                v.drain()
                v.tensor_copy(out=SB["Pcl"][0:32, 0:256], in_=F3[0:32, 1024:1280])
                v.drain()
                v.nop().then_inc(s_v.h, 1)
            em("vector", fvs)
            vs = s_v.bump(1)

            def fss(sc, vs=vs):
                sc.wait_ge(s_v.h, vs)
                sc.activation(out=F3[0:32, 1536:1792], in_=F3[0:32, 1280:1536],
                              func=AF.Ln, bias=SB["eps"][0:32, 0:1])
                sc.drain()
                sc.activation(out=SB["Pcl"][0:32, 256:512], in_=F3[0:32, 1536:1792],
                              func=AF.Exp, scale=-0.5).then_inc(s_sc.h, 1)
            em("scalar", fss)
            ss = s_sc.bump(1)

            vg0 = s_v.v
            for cc in range(NCC):
                wait_v = vg0 + cc if cc >= 1 else None

                def ftb(t, cc=cc, ss=ss, wait_v=wait_v):
                    if cc == 0:
                        t.wait_ge(s_sc.h, ss)
                    if wait_v is not None:
                        t.wait_ge(s_v.h, wait_v)
                    gsl = SB["gn_sel"][:, cc, :]
                    t.matmul(PB[4][:, 0:256], gsl, SB["Pcl"][0:32, 0:256],
                             start=True, stop=True)
                    t.matmul(PB[5][:, 0:256], gsl, SB["Pcl"][0:32, 256:512],
                             start=True, stop=True).then_inc(s_t.h, 1)
                em("tensor", ftb)
                tb = s_t.bump(1)

                def fva(v, cc=cc, tb=tb):
                    v.wait_ge(s_t.h, tb)
                    v.tensor_sub(out=F3[:, 1536:1792],
                                 in0=F1[:, 8192 + 256 * cc:8192 + 256 * (cc + 1)],
                                 in1=PB[4][:, 0:256])
                    v.drain()
                    v.scalar_tensor_tensor(out=F3[:, 1536:1792], in0=F3[:, 1536:1792],
                                           scalar=vcol("ln_g", cc), in1=PB[5][:, 0:256],
                                           op0=ALU.mult, op1=ALU.mult)
                    v.drain()
                    v.scalar_tensor_tensor(out=B3[:, 4096 + 256 * cc:4096 + 256 * (cc + 1)],
                                           in0=F3[:, 1536:1792], scalar=vcol("ln_b", cc),
                                           in1=B2[:, 256 * cc:256 * (cc + 1)],
                                           op0=ALU.add, op1=ALU.mult)
                    v.drain()
                    v.nop().then_inc(s_v.h, 1)
                em("vector", fva)
                s_v.bump(1)
            vgn = s_v.v

            def fsg(s, tc=tc, vgn=vgn):
                s.wait_ge(s_v.h, vgn)
                s.dma_start(out=spl("ogn_sp", TAUC * tc, TAUC * (tc + 1)),
                            in_=s3(B3[:, 4096:8192])).then_inc(s_ds.h, 16)
            em("sync", fsg)
            prev_fsg = s_ds.bump(16)

    def ph_D():
        def flw(g):
            for cc in range(NCC):
                g.dma_start(out=BW[:, 2048 * cc:2048 * (cc + 1)],
                            in_=D["wo_t"][128 * cc:128 * (cc + 1), :]).then_inc(s_dg.h, 16)
        em("gpsimd", flw)
        w_t = s_dg.bump(16 * NCC)

        for tau in range(ntau):
            prev_v = s_v.v
            prev_st = s_ds.v

            def flx(s, tau=tau, prev_v=prev_v, prev_st=prev_st):
                if tau > 0:
                    s.wait_ge(s_v.h, prev_v)
                    s.wait_ge(s_ds.h, prev_st)
                s.dma_start(out=s3(B2[:, 0:8192]),
                            in_=spl("ogn_sp", TAU * tau, TAU * (tau + 1))).then_inc(s_ds.h, 16)
                for cc in range(NCC):
                    s.dma_start(out=F1[:, 512 * cc:512 * (cc + 1)],
                                in_=D["hid_fm"][cc, :, 1 + TAU * tau:1 + TAU * (tau + 1)]).then_inc(s_ds.h, 16)
            em("sync", flx)
            x_t = s_ds.bump(16 * 17)

            v0 = s_v.v
            for mt in range(NCC):
                bank = PB[mt % 4]
                wait_v = v0 + mt - 3 if mt >= 4 else None

                def ftp(t, mt=mt, bank=bank, w_t=w_t, x_t=x_t, wait_v=wait_v):
                    if mt == 0:
                        t.wait_ge(s_dg.h, w_t)
                        t.wait_ge(s_ds.h, x_t)
                    if wait_v is not None:
                        t.wait_ge(s_v.h, wait_v)
                    for cc in range(NCC):
                        ins = t.matmul(bank[:, :],
                                       BW[:, 2048 * cc + 128 * mt:2048 * cc + 128 * (mt + 1)],
                                       B2[:, 512 * cc:512 * (cc + 1)],
                                       start=(cc == 0), stop=(cc == 15))
                    ins.then_inc(s_t.h, 1)
                em("tensor", ftp)
                mm_t = s_t.bump(1)

                def fvr(v, mt=mt, bank=bank, mm_t=mm_t):
                    v.wait_ge(s_t.h, mm_t)
                    v.tensor_add(out=F1[:, 512 * mt:512 * (mt + 1)],
                                 in0=F1[:, 512 * mt:512 * (mt + 1)], in1=bank[:, :])
                    v.drain()
                    v.nop().then_inc(s_v.h, 1)
                em("vector", fvr)
                s_v.bump(1)
            vr = s_v.v

            def fsq(sc, vr=vr):
                sc.wait_ge(s_v.h, vr)
                for cc in range(NCC):
                    ins = sc.activation(out=B2[:, 8192 + 512 * cc:8192 + 512 * (cc + 1)],
                                        in_=F1[:, 512 * cc:512 * (cc + 1)], func=AF.Square)
                ins.then_inc(s_sc.h, 1)
            em("scalar", fsq)
            sq = s_sc.bump(1)

            def fts(t, sq=sq):
                t.wait_ge(s_sc.h, sq)
                for cc in range(NCC):
                    t.matmul(PB[4][0:1, :], SB["ones_col"][:], F1[:, 512 * cc:512 * (cc + 1)],
                             start=(cc == 0), stop=(cc == 15))
                for cc in range(NCC):
                    ins = t.matmul(PB[5][0:1, :], SB["ones_col_bf"][:],
                                   B2[:, 8192 + 512 * cc:8192 + 512 * (cc + 1)],
                                   start=(cc == 0), stop=(cc == 15))
                ins.then_inc(s_t.h, 1)
            em("tensor", fts)
            tS = s_t.bump(1)

            def fvs(v, tS=tS):
                v.wait_ge(s_t.h, tS)
                v.tensor_scalar_mul(out=F3[0:1, 0:512], in0=PB[4][0:1, :], scalar1=1.0 / C)
                v.drain()
                v.tensor_scalar_mul(out=F3[0:1, 512:1024], in0=PB[5][0:1, :], scalar1=1.0 / C)
                v.drain()
                v.tensor_mul(out=F3[0:1, 1024:1536], in0=F3[0:1, 0:512], in1=F3[0:1, 0:512])
                v.drain()
                v.tensor_sub(out=F3[0:1, 512:1024], in0=F3[0:1, 512:1024],
                             in1=F3[0:1, 1024:1536])
                v.drain()
                v.nop().then_inc(s_v.h, 1)
            em("vector", fvs)
            vs = s_v.bump(1)

            def fss(sc, vs=vs):
                sc.wait_ge(s_v.h, vs)
                sc.activation(out=F3[0:1, 1024:1536], in_=F3[0:1, 512:1024],
                              func=AF.Ln, bias=SB["eps"][0:1, 1:2])
                sc.drain()
                sc.activation(out=F3[0:1, 512:1024], in_=F3[0:1, 1024:1536],
                              func=AF.Exp, scale=-0.5).then_inc(s_sc.h, 1)
            em("scalar", fss)
            ss = s_sc.bump(1)

            def ftb(t, ss=ss):
                t.wait_ge(s_sc.h, ss)
                t.matmul(PB[6][:, :], SB["ones_row"][:], F3[0:1, 0:512],
                         start=True, stop=True)
                t.matmul(PB[7][:, :], SB["ones_row"][:], F3[0:1, 512:1024],
                         start=True, stop=True).then_inc(s_t.h, 1)
            em("tensor", ftb)
            tb = s_t.bump(1)

            for cc in range(NCC):
                def fva(v, cc=cc, tb=tb):
                    if cc == 0:
                        v.wait_ge(s_t.h, tb)
                    v.tensor_sub(out=F1[:, 512 * cc:512 * (cc + 1)],
                                 in0=F1[:, 512 * cc:512 * (cc + 1)], in1=PB[6][:, :])
                    v.drain()
                    v.tensor_mul(out=F1[:, 512 * cc:512 * (cc + 1)],
                                 in0=F1[:, 512 * cc:512 * (cc + 1)], in1=PB[7][:, :])
                    v.drain()
                    v.tensor_copy(out=B2[:, 512 * cc:512 * (cc + 1)],
                                  in_=F1[:, 512 * cc:512 * (cc + 1)])
                    v.drain()
                    v.nop().then_inc(s_v.h, 1)
                em("vector", fva)
                s_v.bump(1)
            va = s_v.v

            def fso(s, tau=tau, va=va):
                s.wait_ge(s_v.h, va)
                s.dma_start(out=spl("x1f_sp", TAU * tau, TAU * (tau + 1)),
                            in_=s3(F1[:, 0:8192])).then_inc(s_ds.h, 16)
                s.dma_start(out=spl("x1b_sp", TAU * tau, TAU * (tau + 1)),
                            in_=s3(B2[:, 0:8192])).then_inc(s_ds.h, 16)
            em("sync", fso)
            s_ds.bump(32)

    def ph_E():
        def flx(s):
            ins = s.dma_start(
                out=BW[:, 0:NCC * t_core].rearrange("p (c t) -> p c t", c=NCC),
                in_=D["x1b_sp"].transpose([1, 0, 2]))
            ins.then_inc(s_dx.h, 16)
        em("sync", flx)
        x_t = s_dx.bump(16)

        hs_ready = s_g.v

        prev_eg_mm = 0
        for eg in range(16):
            e0 = 512 * eg
            for wi, wname in [(0, "wu_t"), (1, "wgate_t")]:
                prev_sc = s_sc.v
                prev_fqv = s_v.v

                def flw(s, wname=wname, e0=e0, prev_sc=prev_sc, first=(eg == 0 and wi == 0)):
                    if not first:
                        s.wait_ge(s_sc.h, prev_sc)
                    for cc in range(NCC):
                        s.dma_start(out=F1[:, 512 * cc:512 * (cc + 1)],
                                    in_=D[wname][128 * cc:128 * (cc + 1), e0:e0 + 512]).then_inc(s_ds.h, 16)
                em("sync", flw)
                lw = s_ds.bump(16 * NCC)

                scr = F1[:, 8192:12288].bitcast(BF16)

                def fq(sc, wi=wi, lw=lw, scr=scr, hs_ready=hs_ready,
                       prev_fqv=prev_fqv, prev_eg_mm=prev_eg_mm,
                       first=(eg == 0 and wi == 0)):
                    sc.wait_ge(s_ds.h, lw)
                    if first:
                        sc.wait_ge(s_g.h, hs_ready)
                    if prev_fqv > 0:
                        sc.wait_ge(s_v.h, prev_fqv)
                    if prev_eg_mm > 0:
                        sc.wait_ge(s_t.h, prev_eg_mm)
                    sc.sign(out=scr, in_=F1[:, 0:8192],
                            bias=SB["hs"][:, 4 + wi:5 + wi])
                    sc.drain()
                    sc.sign(out=B2[:, wi * 8192:wi * 8192 + 8192], in_=F1[:, 0:8192],
                            bias=SB["hs"][:, wi:wi + 1]).then_inc(s_sc.h, 1)
                em("scalar", fq)
                qs = s_sc.bump(1)

                def fqv(v, wi=wi, qs=qs, scr=scr):
                    v.wait_ge(s_sc.h, qs)
                    v.tensor_add(out=B2[:, wi * 8192:wi * 8192 + 8192],
                                 in0=B2[:, wi * 8192:wi * 8192 + 8192], in1=scr)
                    v.drain()
                    v.nop().then_inc(s_v.h, 1)
                em("vector", fqv)
                s_v.bump(1)
            q_done = s_v.v

            sc0 = s_sc.v
            vE0 = s_v.v
            prev_fsh = getattr(ph_E, "_pf", 0)
            nslot = 0
            for et in range(4):
                for tau in range(ntau):
                    upb = PB[(nslot % 2) * 2]
                    gtb = PB[(nslot % 2) * 2 + 1]
                    wait_sc = sc0 + (nslot - 1) if nslot >= 2 else None
                    wait_ve = vE0 + (nslot - 1) if nslot >= 2 else None

                    def ftm(t, et=et, tau=tau, upb=upb, gtb=gtb, q_done=q_done,
                            x_t=x_t, wait_sc=wait_sc, wait_ve=wait_ve,
                            first=(nslot == 0)):
                        if first:
                            t.wait_ge(s_v.h, q_done)
                            t.wait_ge(s_dx.h, x_t)
                        if wait_sc is not None:
                            t.wait_ge(s_sc.h, wait_sc)
                        if wait_ve is not None:
                            t.wait_ge(s_v.h, wait_ve)
                        for cc in range(NCC):
                            t.matmul(upb[:, :],
                                     B2[:, 512 * cc + 128 * et:512 * cc + 128 * (et + 1)],
                                     BW[:, t_core * cc + 512 * tau:t_core * cc + 512 * (tau + 1)],
                                     start=(cc == 0), stop=(cc == 15))
                        for cc in range(NCC):
                            ins = t.matmul(gtb[:, :],
                                           B2[:, 8192 + 512 * cc + 128 * et:8192 + 512 * cc + 128 * (et + 1)],
                                           BW[:, t_core * cc + 512 * tau:t_core * cc + 512 * (tau + 1)],
                                           start=(cc == 0), stop=(cc == 15))
                        ins.then_inc(s_t.h, 1)
                    em("tensor", ftm)
                    mm_t = s_t.bump(1)

                    def fup(sc, upb=upb, mm_t=mm_t,
                            wv=(vE0 + nslot if nslot >= 1 else None)):
                        sc.wait_ge(s_t.h, mm_t)
                        if wv is not None:
                            sc.wait_ge(s_v.h, wv)
                        sc.activation(out=SB["Pcl"][:], in_=upb[:, :], func=AF.Silu,
                                      scale=SB["hs"][:, 0:1]).then_inc(s_sc.h, 1)
                    em("scalar", fup)
                    up_t = s_sc.bump(1)

                    def fh(v, et=et, tau=tau, gtb=gtb, up_t=up_t):
                        v.wait_ge(s_sc.h, up_t)
                        v.scalar_tensor_tensor(
                            out=B3[:, 512 * ntau * et + 512 * tau:512 * ntau * et + 512 * (tau + 1)],
                            in0=gtb[:, :], scalar=SB["hs"][:, 1:2], in1=SB["Pcl"][:],
                            op0=ALU.mult, op1=ALU.mult)
                        v.drain()
                        v.nop().then_inc(s_v.h, 1)
                    em("vector", fh)
                    s_v.bump(1)
                    nslot += 1
            h_done = s_v.v
            prev_eg_mm = s_t.v

            def fsh(s, eg=eg, h_done=h_done):
                s.wait_ge(s_v.h, h_done)
                for et in range(4):
                    s.dma_start(out=D["h_sp"][4 * eg + et],
                                in_=B3[:, 512 * ntau * et:512 * ntau * (et + 1)]).then_inc(s_ds.h, 16)
            em("sync", fsh)
            ph_E._pf = s_ds.bump(16 * 4)

        h_stores = s_ds.v
        for sl in range(EXP // 128):
            prev_sc = s_sc.v

            def flw(g, sl=sl, prev_sc=prev_sc):
                if sl >= 1:
                    g.wait_ge(s_sc.h, prev_sc)
                g.dma_start(out=F3[:, 0:2048],
                            in_=D["wd_t"][128 * sl:128 * (sl + 1), :]).then_inc(s_dg.h, 16)
            em("gpsimd", flw)
            lw = s_dg.bump(16)

            def fq(sc, sl=sl, lw=lw, prev_store=s_ds.v):
                sc.wait_ge(s_dg.h, lw)
                sc.wait_ge(s_ds.h, prev_store)
                sc.sign(out=B3[:, 0:2048], in_=F3[:, 0:2048], bias=SB["hs"][:, 6:7])
                sc.sign(out=B3[:, 2048:4096], in_=F3[:, 0:2048],
                        bias=SB["hs"][:, 2:3]).then_inc(s_sc.h, 1)
            em("scalar", fq)
            qs = s_sc.bump(1)

            def fqv(v, sl=sl, qs=qs):
                v.wait_ge(s_sc.h, qs)
                v.tensor_add(out=B3[:, 0:2048], in0=B3[:, 0:2048], in1=B3[:, 2048:4096])
                v.drain()
                v.nop().then_inc(s_v.h, 1)
            em("vector", fqv)
            qv = s_v.bump(1)

            def fso(s, sl=sl, qv=qv):
                s.wait_ge(s_v.h, qv)
                for dt in range(NCC):
                    s.dma_start(out=D["wdq_sp"][dt, :, sl, :],
                                in_=B3[:, 128 * dt:128 * (dt + 1)]).then_inc(s_ds.h, 16)
            em("sync", fso)
            s_ds.bump(16 * NCC)

    def ph_F():
        prev_out = 0
        for tau in range(ntau):
            prev_t = s_t.v

            def flh(s, tau=tau, prev_t=prev_t):
                if tau > 0:
                    s.wait_ge(s_t.h, prev_t)
                s.dma_start(out=BW[:, 0:32768].rearrange("p (e t) -> p e t", e=NET),
                            in_=D["h_sp"][:, :, TAU * tau:TAU * (tau + 1)].transpose([1, 0, 2])).then_inc(s_dx.h, 16)
            em("sync", flh)
            h_t = s_dx.bump(16)

            v0 = s_v.v
            for pair in range(NCC // 2):
                wait_v = v0 + 2 * pair if pair >= 1 else None

                def flw(s, pair=pair, tau=tau, wait_v=wait_v):
                    if wait_v is not None:
                        s.wait_ge(s_v.h, wait_v)
                    for dt in (2 * pair, 2 * pair + 1):
                        s.dma_start(out=B2[:, 8192 * (dt % 2):8192 * (dt % 2) + 8192],
                                    in_=D["wdq_sp"][dt].rearrange("p ec d -> p (ec d)")).then_inc(s_ds.h, 16)
                        s.dma_start(out=F3[:, 512 * (dt % 2):512 * (dt % 2) + 512],
                                    in_=D["x1f_sp"][dt, :, TAU * tau:TAU * (tau + 1)]).then_inc(s_ds.h, 16)
                em("sync", flw)
                lw = s_ds.bump(64)

                for dt in (2 * pair, 2 * pair + 1):
                    bank = PB[dt % 2]

                    def ftm(t, dt=dt, bank=bank, lw=lw, h_t=h_t):
                        t.wait_ge(s_ds.h, lw)
                        if dt == 0:
                            t.wait_ge(s_dx.h, h_t)
                        for ec in range(NET):
                            ins = t.matmul(bank[:, :],
                                           B2[:, 8192 * (dt % 2) + 128 * ec:8192 * (dt % 2) + 128 * (ec + 1)],
                                           BW[:, 512 * ec:512 * (ec + 1)],
                                           start=(ec == 0), stop=(ec == NET - 1))
                        ins.then_inc(s_t.h, 1)
                    em("tensor", ftm)
                    mm_t = s_t.bump(1)

                    def fvr(v, dt=dt, bank=bank, mm_t=mm_t):
                        v.wait_ge(s_t.h, mm_t)
                        v.scalar_tensor_tensor(out=F1[:, 512 * dt:512 * (dt + 1)],
                                               in0=bank[:, :], scalar=SB["hs"][:, 2:3],
                                               in1=F3[:, 512 * (dt % 2):512 * (dt % 2) + 512],
                                               op0=ALU.mult, op1=ALU.add)
                        v.drain()
                        v.nop().then_inc(s_v.h, 1)
                    em("vector", fvr)
                    s_v.bump(1)
            vr = s_v.v

            def fsq(sc, vr=vr):
                sc.wait_ge(s_v.h, vr)
                for cc in range(NCC):
                    ins = sc.activation(out=B3[:, 512 * cc:512 * (cc + 1)],
                                        in_=F1[:, 512 * cc:512 * (cc + 1)], func=AF.Square)
                ins.then_inc(s_sc.h, 1)
            em("scalar", fsq)
            sq = s_sc.bump(1)

            def fts(t, sq=sq):
                t.wait_ge(s_sc.h, sq)
                for cc in range(NCC):
                    t.matmul(PB[4][0:1, :], SB["ones_col"][:], F1[:, 512 * cc:512 * (cc + 1)],
                             start=(cc == 0), stop=(cc == 15))
                for cc in range(NCC):
                    ins = t.matmul(PB[5][0:1, :], SB["ones_col_bf"][:],
                                   B3[:, 512 * cc:512 * (cc + 1)],
                                   start=(cc == 0), stop=(cc == 15))
                ins.then_inc(s_t.h, 1)
            em("tensor", fts)
            tS = s_t.bump(1)

            def fvs(v, tS=tS):
                v.wait_ge(s_t.h, tS)
                v.tensor_scalar_mul(out=F3[0:1, 1024:1536], in0=PB[4][0:1, :], scalar1=1.0 / C)
                v.drain()
                v.tensor_scalar_mul(out=F3[0:1, 1536:2048], in0=PB[5][0:1, :], scalar1=1.0 / C)
                v.drain()
                v.tensor_mul(out=F3[0:32, 1024:1280] if False else F3[0:1, 0:512],
                             in0=F3[0:1, 1024:1536], in1=F3[0:1, 1024:1536])
                v.drain()
                v.tensor_sub(out=F3[0:1, 1536:2048], in0=F3[0:1, 1536:2048],
                             in1=F3[0:1, 0:512])
                v.drain()
                v.nop().then_inc(s_v.h, 1)
            em("vector", fvs)
            vs = s_v.bump(1)

            def fss(sc, vs=vs):
                sc.wait_ge(s_v.h, vs)
                sc.activation(out=F3[0:1, 0:512], in_=F3[0:1, 1536:2048],
                              func=AF.Ln, bias=SB["eps"][0:1, 1:2])
                sc.drain()
                sc.activation(out=F3[0:1, 1536:2048], in_=F3[0:1, 0:512],
                              func=AF.Exp, scale=-0.5).then_inc(s_sc.h, 1)
            em("scalar", fss)
            ss = s_sc.bump(1)

            def ftb(t, ss=ss):
                t.wait_ge(s_sc.h, ss)
                t.matmul(PB[6][:, :], SB["ones_row"][:], F3[0:1, 1024:1536],
                         start=True, stop=True)
                t.matmul(PB[7][:, :], SB["ones_row"][:], F3[0:1, 1536:2048],
                         start=True, stop=True).then_inc(s_t.h, 1)
            em("tensor", ftb)
            tb = s_t.bump(1)

            for cc in range(NCC):
                def fva(v, cc=cc, tb=tb):
                    if cc == 0:
                        v.wait_ge(s_t.h, tb)
                    v.tensor_sub(out=F1[:, 512 * cc:512 * (cc + 1)],
                                 in0=F1[:, 512 * cc:512 * (cc + 1)], in1=PB[6][:, :])
                    v.drain()
                    v.tensor_mul(out=F1[:, 512 * cc:512 * (cc + 1)],
                                 in0=F1[:, 512 * cc:512 * (cc + 1)], in1=PB[7][:, :])
                    v.drain()
                    v.nop().then_inc(s_v.h, 1)
                em("vector", fva)
                s_v.bump(1)
            va = s_v.v

            def fso(s, tau=tau, va=va):
                s.wait_ge(s_v.h, va)
                s.dma_start(out=spl("out_fm", TAU * tau, TAU * (tau + 1)),
                            in_=s3(F1[:, 0:8192])).then_inc(s_ds.h, 16)
            em("sync", fso)
            prev_out = s_ds.bump(16)

    allph = [("setup", ph_setup), ("scale", ph_scale), ("A", ph_A), ("B", ph_B),
             ("C", ph_C), ("AG", ph_AG), ("C2", ph_C2), ("D", ph_D), ("E", ph_E),
             ("F", ph_F)]
    for name, fn in allph:
        fn()
        fence()
        if upto is not None and name == upto:
            break

    blk = ctx.enter_context(nc.Block())

    def runner(engine_name):
        def run(eng):
            for fn in steps[engine_name]:
                fn(eng)
        return run

    blk.sync(runner("sync"))
    blk.gpsimd(runner("gpsimd"))
    blk.vector(runner("vector"))
    blk.scalar(runner("scalar"))
    blk.tensor(runner("tensor"))
    ctx.close()
    return nc, D


def kernel(**inputs):
    in_maps, t_core = prep_inputs(inputs)
    nc, D = build_program(t_core)
    res = run_bass_kernel_spmd(nc, in_maps, core_ids=list(range(NCORES)))
    outs = []
    for r in range(NCORES):
        o = np.asarray(res.results[r]["out_fm"])
        outs.append(o.reshape(C, t_core).T)
    full = np.concatenate(outs, axis=0)
    return np.ascontiguousarray(full.reshape(1, NCORES * t_core, C)).astype(np.float32)


# version 08
# speedup vs baseline: 1.0681x; 1.0681x over previous
import numpy as np
import ml_dtypes
from contextlib import ExitStack

import concourse.bass as bass
import concourse.mybir as mybir
from concourse.bass_utils import run_bass_kernel_spmd

F32 = mybir.dt.float32
BF16 = mybir.dt.bfloat16
ALU = mybir.AluOpType
AF = mybir.ActivationFunctionType

NCORES = 8
C = 2048
NCC = 16
H = 32
HS = 64
EXP = 8192
NET = EXP // 128
LC = 64
TAU = 512
TAUC = 256
NU_CLAMP = 11.0
GN_EPS = 1e-5 * 64.0
LN_EPS = 1e-5
DMIX = 32
DDEC = 64

ENGINES = ("sync", "gpsimd", "vector", "scalar", "tensor")



def _vec_fm(v):
    return np.ascontiguousarray(np.asarray(v, np.float32).reshape(NCC, 128).T)


def _host_constants():
    t = np.arange(TAU)
    consts = {
        "m8": np.broadcast_to((t[:TAUC] % 8 != 0).astype(np.float32), (128, TAUC)).copy(),
        "m32": np.broadcast_to((t[:TAUC] % 32 != 0).astype(np.float32), (128, TAUC)).copy(),
        "m64": np.broadcast_to((t[:TAUC] % LC != 0).astype(np.float32), (128, TAUC)).copy(),
    }
    s = np.arange(LC)[:, None]
    tt = np.arange(LC)[None, :]
    bs, bt = s // 8, tt // 8
    pats = {
        "MA": (bs == bt) & (s < tt),
        "MB": (bt - bs) == 1,
        "MC": (bt - bs) == 2,
        "MD": (bt - bs) == 3,
        "ME": ((bt - bs) >= 4) & (s < 32) & (tt >= 32),
        "MF": (s == tt),
    }
    valid = np.zeros((LC, LC), bool)
    for name, m in pats.items():
        if name == "MA":
            consts[name] = np.tile(m.astype(np.float32), (2, 8)).astype(ml_dtypes.bfloat16)
        else:
            consts[name] = np.tile(m.astype(np.uint8), (2, 8))
        valid |= m
    consts["ident"] = np.eye(128).astype(ml_dtypes.bfloat16)
    go = np.zeros((128, NCC, 32), np.float32)
    for cc in range(NCC):
        go[0:64, cc, 2 * cc] = 1.0
        go[64:128, cc, 2 * cc + 1] = 1.0
    consts["gn_ones"] = go
    consts["gn_ones_bf"] = go.astype(ml_dtypes.bfloat16)
    gs = np.zeros((32, NCC, 128), np.float32)
    for cc in range(NCC):
        gs[2 * cc, cc, 0:64] = 1.0
        gs[2 * cc + 1, cc, 64:128] = 1.0
    consts["gn_sel"] = gs.astype(ml_dtypes.bfloat16)
    consts["ones_col"] = np.ones((128, 1), np.float32)
    consts["ones_col_bf"] = np.ones((128, 1), np.float32).astype(ml_dtypes.bfloat16)
    consts["ones_row"] = np.ones((1, 128), np.float32)
    return consts


def prep_inputs(inp):
    x = np.asarray(inp["L0_attn_hidden"], np.float32).reshape(-1, C)
    t_total = x.shape[0]
    t_core = t_total // NCORES
    consts = _host_constants()

    def T(a):
        return np.ascontiguousarray(np.asarray(a, np.float32).T)

    wu_t = T(inp["L0_ffn_W_up"])
    wgate_t = T(inp["L0_ffn_W_gate"])
    wd_t = T(inp["L0_ffn_W_down"])

    shared = {
        "wr_t": T(inp["L0_attn_W_r"]), "wk_t": T(inp["L0_attn_W_k"]),
        "wv_t": T(inp["L0_attn_W_v"]), "wg_t": T(inp["L0_attn_W_g"]),
        "wo_t": T(inp["L0_attn_W_o"]),
        "wu_t": wu_t, "wgate_t": wgate_t, "wd_t": wd_t,
        "w1": np.ascontiguousarray(np.asarray(inp["L0_attn_time_maa_w1"], np.float32)),
        "w2": np.ascontiguousarray(np.asarray(inp["L0_attn_time_maa_w2"], np.float32)),
        "w2p": np.ascontiguousarray(
            np.asarray(inp["L0_attn_time_maa_w2"], np.float32)[0:4]
            .reshape(4, DMIX, NCC, 128).transpose(0, 1, 2, 3).reshape(128, NCC, 128)),
        "dw1": np.ascontiguousarray(np.asarray(inp["L0_attn_time_decay_w1"], np.float32)),
        "dw2": np.ascontiguousarray(
            np.asarray(inp["L0_attn_time_decay_w2"], np.float32)
            .reshape(DDEC, 8, 2, 128).transpose(2, 0, 1, 3).reshape(128, 8, 128)),
        "vecs": np.stack([
            _vec_fm(np.reshape(np.asarray(inp["L0_attn_time_maa_w"], np.float32), -1)),
            _vec_fm(np.reshape(np.asarray(inp["L0_attn_time_maa_k"], np.float32), -1)),
            _vec_fm(np.reshape(np.asarray(inp["L0_attn_time_maa_v"], np.float32), -1)),
            _vec_fm(np.reshape(np.asarray(inp["L0_attn_time_maa_r"], np.float32), -1)),
            _vec_fm(np.reshape(np.asarray(inp["L0_attn_time_maa_g"], np.float32), -1)),
            _vec_fm(np.reshape(np.asarray(inp["L0_attn_time_maa_x"], np.float32), -1)),
            _vec_fm(np.reshape(np.asarray(inp["L0_attn_time_decay"], np.float32), -1)),
            _vec_fm(np.reshape(np.asarray(inp["L0_attn_time_faaaa"], np.float32), -1)),
            _vec_fm(np.reshape(np.asarray(inp["L0_attn_ln_gamma"], np.float32), -1)),
            _vec_fm(np.reshape(np.asarray(inp["L0_attn_ln_beta"], np.float32), -1)),
        ], axis=0),
    }
    shared.update(consts)

    in_maps = []
    rows_ug = C // NCORES
    rows_d = EXP // NCORES
    for r in range(NCORES):
        d = dict(shared)
        blk = x[r * t_core:(r + 1) * t_core]
        halo = x[r * t_core - 1] if r > 0 else np.zeros(C, np.float32)
        fm = np.concatenate([halo[:, None], blk.T], axis=1)
        d["hid_fm"] = np.ascontiguousarray(fm.reshape(NCC, 128, t_core + 1))
        rm = np.zeros((128, 8), np.float32)
        rm[:, :r] = 1.0
        d["rankmask"] = rm
        d["qs_up"] = np.ascontiguousarray(wu_t[r * rows_ug:(r + 1) * rows_ug])
        d["qs_gate"] = np.ascontiguousarray(wgate_t[r * rows_ug:(r + 1) * rows_ug])
        d["qs_down"] = np.ascontiguousarray(wd_t[r * rows_d:(r + 1) * rows_d])
        in_maps.append(d)
    return in_maps, t_core



class SemCtr:
    def __init__(self, h):
        self.h = h
        self.v = 0

    def bump(self, n=1):
        self.v += n
        return self.v


def build_program(t_core, upto=None, dbg=()):
    nc = bass.Bass("TRN2", target_bir_lowering=False, debug=False)
    ntau = t_core // TAU
    ntc = t_core // TAUC
    nch_tc = TAUC // LC

    D = {}

    def dram(name, shape, dtype, kind="Internal"):
        if kind == "Internal" and name in dbg:
            kind = "ExternalOutput"
        D[name] = nc.dram_tensor(name, shape, dtype, kind=kind).ap()
        return D[name]

    dram("hid_fm", [NCC, 128, t_core + 1], F32, "ExternalInput")
    for n, shp in [("wr_t", [C, C]), ("wk_t", [C, C]), ("wv_t", [C, C]),
                   ("wg_t", [C, C]), ("wo_t", [C, C]),
                   ("wu_t", [C, EXP]), ("wgate_t", [C, EXP]), ("wd_t", [EXP, C]),
                   ("w1", [C, 160]), ("w2", [5, DMIX, C]), ("w2p", [128, NCC, 128]),
                   ("dw1", [C, DDEC]), ("dw2", [128, 8, 128]),
                   ("vecs", [10, 128, NCC]),
                   ("m8", [128, TAUC]), ("m32", [128, TAUC]), ("m64", [128, TAUC]),
                   ("gn_ones", [128, NCC, 32]),
                   ("ones_col", [128, 1]), ("ones_row", [1, 128]),
                   ("rankmask", [128, 8]),
                   ("qs_up", [C // NCORES, EXP]), ("qs_gate", [C // NCORES, EXP]),
                   ("qs_down", [EXP // NCORES, C])]:
        dram(n, shp, F32, "ExternalInput")
    dram("MA", [128, 512], BF16, "ExternalInput")
    for n in ["MB", "MC", "MD", "ME", "MF"]:
        dram(n, [128, 512], mybir.dt.uint8, "ExternalInput")
    dram("ident", [128, 128], BF16, "ExternalInput")
    dram("gn_sel", [32, NCC, 128], BF16, "ExternalInput")
    dram("gn_ones_bf", [128, NCC, 32], BF16, "ExternalInput")
    dram("ones_col_bf", [128, 1], BF16, "ExternalInput")

    dram("out_fm", [NCC, 128, t_core], F32, "ExternalOutput")

    dram("xf_sp", [5, NCC, 128, t_core], BF16)
    dram("r_sp", [NCC, 128, t_core], BF16)
    dram("k_sp", [NCC, 128, t_core], BF16)
    dram("g_sp", [NCC, 128, t_core], BF16)
    dram("vtm_sp", [t_core // 128, 128, C], BF16)
    dram("vtm2_sp", [t_core // 128, 128, C], BF16)
    dram("nu_sp", [NCC, 128, t_core], F32)
    dram("o_sp", [NCC, 128, t_core], F32)
    dram("ogn_sp", [NCC, 128, t_core], BF16)
    dram("x1f_sp", [NCC, 128, t_core], F32)
    dram("x1b_sp", [NCC, 128, t_core], BF16)
    dram("h_sp", [NET, 128, t_core], BF16)
    dram("wdq_sp", [NCC, 128, EXP // 128, 128], BF16)
    dram("sc_in", [1, 4], F32)
    dram("sc_out", [NCORES, 1, 4], F32)
    dram("cc_in", [128, 1040], F32)
    dram("cc_out", [NCORES, 128, 1040], F32)

    ctx = ExitStack()

    def sbuf(name, shape, dtype):
        return ctx.enter_context(nc.sbuf_tensor(name, shape, dtype))

    def psumt(name, shape):
        return ctx.enter_context(nc.psum_tensor(name, shape, F32))

    def mksem(name):
        return SemCtr(ctx.enter_context(nc.semaphore(name)))

    SB = {}
    SB["vecs"] = sbuf("sb_vecs", [128, 10, NCC], F32)
    for n in ["m8", "m32", "m64"]:
        SB[n] = sbuf("sb_" + n, [128, TAUC], F32)
    SB["MA"] = sbuf("sb_MA", [128, 512], BF16)
    for n in ["MB", "MC", "MD", "ME", "MF"]:
        SB[n] = sbuf("sb_" + n, [128, 512], mybir.dt.uint8)
    SB["ident"] = sbuf("sb_ident", [128, 128], BF16)
    SB["gn_ones_bf"] = sbuf("sb_gnonesb", [128, NCC, 32], BF16)
    SB["gn_sel"] = sbuf("sb_gnsel", [DMIX, NCC, 128], BF16)
    SB["ones_col"] = sbuf("sb_onescol", [128, 1], F32)
    SB["ones_col_bf"] = sbuf("sb_onescolb", [128, 1], BF16)
    SB["ones_row"] = sbuf("sb_onesrow", [1, 128], F32)
    SB["rankmask"] = sbuf("sb_rankmask", [128, 8], F32)
    SB["w1"] = sbuf("sb_w1", [128, NCC, 160], BF16)
    SB["w2"] = sbuf("sb_w2", [128, NCC, 128], BF16)
    SB["w2g"] = sbuf("sb_w2g", [DMIX, NCC, 128], BF16)
    SB["dw1"] = sbuf("sb_dw1", [128, NCC, DDEC], BF16)
    SB["dw2"] = sbuf("sb_dw2", [128, NCC // 2, 128], BF16)
    SB["S"] = sbuf("sb_S", [128, NCC, HS], F32)
    SB["Sb"] = sbuf("sb_Sb", [128, NCC, HS], BF16)
    SB["Sinb"] = sbuf("sb_Sinb", [128, NCC, HS], BF16)
    SB["hs"] = sbuf("sb_hs", [128, 8], F32)
    SB["Gfin"] = sbuf("sb_Gfin", [128, NCC], F32)
    SB["Pcl"] = sbuf("sb_Pcl", [128, 512], BF16)
    SB["scl"] = sbuf("sb_scl", [1, 40], F32)
    SB["eps"] = sbuf("sb_eps", [128, 2], F32)

    BW = sbuf("ar_BW", [128, 32768], BF16)
    B2 = sbuf("ar_B2", [128, 16384], BF16)
    B3 = sbuf("ar_B3", [128, 8192], BF16)
    F1 = sbuf("ar_F1", [128, 12288], F32)
    F3 = sbuf("ar_F3", [128, 2048], F32)

    PB = [psumt(f"pb{i}", [128, 512]) for i in range(8)]

    s_ds = mksem("s_ds")
    s_dx = mksem("s_dx")
    s_dg = mksem("s_dg")
    s_v = mksem("s_v")
    s_sc = mksem("s_sc")
    s_t = mksem("s_t")
    s_g = mksem("s_g")
    ALLSEMS = [s_ds, s_dx, s_dg, s_v, s_sc, s_t, s_g]

    steps = {e: [] for e in ENGINES}

    def em(engine, fn):
        steps[engine].append(fn)

    def fence():
        tgt = [(s.h, s.v) for s in ALLSEMS if s.v > 0]
        for e in ENGINES:
            def f(eng, tgt=tuple(tgt)):
                for h, v in tgt:
                    eng.wait_ge(h, v)
            em(e, f)

    VEC = {"maa_w": 0, "maa_k": 1, "maa_v": 2, "maa_r": 3, "maa_g": 4,
           "maa_x": 5, "td": 6, "u": 7, "ln_g": 8, "ln_b": 9}

    def vcol(name, cc):
        return SB["vecs"][:, VEC[name], cc:cc + 1]

    def spl(name, a, b, f=None):
        if f is None:
            return D[name][:, :, a:b].transpose([1, 0, 2])
        return D[name][f, :, :, a:b].transpose([1, 0, 2])

    def s3(ap, t=None):
        n = ap.shape[1]
        t = t if t is not None else n // NCC
        return ap.rearrange("p (c t) -> p c t", t=t)

    def ph_setup():
        def f(s):
            s.dma_start(out=SB["vecs"][:], in_=D["vecs"].rearrange("v p c -> p v c")).then_inc(s_ds.h, 16)
            for n in ["m8", "m32", "m64", "MA", "MB", "MC", "MD", "ME", "MF",
                      "ident", "ones_col", "ones_row", "rankmask", "gn_sel"]:
                s.dma_start(out=SB[n][:], in_=D[n]).then_inc(s_ds.h, 16)
            s.dma_start(out=SB["gn_ones_bf"][:], in_=D["gn_ones_bf"]).then_inc(s_ds.h, 16)
            s.dma_start(out=SB["ones_col_bf"][:], in_=D["ones_col_bf"]).then_inc(s_ds.h, 16)
        em("sync", f)
        s_ds.bump(16 * 17)

        def g(g):
            g.dma_start(out=SB["w1"][:], in_=D["w1"].rearrange("(cc p) f -> p cc f", p=128)).then_inc(s_dg.h, 16)
            g.dma_start(out=SB["w2"][:], in_=D["w2p"]).then_inc(s_dg.h, 16)
            g.dma_start(out=SB["w2g"][:], in_=D["w2"][4].rearrange("d (cc m) -> d cc m", m=128)).then_inc(s_dg.h, 16)
            g.dma_start(out=SB["dw1"][:], in_=D["dw1"].rearrange("(cc p) f -> p cc f", p=128)).then_inc(s_dg.h, 16)
            g.dma_start(out=SB["dw2"][:], in_=D["dw2"]).then_inc(s_dg.h, 16)
        em("gpsimd", g)
        s_dg.bump(16 * 5)

        def v(v):
            v.memset(SB["S"][:], 0.0)
            v.memset(SB["Sb"][:], 0.0)
            v.memset(SB["Sinb"][:], 0.0)
            v.memset(SB["Gfin"][:], 1.0)
            v.memset(SB["scl"][:], 0.0)
            v.memset(F3[:, 2044:2048], 0.0)
            v.memset(SB["eps"][:, 0:1], GN_EPS)
            v.memset(SB["eps"][:, 1:2], LN_EPS)
            v.drain()
            v.nop().then_inc(s_v.h, 1)
        em("vector", v)
        s_v.bump(1)

    def ph_scale():
        slabs = [("qs_up", 0, 2, 8192), ("qs_gate", 1, 2, 8192), ("qs_down", 2, 8, 2048)]
        for wname, wi, nsl, cols in slabs:
            for sl in range(nsl):
                prev_sc = s_sc.v

                def fl(s, wname=wname, sl=sl, cols=cols, prev_sc=prev_sc):
                    if prev_sc > 0:
                        s.wait_ge(s_sc.h, prev_sc)
                    s.dma_start(out=F1[:, 0:cols],
                                in_=D[wname][128 * sl:128 * (sl + 1), :]).then_inc(s_ds.h, 16)
                em("sync", fl)
                ld_t = s_ds.bump(16)

                prev_fv = s_v.v

                def fs(sc, wi=wi, sl=sl, cols=cols, ld_t=ld_t, prev_fv=prev_fv):
                    sc.wait_ge(s_ds.h, ld_t)
                    if prev_fv > 0:
                        sc.wait_ge(s_v.h, prev_fv)
                    sc.activation(out=F1[:, 0:cols], in_=F1[:, 0:cols], func=AF.Abs,
                                  accum_out=F3[:, 4 + wi:5 + wi]).then_inc(s_sc.h, 1)
                em("scalar", fs)
                sc_t = s_sc.bump(1)

                def fv(v, wi=wi, sl=sl, sc_t=sc_t):
                    v.wait_ge(s_sc.h, sc_t)
                    if sl == 0:
                        v.tensor_copy(out=F3[:, wi:wi + 1], in_=F3[:, 4 + wi:5 + wi])
                    else:
                        v.tensor_add(out=F3[:, wi:wi + 1], in0=F3[:, wi:wi + 1],
                                     in1=F3[:, 4 + wi:5 + wi])
                    v.drain()
                    v.nop().then_inc(s_v.h, 1)
                em("vector", fv)
                s_v.bump(1)
        vt = s_v.v

        def ft(t, vt=vt):
            t.wait_ge(s_v.h, vt)
            t.matmul(PB[0][0:1, 0:3], SB["ones_col"][:], F3[:, 0:3],
                     start=True, stop=True).then_inc(s_t.h, 1)
        em("tensor", ft)
        tt = s_t.bump(1)

        def fv2(v, tt=tt):
            v.wait_ge(s_t.h, tt)
            v.tensor_copy(out=SB["scl"][0:1, 0:3], in_=PB[0][0:1, 0:3])
            v.drain()
            v.nop().then_inc(s_v.h, 1)
        em("vector", fv2)
        vt2 = s_v.bump(1)

        dg0 = s_dg.v

        GWAIT1 = s_g.v + 1

        def fg(g, vt2=vt2, dg0=dg0, GWAIT1=GWAIT1):
            g.wait_ge(s_v.h, vt2)
            g.dma_start(out=D["sc_in"], in_=SB["scl"][0:1, 0:4]).then_inc(s_dg.h, 16)
            g.wait_ge(s_dg.h, dg0 + 16)
            g.collective_compute(
                "AllGather", ALU.bypass, replica_groups=[list(range(NCORES))],
                ins=[D["sc_in"]], outs=[D["sc_out"]],
            ).then_inc(s_g.h, 1)
            g.wait_ge(s_g.h, GWAIT1)
            g.dma_start(out=SB["scl"][0:1, 8:40].rearrange("p (r f) -> p r f", r=NCORES),
                        in_=D["sc_out"].transpose([1, 0, 2])).then_inc(s_dg.h, 16)
        em("gpsimd", fg)
        s_dg.bump(32)
        s_g.bump(1)
        dg_t = s_dg.v

        def fv3(v, dg_t=dg_t):
            v.wait_ge(s_dg.h, dg_t)
            for wi, numel in [(0, C * EXP), (1, C * EXP), (2, C * EXP)]:
                src = bass.AP(tensor=SB["scl"], offset=8 + wi, ap=[[40, 1], [4, 8]])
                v.tensor_reduce(out=SB["scl"][0:1, 4 + wi:5 + wi], in_=src,
                                axis=mybir.AxisListType.X, op=ALU.add)
                v.drain()
            v.tensor_scalar(out=SB["scl"][0:1, 0:3], in0=SB["scl"][0:1, 4:7],
                            scalar1=0.5 / float(C * EXP), scalar2=0.5e-8,
                            op0=ALU.mult, op1=ALU.add)
            v.drain()
            v.tensor_scalar_mul(out=SB["scl"][0:1, 4:7], in0=SB["scl"][0:1, 0:3],
                                scalar1=-1.0)
            v.drain()
            v.nop().then_inc(s_v.h, 1)
        em("vector", fv3)
        vt3 = s_v.bump(1)

        def ft2(t, vt3=vt3):
            t.wait_ge(s_v.h, vt3)
            t.matmul(PB[1][:, 0:8], SB["ones_row"][:], SB["scl"][0:1, 0:8],
                     start=True, stop=True).then_inc(s_t.h, 1)
        em("tensor", ft2)
        tt2 = s_t.bump(1)

        def fv4(v, tt2=tt2):
            v.wait_ge(s_t.h, tt2)
            v.tensor_copy(out=SB["hs"][:, 0:8], in_=PB[1][:, 0:8])
            v.drain()
            v.nop().then_inc(s_g.h, 1)
        em("vector", fv4)
        s_g.bump(1)

    def ph_A():
        hid = F1
        prev_st = 0
        for tau in range(ntau):
            prev_v = s_v.v

            def fl(s, tau=tau, prev_v=prev_v):
                if tau > 0:
                    s.wait_ge(s_v.h, prev_v)
                for cc in range(NCC):
                    s.dma_start(out=hid[:, 513 * cc:513 * (cc + 1)],
                                in_=D["hid_fm"][cc, :, TAU * tau:TAU * tau + 513]).then_inc(s_ds.h, 16)
            em("sync", fl)
            ld_t = s_ds.bump(16 * NCC)

            def fv(v, ld_t=ld_t):
                v.wait_ge(s_ds.h, ld_t)
                for cc in range(NCC):
                    v.tensor_sub(out=B2[:, 512 * cc:512 * (cc + 1)],
                                 in0=hid[:, 513 * cc:513 * cc + 512],
                                 in1=hid[:, 513 * cc + 1:513 * cc + 513])
                v.drain()
                for cc in range(NCC):
                    v.scalar_tensor_tensor(
                        out=B2[:, 8192 + 512 * cc:8192 + 512 * (cc + 1)],
                        in0=B2[:, 512 * cc:512 * (cc + 1)], scalar=vcol("maa_x", cc),
                        in1=hid[:, 513 * cc + 1:513 * cc + 513], op0=ALU.mult, op1=ALU.add)
                v.drain()
                v.nop().then_inc(s_v.h, 1)
            em("vector", fv)
            v_t = s_v.bump(1)

            def ft(t, v_t=v_t):
                t.wait_ge(s_v.h, v_t)
                for cc in range(NCC):
                    t.matmul(PB[0][:, :], SB["w1"][:, cc, 0:128],
                             B2[:, 8192 + 512 * cc:8192 + 512 * (cc + 1)],
                             start=(cc == 0), stop=(cc == 15))
                for cc in range(NCC):
                    ins = t.matmul(PB[1][0:32, :], SB["w1"][:, cc, 128:160],
                                   B2[:, 8192 + 512 * cc:8192 + 512 * (cc + 1)],
                                   start=(cc == 0), stop=(cc == 15))
                ins.then_inc(s_t.h, 1)
            em("tensor", ft)
            t_t = s_t.bump(1)

            def fs(sc, t_t=t_t):
                sc.wait_ge(s_t.h, t_t)
                sc.activation(out=BW[:, 0:512], in_=PB[0][:, :], func=AF.Tanh)
                sc.activation(out=BW[0:32, 512:1024], in_=PB[1][0:32, :],
                              func=AF.Tanh).then_inc(s_sc.h, 1)
            em("scalar", fs)
            sc_t = s_sc.bump(1)

            vmix0 = s_v.v
            for f in range(5):
                for cc in range(NCC):
                    idx = f * NCC + cc
                    bank = PB[2 + (idx % 2)]
                    wait_v = vmix0 + idx - 1 if idx >= 2 else None

                    def ftm(t, f=f, cc=cc, bank=bank, sc_t=sc_t, wait_v=wait_v):
                        if f == 0 and cc == 0:
                            t.wait_ge(s_sc.h, sc_t)
                        if wait_v is not None:
                            t.wait_ge(s_v.h, wait_v)
                        zsl = (BW[32 * f:32 * (f + 1), 0:512] if f < 4
                               else BW[0:32, 512:1024])
                        w2sl = (SB["w2"][32 * f:32 * (f + 1), cc, :] if f < 4
                                else SB["w2g"][:, cc, :])
                        tp = (32 * f, 0) if f < 4 else (0, 0)
                        t.matmul(bank[:, :], w2sl, zsl,
                                 start=True, stop=True,
                                 tile_position=tp).then_inc(s_t.h, 1)
                    em("tensor", ftm)
                    mm_t = s_t.bump(1)

                    def fvm(v, f=f, cc=cc, bank=bank, mm_t=mm_t):
                        v.wait_ge(s_t.h, mm_t)
                        v.scalar_tensor_tensor(
                            out=F3[:, 0:512], in0=bank[:, :],
                            scalar=vcol("maa_" + "wkvrg"[f], cc),
                            in1=B2[:, 512 * cc:512 * (cc + 1)],
                            op0=ALU.add, op1=ALU.mult)
                        v.drain()
                        v.tensor_add(out=B3[:, 512 * cc:512 * (cc + 1)],
                                     in0=F3[:, 0:512],
                                     in1=hid[:, 513 * cc + 1:513 * cc + 513])
                        v.drain()
                        v.nop().then_inc(s_v.h, 1)
                    em("vector", fvm)
                    s_v.bump(1)
                v_done = s_v.v

                def fst(s, f=f, tau=tau, v_done=v_done):
                    s.wait_ge(s_v.h, v_done)
                    s.dma_start(out=spl("xf_sp", TAU * tau, TAU * (tau + 1), f),
                                in_=s3(B3[:, 0:8192])).then_inc(s_ds.h, 16)
                em("sync", fst)
                st_done = s_ds.bump(16)

                if f < 4:
                    def fvw(v, st_done=st_done):
                        v.wait_ge(s_ds.h, st_done)
                    em("vector", fvw)
                else:
                    prev_st = st_done

    def ph_B():
        for wname, fidx, spname, act in [
                ("wr_t", 3, "r_sp", None), ("wk_t", 1, "k_sp", None),
                ("wg_t", 4, "g_sp", "silu")]:
            prev_t = s_t.v

            def flw(g, wname=wname, prev_t=prev_t):
                g.wait_ge(s_t.h, prev_t)
                for cc in range(NCC):
                    g.dma_start(out=BW[:, 2048 * cc:2048 * (cc + 1)],
                                in_=D[wname][128 * cc:128 * (cc + 1), :]).then_inc(s_dg.h, 16)
            em("gpsimd", flw)
            w_t = s_dg.bump(16 * NCC)

            for tau in range(ntau):
                prev_mm = s_t.v

                def flx(s, fidx=fidx, tau=tau, prev_mm=prev_mm):
                    if tau > 0:
                        s.wait_ge(s_t.h, prev_mm)
                    s.dma_start(out=s3(B2[:, 0:8192]),
                                in_=spl("xf_sp", TAU * tau, TAU * (tau + 1), fidx)).then_inc(s_ds.h, 16)
                em("sync", flx)
                x_t = s_ds.bump(16)

                sc0 = s_sc.v
                prev_fso = s_ds.v
                for mt in range(NCC):
                    bank = PB[mt % 4]
                    wait_sc = sc0 + mt - 3 if mt >= 4 else None

                    def ftp(t, mt=mt, bank=bank, w_t=w_t, x_t=x_t, wait_sc=wait_sc):
                        if mt == 0:
                            t.wait_ge(s_dg.h, w_t)
                            t.wait_ge(s_ds.h, x_t)
                        if wait_sc is not None:
                            t.wait_ge(s_sc.h, wait_sc)
                        for cc in range(NCC):
                            ins = t.matmul(bank[:, :],
                                           BW[:, 2048 * cc + 128 * mt:2048 * cc + 128 * (mt + 1)],
                                           B2[:, 512 * cc:512 * (cc + 1)],
                                           start=(cc == 0), stop=(cc == 15))
                        ins.then_inc(s_t.h, 1)
                    em("tensor", ftp)
                    mm_t = s_t.bump(1)

                    def fse(sc, mt=mt, bank=bank, mm_t=mm_t, act=act, x_t=x_t):
                        sc.wait_ge(s_t.h, mm_t)
                        if mt == 0:
                            sc.wait_ge(s_ds.h, x_t)
                        fn = AF.Silu if act == "silu" else AF.Copy
                        sc.activation(out=B2[:, 8192 + 512 * mt:8192 + 512 * (mt + 1)],
                                      in_=bank[:, :], func=fn).then_inc(s_sc.h, 1)
                    em("scalar", fse)
                    s_sc.bump(1)
                e_t = s_sc.v

                def fso(s, spname=spname, tau=tau, e_t=e_t):
                    s.wait_ge(s_sc.h, e_t)
                    s.dma_start(out=spl(spname, TAU * tau, TAU * (tau + 1)),
                                in_=s3(B2[:, 8192:16384])).then_inc(s_ds.h, 16)
                em("sync", fso)
                s_ds.bump(16)

        prev_t = s_t.v

        def flw2(g, prev_t=prev_t):
            g.wait_ge(s_t.h, prev_t)
            for cc in range(NCC):
                g.dma_start(out=BW[:, 2048 * cc:2048 * (cc + 1)],
                            in_=D["wv_t"][128 * cc:128 * (cc + 1), :]).then_inc(s_dg.h, 16)
        em("gpsimd", flw2)
        w_t = s_dg.bump(16 * NCC)

        for tau in range(ntau):
            prev_mm = s_t.v

            def flx(s, tau=tau, prev_mm=prev_mm):
                if tau > 0:
                    s.wait_ge(s_t.h, prev_mm)
                s.dma_start(out=s3(B2[:, 0:8192]),
                            in_=spl("xf_sp", TAU * tau, TAU * (tau + 1), 2)).then_inc(s_ds.h, 16)
            em("sync", flx)
            x_t = s_ds.bump(16)

            v0 = s_v.v
            prev_fso2 = s_ds.v
            for tt in range(4):
                for ac in range(4):
                    idx = tt * 4 + ac
                    bank = PB[idx % 4]
                    wait_v = v0 + idx - 3 if idx >= 4 else None

                    def ftv(t, tt=tt, ac=ac, bank=bank, w_t=w_t, x_t=x_t, wait_v=wait_v):
                        if tt == 0 and ac == 0:
                            t.wait_ge(s_dg.h, w_t)
                            t.wait_ge(s_ds.h, x_t)
                        if wait_v is not None:
                            t.wait_ge(s_v.h, wait_v)
                        for cc in range(NCC):
                            ins = t.matmul(bank[:, :],
                                           B2[:, 512 * cc + 128 * tt:512 * cc + 128 * (tt + 1)],
                                           BW[:, 2048 * cc + 512 * ac:2048 * cc + 512 * (ac + 1)],
                                           start=(cc == 0), stop=(cc == 15))
                        ins.then_inc(s_t.h, 1)
                    em("tensor", ftv)
                    mm_t = s_t.bump(1)

                    def fve(v, tt=tt, ac=ac, bank=bank, mm_t=mm_t, x_t=x_t):
                        v.wait_ge(s_t.h, mm_t)
                        if tt == 0 and ac == 0:
                            v.wait_ge(s_ds.h, x_t)
                        v.tensor_copy(out=B3[:, 2048 * tt + 512 * ac:2048 * tt + 512 * (ac + 1)],
                                      in_=bank[:, :])
                        v.drain()
                        v.nop().then_inc(s_v.h, 1)
                    em("vector", fve)
                    s_v.bump(1)
            v_done = s_v.v

            def fso2(s, tau=tau, v_done=v_done):
                s.wait_ge(s_v.h, v_done)
                for tt in range(4):
                    s.dma_start(out=D["vtm_sp"][4 * tau + tt],
                                in_=B3[:, 2048 * tt:2048 * (tt + 1)]).then_inc(s_ds.h, 16)
                for tt in range(4):
                    s.dma_start(out=D["vtm2_sp"][4 * tau + tt][0:64, :],
                                in_=B3[64:128, 2048 * tt:2048 * (tt + 1)]).then_inc(s_ds.h, 16)
                    s.dma_start(out=D["vtm2_sp"][4 * tau + tt][64:128, :],
                                in_=B3[0:64, 2048 * tt:2048 * (tt + 1)]).then_inc(s_ds.h, 16)
            em("sync", fso2)
            s_ds.bump(16 * 12)

        for tau in range(ntau):
            prev_t = s_t.v
            prev_b3 = s_ds.v
            prev_fsn = s_ds.v

            def flx(s, tau=tau, prev_t=prev_t):
                if tau > 0:
                    s.wait_ge(s_t.h, prev_t)
                s.dma_start(out=s3(B2[:, 0:8192]),
                            in_=spl("xf_sp", TAU * tau, TAU * (tau + 1), 0)).then_inc(s_ds.h, 16)
            em("sync", flx)
            x_t = s_ds.bump(16)

            def ftz(t, x_t=x_t):
                t.wait_ge(s_ds.h, x_t)
                for cc in range(NCC):
                    t.matmul(PB[4][0:64, :], SB["dw1"][:, cc, :],
                             B2[:, 512 * cc:512 * (cc + 1)],
                             start=(cc == 0), stop=(cc == 15))
                for cc in range(NCC):
                    ins = t.matmul(PB[4][64:128, :], SB["dw1"][:, cc, :],
                                   B2[:, 512 * cc:512 * (cc + 1)],
                                   start=(cc == 0), stop=(cc == 15),
                                   tile_position=(0, 64))
                ins.then_inc(s_t.h, 1)
            em("tensor", ftz)
            tz = s_t.bump(1)

            def fsz(sc, tz=tz, x_t=x_t):
                sc.wait_ge(s_t.h, tz)
                sc.wait_ge(s_ds.h, x_t)
                sc.activation(out=B3[:, 0:512], in_=PB[4][:, :],
                              func=AF.Tanh).then_inc(s_sc.h, 1)
            em("scalar", fsz)
            scz = s_sc.bump(1)

            v0 = s_v.v
            for cc in range(NCC):
                bank = PB[5 + (cc % 2)]
                wait_v = v0 + cc - 1 if cc >= 2 else None

                def ftd(t, cc=cc, bank=bank, scz=scz, wait_v=wait_v):
                    if cc == 0:
                        t.wait_ge(s_sc.h, scz)
                    if wait_v is not None:
                        t.wait_ge(s_v.h, wait_v)
                    zrow = 64 * (cc % 2)
                    t.matmul(bank[:, :], SB["dw2"][zrow:zrow + 64, cc // 2, :],
                             B3[zrow:zrow + 64, 0:512],
                             start=True, stop=True, tile_position=(zrow, 0)).then_inc(s_t.h, 1)
                em("tensor", ftd)
                td = s_t.bump(1)

                def fsd(sc, cc=cc, bank=bank, td=td, x_t=x_t):
                    sc.wait_ge(s_t.h, td)
                    if cc == 0:
                        sc.wait_ge(s_ds.h, x_t)
                    sc.activation(out=F1[:, 512 * cc:512 * (cc + 1)], in_=bank[:, :],
                                  func=AF.Exp, bias=vcol("td", cc)).then_inc(s_sc.h, 1)
                em("scalar", fsd)
                sd = s_sc.bump(1)

                def fvd(v, cc=cc, sd=sd):
                    v.wait_ge(s_sc.h, sd)
                    v.tensor_scalar_min(out=F1[:, 512 * cc:512 * (cc + 1)],
                                        in0=F1[:, 512 * cc:512 * (cc + 1)],
                                        scalar1=NU_CLAMP)
                    v.drain()
                    v.nop().then_inc(s_v.h, 1)
                em("vector", fvd)
                s_v.bump(1)
            vnu = s_v.v

            def fsn(s, tau=tau, vnu=vnu):
                s.wait_ge(s_v.h, vnu)
                s.dma_start(out=spl("nu_sp", TAU * tau, TAU * (tau + 1)),
                            in_=s3(F1[:, 0:8192])).then_inc(s_ds.h, 16)
            em("sync", fsn)
            s_ds.bump(16)

    FK8, FKA, FKF2, FKF3, FK32, FR32, FR8, FK0 = range(8)

    def bwsl(slot, cc, a=0, b=TAUC):
        return BW[:, 4096 * slot + 256 * cc + a:4096 * slot + 256 * cc + b]

    def ph_C():
        prev_oso = 0
        prev_sO_last = None
        for tc in range(ntc):
            prev_t = s_t.v
            prev_v = s_v.v

            def fl(s, tc=tc, prev_t=prev_t, prev_v=prev_v):
                if tc > 0:
                    s.wait_ge(s_t.h, prev_t)
                    s.wait_ge(s_v.h, prev_v)
                s.dma_start(out=s3(B2[:, 0:4096]),
                            in_=spl("k_sp", TAUC * tc, TAUC * (tc + 1))).then_inc(s_ds.h, 16)
                s.dma_start(out=s3(B2[:, 4096:8192]),
                            in_=spl("r_sp", TAUC * tc, TAUC * (tc + 1))).then_inc(s_ds.h, 16)
                s.dma_start(out=s3(F1[:, 0:4096]),
                            in_=spl("nu_sp", TAUC * tc, TAUC * (tc + 1))).then_inc(s_ds.h, 16)
                s.dma_start(out=B3[:, 0:2048], in_=D["vtm_sp"][2 * tc]).then_inc(s_ds.h, 16)
                s.dma_start(out=B3[:, 2048:4096], in_=D["vtm_sp"][2 * tc + 1]).then_inc(s_ds.h, 16)
                for b in range(2):
                    s.dma_start(out=B2[:, 8192 + 2048 * b:8192 + 2048 * (b + 1)],
                                in_=D["vtm2_sp"][2 * tc + b]).then_inc(s_ds.h, 16)
            em("sync", fl)
            ld_t = s_ds.bump(16 * 7)

            def scn_(cc):
                return F1[:, 4096 + 256 * cc:4096 + 256 * (cc + 1)]

            def nu_(cc):
                return F1[:, 256 * cc:256 * (cc + 1)]

            def k_(cc):
                return B2[:, 256 * cc:256 * (cc + 1)]

            def r_(cc):
                return B2[:, 4096 + 256 * cc:4096 + 256 * (cc + 1)]

            def scr_(cc):
                return F1[:, 8192 + 256 * cc:8192 + 256 * (cc + 1)]

            def k64_(cc, a=0, b=TAUC):
                return B2[:, 12288 + 256 * cc + a:12288 + 256 * cc + b]

            def sV1(v, ld_t=ld_t):
                v.wait_ge(s_ds.h, ld_t)
                for cc in range(NCC):
                    v.tensor_tensor_scan(out=scn_(cc), data0=SB["m8"][:, 0:TAUC],
                                         data1=nu_(cc), initial=0.0,
                                         op0=ALU.mult, op1=ALU.add)
                v.drain()
                v.nop().then_inc(s_v.h, 1)
            em("vector", sV1)
            s_v.bump(1)

            def sS1(sc, vw=s_v.v):
                sc.wait_ge(s_v.h, vw)
                for cc in range(NCC):
                    sc.activation(out=bwsl(FK8, cc), in_=scn_(cc), func=AF.Exp)
                    sc.activation(out=bwsl(FR8, cc, 1, TAUC), in_=scn_(cc)[:, 0:TAUC - 1],
                                  func=AF.Exp, scale=-1.0)
                    g8 = bass.AP(tensor=F1, offset=4096 + 256 * cc + 7,
                                 ap=[[12288, 128], [8, 32]])
                    ins = sc.activation(out=F3[:, 34 * cc:34 * cc + 32], in_=g8,
                                        func=AF.Exp, scale=-1.0)
                ins.then_inc(s_sc.h, 1)
            em("scalar", sS1)
            s_sc.bump(1)

            def sV23(v, sw=s_sc.v):
                v.wait_ge(s_sc.h, sw)
                for cc in range(NCC):
                    fix = bass.AP(tensor=BW, offset=4096 * FR8 + 256 * cc,
                                  ap=[[32768, 128], [8, 32]])
                    v.memset(fix, 1.0)
                    v.memset(F3[:, 34 * cc + 32:34 * cc + 34], 0.0)
                v.drain()
                for cc in range(NCC):
                    v.tensor_mul(out=bwsl(FK8, cc), in0=bwsl(FK8, cc), in1=k_(cc))
                    v.tensor_mul(out=bwsl(FR8, cc), in0=bwsl(FR8, cc), in1=r_(cc))
                v.drain()
                for cc in range(NCC):
                    e8b = bass.AP(tensor=F3, offset=34 * cc,
                                  ap=[[2048, 128], [1, 32], [0, 8]])
                    v.tensor_tensor(out=bwsl(FKA, cc), in0=bwsl(FK8, cc), in1=e8b,
                                    op=ALU.mult)
                v.drain()
                for cc in range(NCC):
                    e8b1 = bass.AP(tensor=F3, offset=34 * cc + 1,
                                   ap=[[2048, 128], [1, 32], [0, 8]])
                    v.tensor_tensor(out=bwsl(FKF2, cc), in0=bwsl(FKA, cc), in1=e8b1,
                                    op=ALU.mult)
                v.drain()
                for cc in range(NCC):
                    e8b2 = bass.AP(tensor=F3, offset=34 * cc + 2,
                                   ap=[[2048, 128], [1, 32], [0, 8]])
                    v.tensor_tensor(out=bwsl(FKF3, cc), in0=bwsl(FKF2, cc), in1=e8b2,
                                    op=ALU.mult)
                v.drain()
                for cc in range(NCC):
                    v.tensor_tensor_scan(out=scn_(cc), data0=SB["m32"][:, 0:TAUC],
                                         data1=nu_(cc), initial=0.0,
                                         op0=ALU.mult, op1=ALU.add)
                v.drain()
                v.nop().then_inc(s_v.h, 1)
            em("vector", sV23)
            s_v.bump(1)

            def sS2(sc, vw=s_v.v):
                sc.wait_ge(s_v.h, vw)
                for cc in range(NCC):
                    sc.activation(out=bwsl(FR32, cc, 1, TAUC), in_=scn_(cc)[:, 0:TAUC - 1],
                                  func=AF.Exp, scale=-1.0)
                    g32 = bass.AP(tensor=F1, offset=4096 + 256 * cc + 31,
                                  ap=[[12288, 128], [32, 8]])
                    ins = sc.activation(out=F3[:, 544 + 8 * cc:544 + 8 * cc + 8], in_=g32,
                                        func=AF.Copy)
                ins.then_inc(s_sc.h, 1)
            em("scalar", sS2)
            s_sc.bump(1)

            def sV4(v, sw=s_sc.v):
                v.wait_ge(s_sc.h, sw)
                for cc in range(NCC):
                    fix = bass.AP(tensor=BW, offset=4096 * FR32 + 256 * cc,
                                  ap=[[32768, 128], [32, 8]])
                    v.memset(fix, 1.0)
                v.drain()
                for cc in range(NCC):
                    v.tensor_mul(out=bwsl(FR32, cc), in0=bwsl(FR32, cc), in1=r_(cc))
                for cc in range(NCC):
                    b32b = bass.AP(tensor=F3, offset=544 + 8 * cc,
                                   ap=[[2048, 128], [1, 8], [0, 32]])
                    v.tensor_tensor(out=scr_(cc), in0=scn_(cc), in1=b32b,
                                    op=ALU.subtract)
                v.drain()
                v.nop().then_inc(s_v.h, 1)
            em("vector", sV4)
            s_v.bump(1)

            def sS3(sc, vw=s_v.v):
                sc.wait_ge(s_v.h, vw)
                for cc in range(NCC):
                    ins = sc.activation(out=bwsl(FK32, cc), in_=scr_(cc), func=AF.Exp)
                ins.then_inc(s_sc.h, 1)
            em("scalar", sS3)
            s_sc.bump(1)

            def sV5(v, sw=s_sc.v):
                v.wait_ge(s_sc.h, sw)
                for cc in range(NCC):
                    v.tensor_mul(out=bwsl(FK32, cc), in0=bwsl(FK32, cc), in1=k_(cc))
                for cc in range(NCC):
                    v.tensor_tensor_scan(out=scn_(cc), data0=SB["m64"][:, 0:TAUC],
                                         data1=nu_(cc), initial=0.0,
                                         op0=ALU.mult, op1=ALU.add)
                v.drain()
                v.nop().then_inc(s_v.h, 1)
            em("vector", sV5)
            s_v.bump(1)

            def sS4(sc, vw=s_v.v):
                sc.wait_ge(s_v.h, vw)
                for cc in range(NCC):
                    g64 = bass.AP(tensor=F1, offset=4096 + 256 * cc + 63,
                                  ap=[[12288, 128], [64, 4]])
                    sc.activation(out=F3[:, 736 + 4 * cc:736 + 4 * cc + 4], in_=g64,
                                  func=AF.Copy)
                    ins = sc.activation(out=F3[:, 672 + 4 * cc:672 + 4 * cc + 4], in_=g64,
                                        func=AF.Exp, scale=-1.0)
                ins.then_inc(s_sc.h, 1)
            em("scalar", sS4)
            s_sc.bump(1)

            def sV6(v, sw=s_sc.v):
                v.wait_ge(s_sc.h, sw)
                for cc in range(NCC):
                    b64b = bass.AP(tensor=F3, offset=736 + 4 * cc,
                                   ap=[[2048, 128], [1, 4], [0, 64]])
                    v.tensor_tensor(out=scr_(cc), in0=scn_(cc), in1=b64b,
                                    op=ALU.subtract)
                for cc in range(NCC):
                    v.tensor_scalar_mul(out=bwsl(FK0, cc), in0=k_(cc),
                                        scalar1=vcol("u", cc))
                v.drain()
                v.nop().then_inc(s_v.h, 1)
            em("vector", sV6)
            s_v.bump(1)

            def sS5(sc, vw=s_v.v):
                sc.wait_ge(s_v.h, vw)
                for cc in range(NCC):
                    ins = sc.activation(out=k64_(cc), in_=scr_(cc), func=AF.Exp)
                ins.then_inc(s_sc.h, 1)
            em("scalar", sS5)
            s_sc.bump(1)

            def sV7(v, sw=s_sc.v):
                v.wait_ge(s_sc.h, sw)
                for cc in range(NCC):
                    v.tensor_mul(out=k64_(cc), in0=k64_(cc), in1=k_(cc))
                v.drain()
                v.nop().then_inc(s_v.h, 1)
            em("vector", sV7)
            s_v.bump(1)

            def sT1(t, vw=s_v.v):
                t.wait_ge(s_v.h, vw)
                for cc in range(NCC):
                    bank, boff = PB[cc // 4], 128 * (cc % 4)
                    for b in range(2):
                        ins = t.transpose(
                            bank[:, boff + 64 * b:boff + 64 * (b + 1)].bitcast(BF16),
                            B2[:, 12288 + 256 * cc + 128 * b:12288 + 256 * cc + 128 * (b + 1)],
                            SB["ident"][:])
                ins.then_inc(s_t.h, 1)
            em("tensor", sT1)
            tT1 = s_t.bump(1)

            def sV8(v, tT1=tT1):
                v.wait_ge(s_t.h, tT1)
                for cc in range(NCC):
                    bank, boff = PB[cc // 4], 128 * (cc % 4)
                    v.tensor_copy(out=B3[:, 4096 + 256 * cc:4096 + 256 * (cc + 1)],
                                  in_=bank[:, boff:boff + 128].bitcast(BF16))
                v.drain()
                v.nop().then_inc(s_v.h, 1)
            em("vector", sV8)
            s_v.bump(1)

            def sS6(sc, tT1=tT1):
                sc.wait_ge(s_t.h, tT1)
                for cc in range(NCC):
                    ins = sc.activation(out=k64_(cc, 1, TAUC), in_=scn_(cc)[:, 0:TAUC - 1],
                                        func=AF.Exp, scale=-1.0)
                ins.then_inc(s_sc.h, 1)
            em("scalar", sS6)
            s_sc.bump(1)

            def sV9(v, sw=s_sc.v):
                v.wait_ge(s_sc.h, sw)
                for cc in range(NCC):
                    fix = bass.AP(tensor=B2, offset=12288 + 256 * cc,
                                  ap=[[16384, 128], [64, 4]])
                    v.memset(fix, 1.0)
                v.drain()
                for cc in range(NCC):
                    v.tensor_mul(out=k64_(cc), in0=k64_(cc), in1=r_(cc))
                v.drain()
                v.nop().then_inc(s_v.h, 1)
            em("vector", sV9)
            fac_done = s_v.bump(1)

            nit = 2 * nch_tc

            def pcl_(i, po=0, fo=0, p=128, w=512):
                if i % 2 == 0:
                    return SB["Pcl"][po:po + p, fo:fo + w]
                return F3[po:po + p, 1536 + fo // 2:1536 + (fo + w) // 2].bitcast(BF16)

            def mk_ftp(i, wv):
                ch, half = divmod(i, 2)
                t0 = LC * ch

                def ftp(t, t0=t0, half=half, wv=wv):
                    t.wait_ge(s_v.h, wv)
                    for ti, cc in enumerate(range(8 * half, 8 * (half + 1))):
                        for hh in range(2):
                            po, fo = 64 * hh, 64 * ti
                            tp = (64 * hh, 64 * hh)
                            for bank, slot in [(PB[0], FK8), (PB[1], FKA),
                                               (PB[2], FKF2), (PB[3], FKF3),
                                               (PB[4], FK32)]:
                                rslot = FR32 if slot == FK32 else FR8
                                t.matmul(bank[po:po + 64, fo:fo + 64],
                                         bwsl(slot, cc, t0, t0 + LC)[64 * hh:64 * hh + 64, :],
                                         bwsl(rslot, cc, t0, t0 + LC)[64 * hh:64 * hh + 64, :],
                                         start=True, stop=True, tile_position=tp)
                            ins = t.matmul(
                                PB[5][po:po + 64, fo:fo + 64],
                                bwsl(FK0, cc, t0, t0 + LC)[64 * hh:64 * hh + 64, :],
                                B2[64 * hh:64 * hh + 64, 4096 + 256 * cc + t0:4096 + 256 * cc + t0 + LC],
                                start=True, stop=True, tile_position=tp)
                    ins.then_inc(s_t.h, 1)
                return ftp

            def mk_fvc(i, wt):
                def fvc(v, i=i, wt=wt):
                    v.wait_ge(s_t.h, wt)
                    v.tensor_tensor(out=pcl_(i), in0=PB[0][:, :], in1=SB["MA"][:],
                                    op=ALU.mult)
                    v.drain()
                    for mask, bank in [("MB", PB[1]), ("MC", PB[2]), ("MD", PB[3]),
                                       ("ME", PB[4]), ("MF", PB[5])]:
                        v.select(out=pcl_(i), mask=SB[mask][:], on_true=bank[:, :],
                                 on_false=pcl_(i), add_drain=True)
                        v.drain()
                    v.nop().then_inc(s_v.h, 1)
                return fvc

            def mk_fti(i, wv, wsc):
                ch, half = divmod(i, 2)
                t0 = LC * ch

                def fti(t, i=i, t0=t0, half=half, wv=wv, wsc=wsc):
                    t.wait_ge(s_v.h, wv)
                    if wsc is not None:
                        t.wait_ge(s_sc.h, wsc)
                    vb = t0 // 128
                    vrow = 64 * ((t0 // 64) % 2)
                    for ti, cc in enumerate(range(8 * half, 8 * (half + 1))):
                        for hh in range(2):
                            po, fo = 64 * hh, 64 * ti
                            hd = 2 * cc + hh
                            t.matmul(PB[6][po:po + 64, fo:fo + 64],
                                     SB["Sb"][64 * hh:64 * hh + 64, cc, :],
                                     B2[64 * hh:64 * hh + 64, 12288 + 256 * cc + t0:12288 + 256 * cc + t0 + LC],
                                     start=True, stop=False, tile_position=(64 * hh, po))
                            if po == vrow:
                                vsrc = B3[po:po + 64, 2048 * vb + 64 * hd:2048 * vb + 64 * hd + 64]
                            else:
                                vsrc = B2[po:po + 64, 8192 + 2048 * vb + 64 * hd:8192 + 2048 * vb + 64 * hd + 64]
                            t.matmul(PB[6][po:po + 64, fo:fo + 64], vsrc,
                                     pcl_(i, po, fo, 64, 64),
                                     start=False, stop=True, tile_position=(po, po))
                            ins = t.matmul(
                                PB[7][po:po + 64, fo:fo + 64],
                                B3[vrow:vrow + 64, 4096 + 256 * cc + 128 * vb + 64 * hh:4096 + 256 * cc + 128 * vb + 64 * hh + 64],
                                B3[vrow:vrow + 64, 2048 * vb + 64 * hd:2048 * vb + 64 * hd + 64],
                                start=True, stop=True, tile_position=(vrow, po))
                    ins.then_inc(s_t.h, 1)
                return fti

            def mk_fve(i, wt):
                ch, half = divmod(i, 2)

                def fve(v, half=half, ch=ch, wt=wt):
                    v.wait_ge(s_t.h, wt)
                    sap = bass.AP(tensor=SB["S"], offset=64 * 8 * half,
                                  ap=[[1024, 128], [64, 8], [1, 64]])
                    d64 = bass.AP(tensor=F3, offset=672 + 4 * 8 * half + ch,
                                  ap=[[2048, 128], [4, 8], [0, 64]])
                    v.tensor_tensor(out=F3[:, 1024:1536], in0=sap, in1=d64, op=ALU.mult)
                    v.drain()
                    v.tensor_add(out=sap, in0=F3[:, 1024:1536], in1=PB[7][:, :])
                    v.drain()
                    sbap = bass.AP(tensor=SB["Sb"], offset=64 * 8 * half,
                                   ap=[[1024, 128], [64, 8], [1, 64]])
                    v.tensor_copy(out=sbap, in_=sap)
                    v.drain()
                    v.nop().then_inc(s_v.h, 1)
                return fve

            def mk_foe(i, wt):
                ch, half = divmod(i, 2)
                t0 = LC * ch

                def foe(sc, half=half, t0=t0, wt=wt):
                    sc.wait_ge(s_t.h, wt)
                    oap = bass.AP(tensor=F1, offset=8192 + 256 * 8 * half + t0,
                                  ap=[[12288, 128], [256, 8], [1, 64]])
                    sc.activation(out=oap, in_=PB[6][:, :],
                                  func=AF.Copy).then_inc(s_sc.h, 1)
                return foe

            tPl = [None] * nit
            vCl = [None] * nit
            tIl = [None] * nit
            vEl = [None] * nit
            sOl = [None] * nit
            for i in range(nit):
                em("tensor", mk_ftp(i, vCl[i - 1] if i > 0 else fac_done))
                tPl[i] = s_t.bump(1)
                em("vector", mk_fvc(i, tPl[i]))
                vCl[i] = s_v.bump(1)
                if i >= 1:
                    j = i - 1
                    wv = vEl[j - 1] if j >= 1 else vCl[0]
                    wsc = sOl[j - 1] if j >= 1 else prev_sO_last
                    em("tensor", mk_fti(j, wv, wsc))
                    tIl[j] = s_t.bump(1)
                    em("vector", mk_fve(j, tIl[j]))
                    vEl[j] = s_v.bump(1)
                    em("scalar", mk_foe(j, tIl[j]))
                    sOl[j] = s_sc.bump(1)
            j = nit - 1
            em("tensor", mk_fti(j, vEl[j - 1], sOl[j - 1]))
            tIl[j] = s_t.bump(1)
            em("vector", mk_fve(j, tIl[j]))
            vEl[j] = s_v.bump(1)
            em("scalar", mk_foe(j, tIl[j]))
            sOl[j] = s_sc.bump(1)
            prev_sO_last = sOl[j]

            def fso(s, tc=tc, wsc=sOl[nit - 1]):
                s.wait_ge(s_sc.h, wsc)
                s.dma_start(out=spl("o_sp", TAUC * tc, TAUC * (tc + 1)),
                            in_=s3(F1[:, 8192:12288])).then_inc(s_ds.h, 16)
            em("sync", fso)
            prev_oso = s_ds.bump(16)

            def fvg(v):
                for ch in range(nch_tc):
                    d64 = bass.AP(tensor=F3, offset=672 + ch, ap=[[2048, 128], [4, NCC]])
                    v.tensor_tensor(out=SB["Gfin"][:], in0=SB["Gfin"][:], in1=d64,
                                    op=ALU.mult)
                    v.drain()
                v.nop().then_inc(s_v.h, 1)
            em("vector", fvg)
            s_v.bump(1)

    def ph_AG():
        vS = s_v.v
        dg0 = s_dg.v

        def fg(g, vS=vS, dg0=dg0):
            g.wait_ge(s_v.h, vS)
            g.dma_start(out=D["cc_in"][:, 0:1024],
                        in_=SB["S"][:].rearrange("p c m -> p (c m)")).then_inc(s_dg.h, 16)
            g.dma_start(out=D["cc_in"][:, 1024:1040], in_=SB["Gfin"][:]).then_inc(s_dg.h, 16)
            g.wait_ge(s_dg.h, dg0 + 32)
            g.collective_compute(
                "AllGather", ALU.bypass, replica_groups=[list(range(NCORES))],
                ins=[D["cc_in"]], outs=[D["cc_out"]],
            ).then_inc(s_g.h, 1)
        em("gpsimd", fg)
        s_dg.bump(32)
        gT = s_g.bump(1)

        def fl(s, gT=gT):
            s.wait_ge(s_g.h, gT)
            s.dma_start(out=F1[:, 0:8320].rearrange("p (r f) -> p r f", r=NCORES),
                        in_=D["cc_out"].transpose([1, 0, 2])).then_inc(s_ds.h, 16)
        em("sync", fl)
        ld = s_ds.bump(16)

        def fv(v, ld=ld):
            v.wait_ge(s_ds.h, ld)
            v.memset(F1[:, 8320:9344], 0.0)
            v.memset(F3[:, 0:NCC], 1.0)
            v.drain()
            for j in range(NCORES - 1, -1, -1):
                v.tensor_scalar_mul(out=F3[:, 16:32], in0=F3[:, 0:NCC],
                                    scalar1=SB["rankmask"][:, j:j + 1])
                v.drain()
                facb = bass.AP(tensor=F3, offset=16, ap=[[2048, 128], [1, NCC], [0, HS]])
                v.tensor_tensor(out=F3[:, 1024:2048],
                                in0=F1[:, 1040 * j:1040 * j + 1024], in1=facb, op=ALU.mult)
                v.drain()
                v.tensor_add(out=F1[:, 8320:9344], in0=F1[:, 8320:9344],
                             in1=F3[:, 1024:2048])
                v.drain()
                v.tensor_single_scalar(out=F3[:, 32:48],
                                       in_=F1[:, 1040 * j + 1024:1040 * j + 1040],
                                       scalar=1.0, op=ALU.subtract)
                v.drain()
                v.tensor_scalar(out=F3[:, 32:48], in0=F3[:, 32:48],
                                scalar1=SB["rankmask"][:, j:j + 1], scalar2=1.0,
                                op0=ALU.mult, op1=ALU.add)
                v.drain()
                v.tensor_mul(out=F3[:, 0:NCC], in0=F3[:, 0:NCC], in1=F3[:, 32:48])
                v.drain()
            v.tensor_copy(out=SB["Sinb"][:].rearrange("p c m -> p (c m)"),
                          in_=F1[:, 8320:9344])
            v.drain()
            v.memset(F3[:, 2044:2048], 0.0)
            v.drain()
            v.nop().then_inc(s_v.h, 1)
        em("vector", fv)
        vcomb = s_v.bump(1)

        if "ag_dbg" in dbg:
            D["ag_dbg"] = nc.dram_tensor("ag_dbg", [NCORES, 128, 1040], F32,
                                         kind="ExternalOutput").ap()
            D["sin_dbg"] = nc.dram_tensor("sin_dbg", [128, NCC, HS], BF16,
                                          kind="ExternalOutput").ap()

            def fdbg(s, vcomb=vcomb):
                s.wait_ge(s_v.h, vcomb)
                s.dma_start(out=D["ag_dbg"], in_=D["cc_out"]).then_inc(s_ds.h, 16)
                s.dma_start(out=D["sin_dbg"], in_=SB["Sinb"][:]).then_inc(s_ds.h, 16)
            em("sync", fdbg)
            s_ds.bump(32)

    def ph_C2():
        prev_fsg = 0
        for tc in range(ntc):
            prev_t = s_t.v
            prev_v = s_v.v

            def fl(s, tc=tc, prev_t=prev_t, prev_v=prev_v):
                if tc > 0:
                    s.wait_ge(s_t.h, prev_t)
                    s.wait_ge(s_v.h, prev_v)
                s.dma_start(out=s3(F1[:, 8192:12288]),
                            in_=spl("o_sp", TAUC * tc, TAUC * (tc + 1))).then_inc(s_ds.h, 16)
                s.dma_start(out=s3(F1[:, 0:4096]),
                            in_=spl("nu_sp", TAUC * tc, TAUC * (tc + 1))).then_inc(s_ds.h, 16)
                s.dma_start(out=s3(B2[:, 4096:8192]),
                            in_=spl("r_sp", TAUC * tc, TAUC * (tc + 1))).then_inc(s_ds.h, 16)
                s.dma_start(out=s3(B2[:, 0:4096]),
                            in_=spl("g_sp", TAUC * tc, TAUC * (tc + 1))).then_inc(s_ds.h, 16)
            em("sync", fl)
            ld = s_ds.bump(16 * 4)

            moff = (TAUC * tc) % TAU
            for cc in range(NCC):
                nu = F1[:, 256 * cc:256 * (cc + 1)]
                scn = F1[:, 4096 + 256 * cc:4096 + 256 * (cc + 1)]
                r = B2[:, 4096 + 256 * cc:4096 + 256 * (cc + 1)]

                def fv(v, nu=nu, scn=scn, moff=moff, ld=ld, first=(cc == 0)):
                    if first:
                        v.wait_ge(s_ds.h, ld)
                    v.tensor_tensor_scan(out=scn, data0=SB["m64"][:, 0:TAUC],
                                         data1=nu, initial=0.0, op0=ALU.mult, op1=ALU.add)
                    v.drain()
                    v.nop().then_inc(s_v.h, 1)
                em("vector", fv)
                vb = s_v.bump(1)

                def fs(sc, cc=cc, scn=scn, vb=vb):
                    sc.wait_ge(s_v.h, vb)
                    sc.activation(out=B3[:, 256 * cc + 1:256 * (cc + 1)],
                                  in_=scn[:, 0:TAUC - 1], func=AF.Exp, scale=-1.0)
                    g64 = bass.AP(tensor=F1, offset=4096 + 256 * cc + 63,
                                  ap=[[12288, 128], [64, 4]])
                    sc.activation(out=F3[:, 672 + 4 * cc:672 + 4 * cc + 4], in_=g64,
                                  func=AF.Exp, scale=-1.0).then_inc(s_sc.h, 1)
                em("scalar", fs)
                sR = s_sc.bump(1)

                def fv2(v, cc=cc, r=r, sR=sR, tc=tc):
                    v.wait_ge(s_sc.h, sR)
                    fix = bass.AP(tensor=B3, offset=256 * cc, ap=[[8192, 128], [64, 4]])
                    v.memset(fix, 1.0)
                    if tc == 0:
                        v.memset(F3[:, 1792 + cc:1792 + cc + 1], 1.0)
                    v.drain()
                    v.tensor_tensor_scan(out=F3[:, 832 + 4 * cc:832 + 4 * cc + 4],
                                         data0=F3[:, 672 + 4 * cc:672 + 4 * cc + 4],
                                         data1=F3[:, 2044:2048],
                                         initial=F3[:, 1792 + cc:1792 + cc + 1],
                                         op0=ALU.mult, op1=ALU.add)
                    v.drain()
                    v.tensor_copy(out=F3[:, 896 + 4 * cc + 1:896 + 4 * cc + 4],
                                  in_=F3[:, 832 + 4 * cc:832 + 4 * cc + 3])
                    v.tensor_copy(out=F3[:, 896 + 4 * cc:896 + 4 * cc + 1],
                                  in_=F3[:, 1792 + cc:1792 + cc + 1])
                    v.drain()
                    v.tensor_copy(out=F3[:, 1792 + cc:1792 + cc + 1],
                                  in_=F3[:, 832 + 4 * cc + 3:832 + 4 * cc + 4])
                    v.drain()
                    v.tensor_mul(out=B3[:, 256 * cc:256 * (cc + 1)],
                                 in0=B3[:, 256 * cc:256 * (cc + 1)], in1=r)
                    v.drain()
                    gpb = bass.AP(tensor=F3, offset=896 + 4 * cc,
                                  ap=[[2048, 128], [1, 4], [0, 64]])
                    v.tensor_tensor(out=B3[:, 256 * cc:256 * (cc + 1)],
                                    in0=B3[:, 256 * cc:256 * (cc + 1)], in1=gpb,
                                    op=ALU.mult)
                    v.drain()
                    v.nop().then_inc(s_v.h, 1)
                em("vector", fv2)
                s_v.bump(1)
            vR = s_v.v

            vc0 = s_v.v
            for cc in range(NCC):
                wait_v = vc0 + cc - 1 if cc >= 2 else None

                def ftc(t, cc=cc, vR=vR, wait_v=wait_v):
                    if cc == 0:
                        t.wait_ge(s_v.h, vR)
                    if wait_v is not None:
                        t.wait_ge(s_v.h, wait_v)
                    for hh in range(2):
                        ins = t.matmul(PB[cc % 2][64 * hh:64 * hh + 64, 0:256],
                                       SB["Sinb"][64 * hh:64 * hh + 64, cc, :],
                                       B3[64 * hh:64 * hh + 64, 256 * cc:256 * (cc + 1)],
                                       start=True, stop=True, tile_position=(64 * hh, 64 * hh))
                    ins.then_inc(s_t.h, 1)
                em("tensor", ftc)
                tC = s_t.bump(1)

                def fvc(v, cc=cc, tC=tC):
                    v.wait_ge(s_t.h, tC)
                    v.tensor_add(out=F1[:, 8192 + 256 * cc:8192 + 256 * (cc + 1)],
                                 in0=F1[:, 8192 + 256 * cc:8192 + 256 * (cc + 1)],
                                 in1=PB[cc % 2][:, 0:256])
                    v.drain()
                    v.nop().then_inc(s_v.h, 1)
                em("vector", fvc)
                s_v.bump(1)
            vO = s_v.v

            def fsq(sc, vO=vO):
                sc.wait_ge(s_v.h, vO)
                for cc in range(NCC):
                    sc.activation(out=B2[:, 8192 + 256 * cc:8192 + 256 * (cc + 1)],
                                  in_=F1[:, 8192 + 256 * cc:8192 + 256 * (cc + 1)],
                                  func=AF.Square)
                for cc in range(NCC):
                    ins = sc.activation(out=B2[:, 4096 + 256 * cc:4096 + 256 * (cc + 1)],
                                        in_=F1[:, 8192 + 256 * cc:8192 + 256 * (cc + 1)],
                                        func=AF.Copy)
                ins.then_inc(s_sc.h, 1)
            em("scalar", fsq)
            sq = s_sc.bump(1)

            def fts(t, sq=sq):
                t.wait_ge(s_sc.h, sq)
                for cc in range(NCC):
                    t.matmul(PB[2][0:32, 0:256], SB["gn_ones_bf"][:, cc, :],
                             B2[:, 4096 + 256 * cc:4096 + 256 * (cc + 1)],
                             start=(cc == 0), stop=(cc == 15))
                for cc in range(NCC):
                    ins = t.matmul(PB[3][0:32, 0:256], SB["gn_ones_bf"][:, cc, :],
                                   B2[:, 8192 + 256 * cc:8192 + 256 * (cc + 1)],
                                   start=(cc == 0), stop=(cc == 15))
                ins.then_inc(s_t.h, 1)
            em("tensor", fts)
            tS = s_t.bump(1)

            def fvs(v, tS=tS):
                v.wait_ge(s_t.h, tS)
                v.tensor_scalar_mul(out=F3[0:32, 1024:1280], in0=PB[2][0:32, 0:256],
                                    scalar1=1.0 / HS)
                v.drain()
                v.tensor_scalar_mul(out=F3[0:32, 1280:1536], in0=PB[3][0:32, 0:256],
                                    scalar1=1.0 / HS)
                v.drain()
                v.tensor_mul(out=F3[0:32, 1536:1792], in0=F3[0:32, 1024:1280],
                             in1=F3[0:32, 1024:1280])
                v.drain()
                v.tensor_sub(out=F3[0:32, 1280:1536], in0=F3[0:32, 1280:1536],
                             in1=F3[0:32, 1536:1792])
                v.drain()
                v.tensor_copy(out=SB["Pcl"][0:32, 0:256], in_=F3[0:32, 1024:1280])
                v.drain()
                v.nop().then_inc(s_v.h, 1)
            em("vector", fvs)
            vs = s_v.bump(1)

            def fss(sc, vs=vs):
                sc.wait_ge(s_v.h, vs)
                sc.activation(out=F3[0:32, 1536:1792], in_=F3[0:32, 1280:1536],
                              func=AF.Ln, bias=SB["eps"][0:32, 0:1])
                sc.drain()
                sc.activation(out=SB["Pcl"][0:32, 256:512], in_=F3[0:32, 1536:1792],
                              func=AF.Exp, scale=-0.5).then_inc(s_sc.h, 1)
            em("scalar", fss)
            ss = s_sc.bump(1)

            vg0 = s_v.v
            for cc in range(NCC):
                wait_v = vg0 + cc if cc >= 1 else None

                def ftb(t, cc=cc, ss=ss, wait_v=wait_v):
                    if cc == 0:
                        t.wait_ge(s_sc.h, ss)
                    if wait_v is not None:
                        t.wait_ge(s_v.h, wait_v)
                    gsl = SB["gn_sel"][:, cc, :]
                    t.matmul(PB[4][:, 0:256], gsl, SB["Pcl"][0:32, 0:256],
                             start=True, stop=True)
                    t.matmul(PB[5][:, 0:256], gsl, SB["Pcl"][0:32, 256:512],
                             start=True, stop=True).then_inc(s_t.h, 1)
                em("tensor", ftb)
                tb = s_t.bump(1)

                def fva(v, cc=cc, tb=tb):
                    v.wait_ge(s_t.h, tb)
                    v.tensor_sub(out=F3[:, 1536:1792],
                                 in0=F1[:, 8192 + 256 * cc:8192 + 256 * (cc + 1)],
                                 in1=PB[4][:, 0:256])
                    v.drain()
                    v.scalar_tensor_tensor(out=F3[:, 1536:1792], in0=F3[:, 1536:1792],
                                           scalar=vcol("ln_g", cc), in1=PB[5][:, 0:256],
                                           op0=ALU.mult, op1=ALU.mult)
                    v.drain()
                    v.scalar_tensor_tensor(out=B3[:, 4096 + 256 * cc:4096 + 256 * (cc + 1)],
                                           in0=F3[:, 1536:1792], scalar=vcol("ln_b", cc),
                                           in1=B2[:, 256 * cc:256 * (cc + 1)],
                                           op0=ALU.add, op1=ALU.mult)
                    v.drain()
                    v.nop().then_inc(s_v.h, 1)
                em("vector", fva)
                s_v.bump(1)
            vgn = s_v.v

            def fsg(s, tc=tc, vgn=vgn):
                s.wait_ge(s_v.h, vgn)
                s.dma_start(out=spl("ogn_sp", TAUC * tc, TAUC * (tc + 1)),
                            in_=s3(B3[:, 4096:8192])).then_inc(s_ds.h, 16)
            em("sync", fsg)
            prev_fsg = s_ds.bump(16)

    def ph_D():
        def flw(g):
            for cc in range(NCC):
                g.dma_start(out=BW[:, 2048 * cc:2048 * (cc + 1)],
                            in_=D["wo_t"][128 * cc:128 * (cc + 1), :]).then_inc(s_dg.h, 16)
        em("gpsimd", flw)
        w_t = s_dg.bump(16 * NCC)

        for tau in range(ntau):
            prev_v = s_v.v
            prev_st = s_ds.v

            def flx(s, tau=tau, prev_v=prev_v, prev_st=prev_st):
                if tau > 0:
                    s.wait_ge(s_v.h, prev_v)
                    s.wait_ge(s_ds.h, prev_st)
                s.dma_start(out=s3(B2[:, 0:8192]),
                            in_=spl("ogn_sp", TAU * tau, TAU * (tau + 1))).then_inc(s_ds.h, 16)
                for cc in range(NCC):
                    s.dma_start(out=F1[:, 512 * cc:512 * (cc + 1)],
                                in_=D["hid_fm"][cc, :, 1 + TAU * tau:1 + TAU * (tau + 1)]).then_inc(s_ds.h, 16)
            em("sync", flx)
            x_t = s_ds.bump(16 * 17)

            v0 = s_v.v
            for mt in range(NCC):
                bank = PB[mt % 4]
                wait_v = v0 + mt - 3 if mt >= 4 else None

                def ftp(t, mt=mt, bank=bank, w_t=w_t, x_t=x_t, wait_v=wait_v):
                    if mt == 0:
                        t.wait_ge(s_dg.h, w_t)
                        t.wait_ge(s_ds.h, x_t)
                    if wait_v is not None:
                        t.wait_ge(s_v.h, wait_v)
                    for cc in range(NCC):
                        ins = t.matmul(bank[:, :],
                                       BW[:, 2048 * cc + 128 * mt:2048 * cc + 128 * (mt + 1)],
                                       B2[:, 512 * cc:512 * (cc + 1)],
                                       start=(cc == 0), stop=(cc == 15))
                    ins.then_inc(s_t.h, 1)
                em("tensor", ftp)
                mm_t = s_t.bump(1)

                def fvr(v, mt=mt, bank=bank, mm_t=mm_t):
                    v.wait_ge(s_t.h, mm_t)
                    v.tensor_add(out=F1[:, 512 * mt:512 * (mt + 1)],
                                 in0=F1[:, 512 * mt:512 * (mt + 1)], in1=bank[:, :])
                    v.drain()
                    v.nop().then_inc(s_v.h, 1)
                em("vector", fvr)
                s_v.bump(1)
            vr = s_v.v

            def fsq(sc, vr=vr):
                sc.wait_ge(s_v.h, vr)
                for cc in range(NCC):
                    ins = sc.activation(out=B2[:, 8192 + 512 * cc:8192 + 512 * (cc + 1)],
                                        in_=F1[:, 512 * cc:512 * (cc + 1)], func=AF.Square)
                ins.then_inc(s_sc.h, 1)
            em("scalar", fsq)
            sq = s_sc.bump(1)

            def fts(t, sq=sq):
                t.wait_ge(s_sc.h, sq)
                for cc in range(NCC):
                    t.matmul(PB[4][0:1, :], SB["ones_col"][:], F1[:, 512 * cc:512 * (cc + 1)],
                             start=(cc == 0), stop=(cc == 15))
                for cc in range(NCC):
                    ins = t.matmul(PB[5][0:1, :], SB["ones_col_bf"][:],
                                   B2[:, 8192 + 512 * cc:8192 + 512 * (cc + 1)],
                                   start=(cc == 0), stop=(cc == 15))
                ins.then_inc(s_t.h, 1)
            em("tensor", fts)
            tS = s_t.bump(1)

            def fvs(v, tS=tS):
                v.wait_ge(s_t.h, tS)
                v.tensor_scalar_mul(out=F3[0:1, 0:512], in0=PB[4][0:1, :], scalar1=1.0 / C)
                v.drain()
                v.tensor_scalar_mul(out=F3[0:1, 512:1024], in0=PB[5][0:1, :], scalar1=1.0 / C)
                v.drain()
                v.tensor_mul(out=F3[0:1, 1024:1536], in0=F3[0:1, 0:512], in1=F3[0:1, 0:512])
                v.drain()
                v.tensor_sub(out=F3[0:1, 512:1024], in0=F3[0:1, 512:1024],
                             in1=F3[0:1, 1024:1536])
                v.drain()
                v.nop().then_inc(s_v.h, 1)
            em("vector", fvs)
            vs = s_v.bump(1)

            def fss(sc, vs=vs):
                sc.wait_ge(s_v.h, vs)
                sc.activation(out=F3[0:1, 1024:1536], in_=F3[0:1, 512:1024],
                              func=AF.Ln, bias=SB["eps"][0:1, 1:2])
                sc.drain()
                sc.activation(out=F3[0:1, 512:1024], in_=F3[0:1, 1024:1536],
                              func=AF.Exp, scale=-0.5).then_inc(s_sc.h, 1)
            em("scalar", fss)
            ss = s_sc.bump(1)

            def ftb(t, ss=ss):
                t.wait_ge(s_sc.h, ss)
                t.matmul(PB[6][:, :], SB["ones_row"][:], F3[0:1, 0:512],
                         start=True, stop=True)
                t.matmul(PB[7][:, :], SB["ones_row"][:], F3[0:1, 512:1024],
                         start=True, stop=True).then_inc(s_t.h, 1)
            em("tensor", ftb)
            tb = s_t.bump(1)

            for cc in range(NCC):
                def fva(v, cc=cc, tb=tb):
                    if cc == 0:
                        v.wait_ge(s_t.h, tb)
                    v.tensor_sub(out=F1[:, 512 * cc:512 * (cc + 1)],
                                 in0=F1[:, 512 * cc:512 * (cc + 1)], in1=PB[6][:, :])
                    v.drain()
                    v.tensor_mul(out=F1[:, 512 * cc:512 * (cc + 1)],
                                 in0=F1[:, 512 * cc:512 * (cc + 1)], in1=PB[7][:, :])
                    v.drain()
                    v.tensor_copy(out=B2[:, 512 * cc:512 * (cc + 1)],
                                  in_=F1[:, 512 * cc:512 * (cc + 1)])
                    v.drain()
                    v.nop().then_inc(s_v.h, 1)
                em("vector", fva)
                s_v.bump(1)
            va = s_v.v

            def fso(s, tau=tau, va=va):
                s.wait_ge(s_v.h, va)
                s.dma_start(out=spl("x1f_sp", TAU * tau, TAU * (tau + 1)),
                            in_=s3(F1[:, 0:8192])).then_inc(s_ds.h, 16)
                s.dma_start(out=spl("x1b_sp", TAU * tau, TAU * (tau + 1)),
                            in_=s3(B2[:, 0:8192])).then_inc(s_ds.h, 16)
            em("sync", fso)
            s_ds.bump(32)

    def ph_E():
        def flx(s):
            ins = s.dma_start(
                out=BW[:, 0:NCC * t_core].rearrange("p (c t) -> p c t", c=NCC),
                in_=D["x1b_sp"].transpose([1, 0, 2]))
            ins.then_inc(s_dx.h, 16)
        em("sync", flx)
        x_t = s_dx.bump(16)

        hs_ready = s_g.v

        prev_eg_mm = 0
        for eg in range(16):
            e0 = 512 * eg
            for wi, wname in [(0, "wu_t"), (1, "wgate_t")]:
                prev_sc = s_sc.v
                prev_fqv = s_v.v

                def flw(s, wname=wname, e0=e0, prev_sc=prev_sc, first=(eg == 0 and wi == 0)):
                    if not first:
                        s.wait_ge(s_sc.h, prev_sc)
                    for cc in range(NCC):
                        s.dma_start(out=F1[:, 512 * cc:512 * (cc + 1)],
                                    in_=D[wname][128 * cc:128 * (cc + 1), e0:e0 + 512]).then_inc(s_ds.h, 16)
                em("sync", flw)
                lw = s_ds.bump(16 * NCC)

                scr = F1[:, 8192:12288].bitcast(BF16)

                def fq(sc, wi=wi, lw=lw, scr=scr, hs_ready=hs_ready,
                       prev_fqv=prev_fqv, prev_eg_mm=prev_eg_mm,
                       first=(eg == 0 and wi == 0)):
                    sc.wait_ge(s_ds.h, lw)
                    if first:
                        sc.wait_ge(s_g.h, hs_ready)
                    if prev_fqv > 0:
                        sc.wait_ge(s_v.h, prev_fqv)
                    if prev_eg_mm > 0:
                        sc.wait_ge(s_t.h, prev_eg_mm)
                    sc.sign(out=scr, in_=F1[:, 0:8192],
                            bias=SB["hs"][:, 4 + wi:5 + wi])
                    sc.drain()
                    sc.sign(out=B2[:, wi * 8192:wi * 8192 + 8192], in_=F1[:, 0:8192],
                            bias=SB["hs"][:, wi:wi + 1]).then_inc(s_sc.h, 1)
                em("scalar", fq)
                qs = s_sc.bump(1)

                def fqv(v, wi=wi, qs=qs, scr=scr):
                    v.wait_ge(s_sc.h, qs)
                    v.tensor_add(out=B2[:, wi * 8192:wi * 8192 + 8192],
                                 in0=B2[:, wi * 8192:wi * 8192 + 8192], in1=scr)
                    v.drain()
                    v.nop().then_inc(s_v.h, 1)
                em("vector", fqv)
                s_v.bump(1)
            q_done = s_v.v

            sc0 = s_sc.v
            vE0 = s_v.v
            prev_fsh = getattr(ph_E, "_pf", 0)
            nslot = 0
            for et in range(4):
                for tau in range(ntau):
                    upb = PB[(nslot % 2) * 2]
                    gtb = PB[(nslot % 2) * 2 + 1]
                    wait_sc = sc0 + (nslot - 1) if nslot >= 2 else None
                    wait_ve = vE0 + (nslot - 1) if nslot >= 2 else None

                    def ftm(t, et=et, tau=tau, upb=upb, gtb=gtb, q_done=q_done,
                            x_t=x_t, wait_sc=wait_sc, wait_ve=wait_ve,
                            first=(nslot == 0)):
                        if first:
                            t.wait_ge(s_v.h, q_done)
                            t.wait_ge(s_dx.h, x_t)
                        if wait_sc is not None:
                            t.wait_ge(s_sc.h, wait_sc)
                        if wait_ve is not None:
                            t.wait_ge(s_v.h, wait_ve)
                        for cc in range(NCC):
                            t.matmul(upb[:, :],
                                     B2[:, 512 * cc + 128 * et:512 * cc + 128 * (et + 1)],
                                     BW[:, t_core * cc + 512 * tau:t_core * cc + 512 * (tau + 1)],
                                     start=(cc == 0), stop=(cc == 15))
                        for cc in range(NCC):
                            ins = t.matmul(gtb[:, :],
                                           B2[:, 8192 + 512 * cc + 128 * et:8192 + 512 * cc + 128 * (et + 1)],
                                           BW[:, t_core * cc + 512 * tau:t_core * cc + 512 * (tau + 1)],
                                           start=(cc == 0), stop=(cc == 15))
                        ins.then_inc(s_t.h, 1)
                    em("tensor", ftm)
                    mm_t = s_t.bump(1)

                    def fup(sc, upb=upb, mm_t=mm_t,
                            wv=(vE0 + nslot if nslot >= 1 else None)):
                        sc.wait_ge(s_t.h, mm_t)
                        if wv is not None:
                            sc.wait_ge(s_v.h, wv)
                        sc.activation(out=SB["Pcl"][:], in_=upb[:, :], func=AF.Silu,
                                      scale=SB["hs"][:, 0:1]).then_inc(s_sc.h, 1)
                    em("scalar", fup)
                    up_t = s_sc.bump(1)

                    def fh(v, et=et, tau=tau, gtb=gtb, up_t=up_t):
                        v.wait_ge(s_sc.h, up_t)
                        v.scalar_tensor_tensor(
                            out=B3[:, 512 * ntau * et + 512 * tau:512 * ntau * et + 512 * (tau + 1)],
                            in0=gtb[:, :], scalar=SB["hs"][:, 1:2], in1=SB["Pcl"][:],
                            op0=ALU.mult, op1=ALU.mult)
                        v.drain()
                        v.nop().then_inc(s_v.h, 1)
                    em("vector", fh)
                    s_v.bump(1)
                    nslot += 1
            h_done = s_v.v
            prev_eg_mm = s_t.v

            def fsh(s, eg=eg, h_done=h_done):
                s.wait_ge(s_v.h, h_done)
                for et in range(4):
                    s.dma_start(out=D["h_sp"][4 * eg + et],
                                in_=B3[:, 512 * ntau * et:512 * ntau * (et + 1)]).then_inc(s_ds.h, 16)
            em("sync", fsh)
            ph_E._pf = s_ds.bump(16 * 4)

        h_stores = s_ds.v
        for sl in range(EXP // 128):
            prev_sc = s_sc.v

            def flw(g, sl=sl, prev_sc=prev_sc):
                if sl >= 1:
                    g.wait_ge(s_sc.h, prev_sc)
                g.dma_start(out=F3[:, 0:2048],
                            in_=D["wd_t"][128 * sl:128 * (sl + 1), :]).then_inc(s_dg.h, 16)
            em("gpsimd", flw)
            lw = s_dg.bump(16)

            def fq(sc, sl=sl, lw=lw, prev_store=s_ds.v):
                sc.wait_ge(s_dg.h, lw)
                sc.wait_ge(s_ds.h, prev_store)
                sc.sign(out=B3[:, 0:2048], in_=F3[:, 0:2048], bias=SB["hs"][:, 6:7])
                sc.sign(out=B3[:, 2048:4096], in_=F3[:, 0:2048],
                        bias=SB["hs"][:, 2:3]).then_inc(s_sc.h, 1)
            em("scalar", fq)
            qs = s_sc.bump(1)

            def fqv(v, sl=sl, qs=qs):
                v.wait_ge(s_sc.h, qs)
                v.tensor_add(out=B3[:, 0:2048], in0=B3[:, 0:2048], in1=B3[:, 2048:4096])
                v.drain()
                v.nop().then_inc(s_v.h, 1)
            em("vector", fqv)
            qv = s_v.bump(1)

            def fso(s, sl=sl, qv=qv):
                s.wait_ge(s_v.h, qv)
                for dt in range(NCC):
                    s.dma_start(out=D["wdq_sp"][dt, :, sl, :],
                                in_=B3[:, 128 * dt:128 * (dt + 1)]).then_inc(s_ds.h, 16)
            em("sync", fso)
            s_ds.bump(16 * NCC)

    def ph_F():
        prev_out = 0
        for tau in range(ntau):
            prev_t = s_t.v

            def flh(s, tau=tau, prev_t=prev_t):
                if tau > 0:
                    s.wait_ge(s_t.h, prev_t)
                s.dma_start(out=BW[:, 0:32768].rearrange("p (e t) -> p e t", e=NET),
                            in_=D["h_sp"][:, :, TAU * tau:TAU * (tau + 1)].transpose([1, 0, 2])).then_inc(s_dx.h, 16)
            em("sync", flh)
            h_t = s_dx.bump(16)

            v0 = s_v.v
            for pair in range(NCC // 2):
                wait_v = v0 + 2 * pair if pair >= 1 else None

                def flw(s, pair=pair, tau=tau, wait_v=wait_v):
                    if wait_v is not None:
                        s.wait_ge(s_v.h, wait_v)
                    for dt in (2 * pair, 2 * pair + 1):
                        s.dma_start(out=B2[:, 8192 * (dt % 2):8192 * (dt % 2) + 8192],
                                    in_=D["wdq_sp"][dt].rearrange("p ec d -> p (ec d)")).then_inc(s_ds.h, 16)
                        s.dma_start(out=F3[:, 512 * (dt % 2):512 * (dt % 2) + 512],
                                    in_=D["x1f_sp"][dt, :, TAU * tau:TAU * (tau + 1)]).then_inc(s_ds.h, 16)
                em("sync", flw)
                lw = s_ds.bump(64)

                for dt in (2 * pair, 2 * pair + 1):
                    bank = PB[dt % 2]

                    def ftm(t, dt=dt, bank=bank, lw=lw, h_t=h_t):
                        t.wait_ge(s_ds.h, lw)
                        if dt == 0:
                            t.wait_ge(s_dx.h, h_t)
                        for ec in range(NET):
                            ins = t.matmul(bank[:, :],
                                           B2[:, 8192 * (dt % 2) + 128 * ec:8192 * (dt % 2) + 128 * (ec + 1)],
                                           BW[:, 512 * ec:512 * (ec + 1)],
                                           start=(ec == 0), stop=(ec == NET - 1))
                        ins.then_inc(s_t.h, 1)
                    em("tensor", ftm)
                    mm_t = s_t.bump(1)

                    def fvr(v, dt=dt, bank=bank, mm_t=mm_t):
                        v.wait_ge(s_t.h, mm_t)
                        v.scalar_tensor_tensor(out=F1[:, 512 * dt:512 * (dt + 1)],
                                               in0=bank[:, :], scalar=SB["hs"][:, 2:3],
                                               in1=F3[:, 512 * (dt % 2):512 * (dt % 2) + 512],
                                               op0=ALU.mult, op1=ALU.add)
                        v.drain()
                        v.nop().then_inc(s_v.h, 1)
                    em("vector", fvr)
                    s_v.bump(1)
            vr = s_v.v

            def fsq(sc, vr=vr):
                sc.wait_ge(s_v.h, vr)
                for cc in range(NCC):
                    ins = sc.activation(out=B3[:, 512 * cc:512 * (cc + 1)],
                                        in_=F1[:, 512 * cc:512 * (cc + 1)], func=AF.Square)
                ins.then_inc(s_sc.h, 1)
            em("scalar", fsq)
            sq = s_sc.bump(1)

            def fts(t, sq=sq):
                t.wait_ge(s_sc.h, sq)
                for cc in range(NCC):
                    t.matmul(PB[4][0:1, :], SB["ones_col"][:], F1[:, 512 * cc:512 * (cc + 1)],
                             start=(cc == 0), stop=(cc == 15))
                for cc in range(NCC):
                    ins = t.matmul(PB[5][0:1, :], SB["ones_col_bf"][:],
                                   B3[:, 512 * cc:512 * (cc + 1)],
                                   start=(cc == 0), stop=(cc == 15))
                ins.then_inc(s_t.h, 1)
            em("tensor", fts)
            tS = s_t.bump(1)

            def fvs(v, tS=tS):
                v.wait_ge(s_t.h, tS)
                v.tensor_scalar_mul(out=F3[0:1, 1024:1536], in0=PB[4][0:1, :], scalar1=1.0 / C)
                v.drain()
                v.tensor_scalar_mul(out=F3[0:1, 1536:2048], in0=PB[5][0:1, :], scalar1=1.0 / C)
                v.drain()
                v.tensor_mul(out=F3[0:32, 1024:1280] if False else F3[0:1, 0:512],
                             in0=F3[0:1, 1024:1536], in1=F3[0:1, 1024:1536])
                v.drain()
                v.tensor_sub(out=F3[0:1, 1536:2048], in0=F3[0:1, 1536:2048],
                             in1=F3[0:1, 0:512])
                v.drain()
                v.nop().then_inc(s_v.h, 1)
            em("vector", fvs)
            vs = s_v.bump(1)

            def fss(sc, vs=vs):
                sc.wait_ge(s_v.h, vs)
                sc.activation(out=F3[0:1, 0:512], in_=F3[0:1, 1536:2048],
                              func=AF.Ln, bias=SB["eps"][0:1, 1:2])
                sc.drain()
                sc.activation(out=F3[0:1, 1536:2048], in_=F3[0:1, 0:512],
                              func=AF.Exp, scale=-0.5).then_inc(s_sc.h, 1)
            em("scalar", fss)
            ss = s_sc.bump(1)

            def ftb(t, ss=ss):
                t.wait_ge(s_sc.h, ss)
                t.matmul(PB[6][:, :], SB["ones_row"][:], F3[0:1, 1024:1536],
                         start=True, stop=True)
                t.matmul(PB[7][:, :], SB["ones_row"][:], F3[0:1, 1536:2048],
                         start=True, stop=True).then_inc(s_t.h, 1)
            em("tensor", ftb)
            tb = s_t.bump(1)

            for cc in range(NCC):
                def fva(v, cc=cc, tb=tb):
                    if cc == 0:
                        v.wait_ge(s_t.h, tb)
                    v.tensor_sub(out=F1[:, 512 * cc:512 * (cc + 1)],
                                 in0=F1[:, 512 * cc:512 * (cc + 1)], in1=PB[6][:, :])
                    v.drain()
                    v.tensor_mul(out=F1[:, 512 * cc:512 * (cc + 1)],
                                 in0=F1[:, 512 * cc:512 * (cc + 1)], in1=PB[7][:, :])
                    v.drain()
                    v.nop().then_inc(s_v.h, 1)
                em("vector", fva)
                s_v.bump(1)
            va = s_v.v

            def fso(s, tau=tau, va=va):
                s.wait_ge(s_v.h, va)
                s.dma_start(out=spl("out_fm", TAU * tau, TAU * (tau + 1)),
                            in_=s3(F1[:, 0:8192])).then_inc(s_ds.h, 16)
            em("sync", fso)
            prev_out = s_ds.bump(16)

    allph = [("setup", ph_setup), ("scale", ph_scale), ("A", ph_A), ("B", ph_B),
             ("C", ph_C), ("AG", ph_AG), ("C2", ph_C2), ("D", ph_D), ("E", ph_E),
             ("F", ph_F)]
    for name, fn in allph:
        fn()
        fence()
        if upto is not None and name == upto:
            break

    blk = ctx.enter_context(nc.Block())

    def runner(engine_name):
        def run(eng):
            for fn in steps[engine_name]:
                fn(eng)
        return run

    blk.sync(runner("sync"))
    blk.gpsimd(runner("gpsimd"))
    blk.vector(runner("vector"))
    blk.scalar(runner("scalar"))
    blk.tensor(runner("tensor"))
    ctx.close()
    return nc, D


def kernel(**inputs):
    in_maps, t_core = prep_inputs(inputs)
    nc, D = build_program(t_core)
    res = run_bass_kernel_spmd(nc, in_maps, core_ids=list(range(NCORES)))
    outs = []
    for r in range(NCORES):
        o = np.asarray(res.results[r]["out_fm"])
        outs.append(o.reshape(C, t_core).T)
    full = np.concatenate(outs, axis=0)
    return np.ascontiguousarray(full.reshape(1, NCORES * t_core, C)).astype(np.float32)


# version 09
# speedup vs baseline: 1.1707x; 1.0960x over previous
import numpy as np
import ml_dtypes
from contextlib import ExitStack

import concourse.bass as bass
import concourse.mybir as mybir
from concourse.bass_utils import run_bass_kernel_spmd

F32 = mybir.dt.float32
BF16 = mybir.dt.bfloat16
ALU = mybir.AluOpType
AF = mybir.ActivationFunctionType

NCORES = 8
C = 2048
NCC = 16
H = 32
HS = 64
EXP = 8192
NET = EXP // 128
LC = 64
TAU = 512
TAUC = 256
NU_CLAMP = 11.0
GN_EPS = 1e-5 * 64.0
LN_EPS = 1e-5
DMIX = 32
DDEC = 64

ENGINES = ("sync", "gpsimd", "vector", "scalar", "tensor")



def _vec_fm(v):
    return np.ascontiguousarray(np.asarray(v, np.float32).reshape(NCC, 128).T)


def _host_constants():
    t = np.arange(TAU)
    consts = {
        "m8": np.broadcast_to((t[:TAUC] % 8 != 0).astype(np.float32), (128, TAUC)).copy(),
        "m32": np.broadcast_to((t[:TAUC] % 32 != 0).astype(np.float32), (128, TAUC)).copy(),
        "m64": np.broadcast_to((t[:TAUC] % LC != 0).astype(np.float32), (128, TAUC)).copy(),
    }
    s = np.arange(LC)[:, None]
    tt = np.arange(LC)[None, :]
    bs, bt = s // 8, tt // 8
    pats = {
        "MA": (bs == bt) & (s < tt),
        "MB": (bt - bs) == 1,
        "MC": (bt - bs) == 2,
        "MD": (bt - bs) == 3,
        "ME": ((bt - bs) >= 4) & (s < 32) & (tt >= 32),
        "MF": (s == tt),
    }
    valid = np.zeros((LC, LC), bool)
    for name, m in pats.items():
        if name == "MA":
            consts[name] = np.tile(m.astype(np.float32), (2, 8)).astype(ml_dtypes.bfloat16)
        else:
            consts[name] = np.tile(m.astype(np.uint8), (2, 8))
        valid |= m
    consts["ident"] = np.eye(128).astype(ml_dtypes.bfloat16)
    go = np.zeros((128, NCC, 32), np.float32)
    for cc in range(NCC):
        go[0:64, cc, 2 * cc] = 1.0
        go[64:128, cc, 2 * cc + 1] = 1.0
    consts["gn_ones"] = go
    consts["gn_ones_bf"] = go.astype(ml_dtypes.bfloat16)
    gs = np.zeros((32, NCC, 128), np.float32)
    for cc in range(NCC):
        gs[2 * cc, cc, 0:64] = 1.0
        gs[2 * cc + 1, cc, 64:128] = 1.0
    consts["gn_sel"] = gs.astype(ml_dtypes.bfloat16)
    consts["ones_col"] = np.ones((128, 1), np.float32)
    consts["ones_col_bf"] = np.ones((128, 1), np.float32).astype(ml_dtypes.bfloat16)
    consts["ones_row"] = np.ones((1, 128), np.float32)
    return consts


def prep_inputs(inp):
    x = np.asarray(inp["L0_attn_hidden"], np.float32).reshape(-1, C)
    t_total = x.shape[0]
    t_core = t_total // NCORES
    consts = _host_constants()

    def T(a):
        return np.ascontiguousarray(np.asarray(a, np.float32).T)

    wu_t = T(inp["L0_ffn_W_up"])
    wgate_t = T(inp["L0_ffn_W_gate"])
    wd_t = T(inp["L0_ffn_W_down"])

    shared = {
        "wr_t": T(inp["L0_attn_W_r"]), "wk_t": T(inp["L0_attn_W_k"]),
        "wv_t": T(inp["L0_attn_W_v"]), "wg_t": T(inp["L0_attn_W_g"]),
        "wo_t": T(inp["L0_attn_W_o"]),
        "wu_t": wu_t, "wgate_t": wgate_t, "wd_t": wd_t,
        "w1": np.ascontiguousarray(np.asarray(inp["L0_attn_time_maa_w1"], np.float32)),
        "w2": np.ascontiguousarray(np.asarray(inp["L0_attn_time_maa_w2"], np.float32)),
        "w2p": np.ascontiguousarray(
            np.asarray(inp["L0_attn_time_maa_w2"], np.float32)[0:4]
            .reshape(4, DMIX, NCC, 128).transpose(0, 1, 2, 3).reshape(128, NCC, 128)),
        "dw1": np.ascontiguousarray(np.asarray(inp["L0_attn_time_decay_w1"], np.float32)),
        "dw2": np.ascontiguousarray(
            np.asarray(inp["L0_attn_time_decay_w2"], np.float32)
            .reshape(DDEC, 8, 2, 128).transpose(2, 0, 1, 3).reshape(128, 8, 128)),
        "vecs": np.stack([
            _vec_fm(np.reshape(np.asarray(inp["L0_attn_time_maa_w"], np.float32), -1)),
            _vec_fm(np.reshape(np.asarray(inp["L0_attn_time_maa_k"], np.float32), -1)),
            _vec_fm(np.reshape(np.asarray(inp["L0_attn_time_maa_v"], np.float32), -1)),
            _vec_fm(np.reshape(np.asarray(inp["L0_attn_time_maa_r"], np.float32), -1)),
            _vec_fm(np.reshape(np.asarray(inp["L0_attn_time_maa_g"], np.float32), -1)),
            _vec_fm(np.reshape(np.asarray(inp["L0_attn_time_maa_x"], np.float32), -1)),
            _vec_fm(np.reshape(np.asarray(inp["L0_attn_time_decay"], np.float32), -1)),
            _vec_fm(np.reshape(np.asarray(inp["L0_attn_time_faaaa"], np.float32), -1)),
            _vec_fm(np.reshape(np.asarray(inp["L0_attn_ln_gamma"], np.float32), -1)),
            _vec_fm(np.reshape(np.asarray(inp["L0_attn_ln_beta"], np.float32), -1)),
        ], axis=0),
    }
    shared.update(consts)

    in_maps = []
    rows_ug = C // NCORES
    rows_d = EXP // NCORES
    for r in range(NCORES):
        d = dict(shared)
        blk = x[r * t_core:(r + 1) * t_core]
        halo = x[r * t_core - 1] if r > 0 else np.zeros(C, np.float32)
        fm = np.concatenate([halo[:, None], blk.T], axis=1)
        d["hid_fm"] = np.ascontiguousarray(fm.reshape(NCC, 128, t_core + 1))
        rm = np.zeros((128, 8), np.float32)
        rm[:, :r] = 1.0
        d["rankmask"] = rm
        d["qs_up"] = np.ascontiguousarray(wu_t[r * rows_ug:(r + 1) * rows_ug])
        d["qs_gate"] = np.ascontiguousarray(wgate_t[r * rows_ug:(r + 1) * rows_ug])
        d["qs_down"] = np.ascontiguousarray(wd_t[r * rows_d:(r + 1) * rows_d])
        in_maps.append(d)
    return in_maps, t_core



class SemCtr:
    def __init__(self, h):
        self.h = h
        self.v = 0

    def bump(self, n=1):
        self.v += n
        return self.v


def build_program(t_core, upto=None, dbg=()):
    nc = bass.Bass("TRN2", target_bir_lowering=False, debug=False)
    ntau = t_core // TAU
    ntc = t_core // TAUC
    nch_tc = TAUC // LC

    D = {}

    def dram(name, shape, dtype, kind="Internal"):
        if kind == "Internal" and name in dbg:
            kind = "ExternalOutput"
        D[name] = nc.dram_tensor(name, shape, dtype, kind=kind).ap()
        return D[name]

    dram("hid_fm", [NCC, 128, t_core + 1], F32, "ExternalInput")
    for n, shp in [("wr_t", [C, C]), ("wk_t", [C, C]), ("wv_t", [C, C]),
                   ("wg_t", [C, C]), ("wo_t", [C, C]),
                   ("wu_t", [C, EXP]), ("wgate_t", [C, EXP]), ("wd_t", [EXP, C]),
                   ("w1", [C, 160]), ("w2", [5, DMIX, C]), ("w2p", [128, NCC, 128]),
                   ("dw1", [C, DDEC]), ("dw2", [128, 8, 128]),
                   ("vecs", [10, 128, NCC]),
                   ("m8", [128, TAUC]), ("m32", [128, TAUC]), ("m64", [128, TAUC]),
                   ("gn_ones", [128, NCC, 32]),
                   ("ones_col", [128, 1]), ("ones_row", [1, 128]),
                   ("rankmask", [128, 8]),
                   ("qs_up", [C // NCORES, EXP]), ("qs_gate", [C // NCORES, EXP]),
                   ("qs_down", [EXP // NCORES, C])]:
        dram(n, shp, F32, "ExternalInput")
    dram("MA", [128, 512], BF16, "ExternalInput")
    for n in ["MB", "MC", "MD", "ME", "MF"]:
        dram(n, [128, 512], mybir.dt.uint8, "ExternalInput")
    dram("ident", [128, 128], BF16, "ExternalInput")
    dram("gn_sel", [32, NCC, 128], BF16, "ExternalInput")
    dram("gn_ones_bf", [128, NCC, 32], BF16, "ExternalInput")
    dram("ones_col_bf", [128, 1], BF16, "ExternalInput")

    dram("out_fm", [NCC, 128, t_core], F32, "ExternalOutput")

    dram("xf_sp", [5, NCC, 128, t_core], BF16)
    dram("r_sp", [NCC, 128, t_core], BF16)
    dram("k_sp", [NCC, 128, t_core], BF16)
    dram("g_sp", [NCC, 128, t_core], BF16)
    dram("vtm_sp", [t_core // 128, 128, C], BF16)
    dram("vtm2_sp", [t_core // 128, 128, C], BF16)
    dram("nu_sp", [NCC, 128, t_core], F32)
    dram("o_sp", [NCC, 128, t_core], F32)
    dram("ogn_sp", [NCC, 128, t_core], BF16)
    dram("x1f_sp", [NCC, 128, t_core], F32)
    dram("x1b_sp", [NCC, 128, t_core], BF16)
    dram("h_sp", [NET, 128, t_core], BF16)
    dram("wdq_sp", [NCC, 128, EXP // 128, 128], BF16)
    dram("sc_in", [1, 4], F32)
    dram("sc_out", [NCORES, 1, 4], F32)
    dram("cc_in", [128, 1040], F32)
    dram("cc_out", [NCORES, 128, 1040], F32)

    ctx = ExitStack()

    def sbuf(name, shape, dtype):
        return ctx.enter_context(nc.sbuf_tensor(name, shape, dtype))

    def psumt(name, shape):
        return ctx.enter_context(nc.psum_tensor(name, shape, F32))

    def mksem(name):
        return SemCtr(ctx.enter_context(nc.semaphore(name)))

    SB = {}
    SB["vecs"] = sbuf("sb_vecs", [128, 10, NCC], F32)
    for n in ["m8", "m32", "m64"]:
        SB[n] = sbuf("sb_" + n, [128, TAUC], F32)
    SB["MA"] = sbuf("sb_MA", [128, 512], BF16)
    for n in ["MB", "MC", "MD", "ME", "MF"]:
        SB[n] = sbuf("sb_" + n, [128, 512], mybir.dt.uint8)
    SB["ident"] = sbuf("sb_ident", [128, 128], BF16)
    SB["gn_ones_bf"] = sbuf("sb_gnonesb", [128, NCC, 32], BF16)
    SB["gn_sel"] = sbuf("sb_gnsel", [DMIX, NCC, 128], BF16)
    SB["ones_col"] = sbuf("sb_onescol", [128, 1], F32)
    SB["ones_col_bf"] = sbuf("sb_onescolb", [128, 1], BF16)
    SB["ones_row"] = sbuf("sb_onesrow", [1, 128], F32)
    SB["rankmask"] = sbuf("sb_rankmask", [128, 8], F32)
    SB["w1"] = sbuf("sb_w1", [128, NCC, 160], BF16)
    SB["w2"] = sbuf("sb_w2", [128, NCC, 128], BF16)
    SB["w2g"] = sbuf("sb_w2g", [DMIX, NCC, 128], BF16)
    SB["dw1"] = sbuf("sb_dw1", [128, NCC, DDEC], BF16)
    SB["dw2"] = sbuf("sb_dw2", [128, NCC // 2, 128], BF16)
    SB["S"] = sbuf("sb_S", [128, NCC, HS], F32)
    SB["Sb"] = sbuf("sb_Sb", [128, NCC, HS], BF16)
    SB["Sinb"] = sbuf("sb_Sinb", [128, NCC, HS], BF16)
    SB["hs"] = sbuf("sb_hs", [128, 8], F32)
    SB["Gfin"] = sbuf("sb_Gfin", [128, NCC], F32)
    SB["Pcl"] = sbuf("sb_Pcl", [128, 512], BF16)
    SB["scl"] = sbuf("sb_scl", [1, 40], F32)
    SB["eps"] = sbuf("sb_eps", [128, 2], F32)

    BW = sbuf("ar_BW", [128, 32768], BF16)
    B2 = sbuf("ar_B2", [128, 16384], BF16)
    B3 = sbuf("ar_B3", [128, 8192], BF16)
    F1 = sbuf("ar_F1", [128, 12288], F32)
    F3 = sbuf("ar_F3", [128, 2048], F32)

    PB = [psumt(f"pb{i}", [128, 512]) for i in range(8)]

    s_ds = mksem("s_ds")
    s_dx = mksem("s_dx")
    s_dg = mksem("s_dg")
    s_v = mksem("s_v")
    s_sc = mksem("s_sc")
    s_t = mksem("s_t")
    s_g = mksem("s_g")
    s_dh = mksem("s_dh")
    ALLSEMS = [s_ds, s_dx, s_dg, s_v, s_sc, s_t, s_g, s_dh]

    steps = {e: [] for e in ENGINES}

    def em(engine, fn):
        steps[engine].append(fn)

    def fence():
        tgt = [(s.h, s.v) for s in ALLSEMS if s.v > 0]
        for e in ENGINES:
            def f(eng, tgt=tuple(tgt)):
                for h, v in tgt:
                    eng.wait_ge(h, v)
            em(e, f)

    VEC = {"maa_w": 0, "maa_k": 1, "maa_v": 2, "maa_r": 3, "maa_g": 4,
           "maa_x": 5, "td": 6, "u": 7, "ln_g": 8, "ln_b": 9}

    def vcol(name, cc):
        return SB["vecs"][:, VEC[name], cc:cc + 1]

    def spl(name, a, b, f=None):
        if f is None:
            return D[name][:, :, a:b].transpose([1, 0, 2])
        return D[name][f, :, :, a:b].transpose([1, 0, 2])

    def s3(ap, t=None):
        n = ap.shape[1]
        t = t if t is not None else n // NCC
        return ap.rearrange("p (c t) -> p c t", t=t)

    def ph_setup():
        def f(s):
            s.dma_start(out=SB["vecs"][:], in_=D["vecs"].rearrange("v p c -> p v c")).then_inc(s_ds.h, 16)
            for n in ["m8", "m32", "m64", "MA", "MB", "MC", "MD", "ME", "MF",
                      "ident", "ones_col", "ones_row", "rankmask", "gn_sel"]:
                s.dma_start(out=SB[n][:], in_=D[n]).then_inc(s_ds.h, 16)
            s.dma_start(out=SB["gn_ones_bf"][:], in_=D["gn_ones_bf"]).then_inc(s_ds.h, 16)
            s.dma_start(out=SB["ones_col_bf"][:], in_=D["ones_col_bf"]).then_inc(s_ds.h, 16)
        em("sync", f)
        s_ds.bump(16 * 17)

        def g(g):
            g.dma_start(out=SB["w1"][:], in_=D["w1"].rearrange("(cc p) f -> p cc f", p=128)).then_inc(s_dg.h, 16)
            g.dma_start(out=SB["w2"][:], in_=D["w2p"]).then_inc(s_dg.h, 16)
            g.dma_start(out=SB["w2g"][:], in_=D["w2"][4].rearrange("d (cc m) -> d cc m", m=128)).then_inc(s_dg.h, 16)
            g.dma_start(out=SB["dw1"][:], in_=D["dw1"].rearrange("(cc p) f -> p cc f", p=128)).then_inc(s_dg.h, 16)
            g.dma_start(out=SB["dw2"][:], in_=D["dw2"]).then_inc(s_dg.h, 16)
        em("gpsimd", g)
        s_dg.bump(16 * 5)

        def v(v):
            v.memset(SB["S"][:], 0.0)
            v.memset(SB["Sb"][:], 0.0)
            v.memset(SB["Sinb"][:], 0.0)
            v.memset(SB["Gfin"][:], 1.0)
            v.memset(SB["scl"][:], 0.0)
            v.memset(F3[:, 2044:2048], 0.0)
            v.memset(SB["eps"][:, 0:1], GN_EPS)
            v.memset(SB["eps"][:, 1:2], LN_EPS)
            v.drain()
            v.nop().then_inc(s_v.h, 1)
        em("vector", v)
        s_v.bump(1)

    def ph_scale():
        slabs = [("qs_up", 0, 2, 8192), ("qs_gate", 1, 2, 8192), ("qs_down", 2, 8, 2048)]
        for wname, wi, nsl, cols in slabs:
            for sl in range(nsl):
                prev_sc = s_sc.v

                def fl(s, wname=wname, sl=sl, cols=cols, prev_sc=prev_sc):
                    if prev_sc > 0:
                        s.wait_ge(s_sc.h, prev_sc)
                    s.dma_start(out=F1[:, 0:cols],
                                in_=D[wname][128 * sl:128 * (sl + 1), :]).then_inc(s_ds.h, 16)
                em("sync", fl)
                ld_t = s_ds.bump(16)

                prev_fv = s_v.v

                def fs(sc, wi=wi, sl=sl, cols=cols, ld_t=ld_t, prev_fv=prev_fv):
                    sc.wait_ge(s_ds.h, ld_t)
                    if prev_fv > 0:
                        sc.wait_ge(s_v.h, prev_fv)
                    sc.activation(out=F1[:, 0:cols], in_=F1[:, 0:cols], func=AF.Abs,
                                  accum_out=F3[:, 4 + wi:5 + wi]).then_inc(s_sc.h, 1)
                em("scalar", fs)
                sc_t = s_sc.bump(1)

                def fv(v, wi=wi, sl=sl, sc_t=sc_t):
                    v.wait_ge(s_sc.h, sc_t)
                    if sl == 0:
                        v.tensor_copy(out=F3[:, wi:wi + 1], in_=F3[:, 4 + wi:5 + wi])
                    else:
                        v.tensor_add(out=F3[:, wi:wi + 1], in0=F3[:, wi:wi + 1],
                                     in1=F3[:, 4 + wi:5 + wi])
                    v.drain()
                    v.nop().then_inc(s_v.h, 1)
                em("vector", fv)
                s_v.bump(1)
        vt = s_v.v

        def ft(t, vt=vt):
            t.wait_ge(s_v.h, vt)
            t.matmul(PB[0][0:1, 0:3], SB["ones_col"][:], F3[:, 0:3],
                     start=True, stop=True).then_inc(s_t.h, 1)
        em("tensor", ft)
        tt = s_t.bump(1)

        def fv2(v, tt=tt):
            v.wait_ge(s_t.h, tt)
            v.tensor_copy(out=SB["scl"][0:1, 0:3], in_=PB[0][0:1, 0:3])
            v.drain()
            v.nop().then_inc(s_v.h, 1)
        em("vector", fv2)
        vt2 = s_v.bump(1)

        dg0 = s_dg.v

        GWAIT1 = s_g.v + 1

        def fg(g, vt2=vt2, dg0=dg0, GWAIT1=GWAIT1):
            g.wait_ge(s_v.h, vt2)
            g.dma_start(out=D["sc_in"], in_=SB["scl"][0:1, 0:4]).then_inc(s_dg.h, 16)
            g.wait_ge(s_dg.h, dg0 + 16)
            g.collective_compute(
                "AllGather", ALU.bypass, replica_groups=[list(range(NCORES))],
                ins=[D["sc_in"]], outs=[D["sc_out"]],
            ).then_inc(s_g.h, 1)
            g.wait_ge(s_g.h, GWAIT1)
            g.dma_start(out=SB["scl"][0:1, 8:40].rearrange("p (r f) -> p r f", r=NCORES),
                        in_=D["sc_out"].transpose([1, 0, 2])).then_inc(s_dg.h, 16)
        em("gpsimd", fg)
        s_dg.bump(32)
        s_g.bump(1)
        dg_t = s_dg.v

        def fv3(v, dg_t=dg_t):
            v.wait_ge(s_dg.h, dg_t)
            for wi, numel in [(0, C * EXP), (1, C * EXP), (2, C * EXP)]:
                src = bass.AP(tensor=SB["scl"], offset=8 + wi, ap=[[40, 1], [4, 8]])
                v.tensor_reduce(out=SB["scl"][0:1, 4 + wi:5 + wi], in_=src,
                                axis=mybir.AxisListType.X, op=ALU.add)
                v.drain()
            v.tensor_scalar(out=SB["scl"][0:1, 0:3], in0=SB["scl"][0:1, 4:7],
                            scalar1=0.5 / float(C * EXP), scalar2=0.5e-8,
                            op0=ALU.mult, op1=ALU.add)
            v.drain()
            v.tensor_scalar_mul(out=SB["scl"][0:1, 4:7], in0=SB["scl"][0:1, 0:3],
                                scalar1=-1.0)
            v.drain()
            v.nop().then_inc(s_v.h, 1)
        em("vector", fv3)
        vt3 = s_v.bump(1)

        def ft2(t, vt3=vt3):
            t.wait_ge(s_v.h, vt3)
            t.matmul(PB[1][:, 0:8], SB["ones_row"][:], SB["scl"][0:1, 0:8],
                     start=True, stop=True).then_inc(s_t.h, 1)
        em("tensor", ft2)
        tt2 = s_t.bump(1)

        def fv4(v, tt2=tt2):
            v.wait_ge(s_t.h, tt2)
            v.tensor_copy(out=SB["hs"][:, 0:8], in_=PB[1][:, 0:8])
            v.drain()
            v.nop().then_inc(s_g.h, 1)
        em("vector", fv4)
        s_g.bump(1)

    def ph_A():
        hid = F1
        prev_st = 0
        for tau in range(ntau):
            prev_v = s_v.v

            def fl(s, tau=tau, prev_v=prev_v):
                if tau > 0:
                    s.wait_ge(s_v.h, prev_v)
                for cc in range(NCC):
                    s.dma_start(out=hid[:, 513 * cc:513 * (cc + 1)],
                                in_=D["hid_fm"][cc, :, TAU * tau:TAU * tau + 513]).then_inc(s_ds.h, 16)
            em("sync", fl)
            ld_t = s_ds.bump(16 * NCC)

            def fv(v, ld_t=ld_t):
                v.wait_ge(s_ds.h, ld_t)
                for cc in range(NCC):
                    v.tensor_sub(out=B2[:, 512 * cc:512 * (cc + 1)],
                                 in0=hid[:, 513 * cc:513 * cc + 512],
                                 in1=hid[:, 513 * cc + 1:513 * cc + 513])
                v.drain()
                for cc in range(NCC):
                    v.scalar_tensor_tensor(
                        out=B2[:, 8192 + 512 * cc:8192 + 512 * (cc + 1)],
                        in0=B2[:, 512 * cc:512 * (cc + 1)], scalar=vcol("maa_x", cc),
                        in1=hid[:, 513 * cc + 1:513 * cc + 513], op0=ALU.mult, op1=ALU.add)
                v.drain()
                v.nop().then_inc(s_v.h, 1)
            em("vector", fv)
            v_t = s_v.bump(1)

            def ft(t, v_t=v_t):
                t.wait_ge(s_v.h, v_t)
                for cc in range(NCC):
                    t.matmul(PB[0][:, :], SB["w1"][:, cc, 0:128],
                             B2[:, 8192 + 512 * cc:8192 + 512 * (cc + 1)],
                             start=(cc == 0), stop=(cc == 15))
                for cc in range(NCC):
                    ins = t.matmul(PB[1][0:32, :], SB["w1"][:, cc, 128:160],
                                   B2[:, 8192 + 512 * cc:8192 + 512 * (cc + 1)],
                                   start=(cc == 0), stop=(cc == 15))
                ins.then_inc(s_t.h, 1)
            em("tensor", ft)
            t_t = s_t.bump(1)

            def fs(sc, t_t=t_t):
                sc.wait_ge(s_t.h, t_t)
                sc.activation(out=BW[:, 0:512], in_=PB[0][:, :], func=AF.Tanh)
                sc.activation(out=BW[0:32, 512:1024], in_=PB[1][0:32, :],
                              func=AF.Tanh).then_inc(s_sc.h, 1)
            em("scalar", fs)
            sc_t = s_sc.bump(1)

            vmix0 = s_v.v
            for f in range(5):
                for cc in range(NCC):
                    idx = f * NCC + cc
                    bank = PB[2 + (idx % 2)]
                    wait_v = vmix0 + idx - 1 if idx >= 2 else None

                    def ftm(t, f=f, cc=cc, bank=bank, sc_t=sc_t, wait_v=wait_v):
                        if f == 0 and cc == 0:
                            t.wait_ge(s_sc.h, sc_t)
                        if wait_v is not None:
                            t.wait_ge(s_v.h, wait_v)
                        zsl = (BW[32 * f:32 * (f + 1), 0:512] if f < 4
                               else BW[0:32, 512:1024])
                        w2sl = (SB["w2"][32 * f:32 * (f + 1), cc, :] if f < 4
                                else SB["w2g"][:, cc, :])
                        tp = (32 * f, 0) if f < 4 else (0, 0)
                        t.matmul(bank[:, :], w2sl, zsl,
                                 start=True, stop=True,
                                 tile_position=tp).then_inc(s_t.h, 1)
                    em("tensor", ftm)
                    mm_t = s_t.bump(1)

                    def fvm(v, f=f, cc=cc, bank=bank, mm_t=mm_t):
                        v.wait_ge(s_t.h, mm_t)
                        v.scalar_tensor_tensor(
                            out=F3[:, 0:512], in0=bank[:, :],
                            scalar=vcol("maa_" + "wkvrg"[f], cc),
                            in1=B2[:, 512 * cc:512 * (cc + 1)],
                            op0=ALU.add, op1=ALU.mult)
                        v.drain()
                        v.tensor_add(out=B3[:, 512 * cc:512 * (cc + 1)],
                                     in0=F3[:, 0:512],
                                     in1=hid[:, 513 * cc + 1:513 * cc + 513])
                        v.drain()
                        v.nop().then_inc(s_v.h, 1)
                    em("vector", fvm)
                    s_v.bump(1)
                v_done = s_v.v

                def fst(s, f=f, tau=tau, v_done=v_done):
                    s.wait_ge(s_v.h, v_done)
                    s.dma_start(out=spl("xf_sp", TAU * tau, TAU * (tau + 1), f),
                                in_=s3(B3[:, 0:8192])).then_inc(s_ds.h, 16)
                em("sync", fst)
                st_done = s_ds.bump(16)

                if f < 4:
                    def fvw(v, st_done=st_done):
                        v.wait_ge(s_ds.h, st_done)
                    em("vector", fvw)
                else:
                    prev_st = st_done

    def ph_B():
        for wname, fidx, spname, act in [
                ("wr_t", 3, "r_sp", None), ("wk_t", 1, "k_sp", None),
                ("wg_t", 4, "g_sp", "silu")]:
            prev_t = s_t.v

            def flw(g, wname=wname, prev_t=prev_t):
                g.wait_ge(s_t.h, prev_t)
                for cc in range(NCC):
                    g.dma_start(out=BW[:, 2048 * cc:2048 * (cc + 1)],
                                in_=D[wname][128 * cc:128 * (cc + 1), :]).then_inc(s_dg.h, 16)
            em("gpsimd", flw)
            w_t = s_dg.bump(16 * NCC)

            for tau in range(ntau):
                prev_mm = s_t.v

                def flx(s, fidx=fidx, tau=tau, prev_mm=prev_mm):
                    if tau > 0:
                        s.wait_ge(s_t.h, prev_mm)
                    s.dma_start(out=s3(B2[:, 0:8192]),
                                in_=spl("xf_sp", TAU * tau, TAU * (tau + 1), fidx)).then_inc(s_ds.h, 16)
                em("sync", flx)
                x_t = s_ds.bump(16)

                sc0 = s_sc.v
                prev_fso = s_ds.v
                for mt in range(NCC):
                    bank = PB[mt % 4]
                    wait_sc = sc0 + mt - 3 if mt >= 4 else None

                    def ftp(t, mt=mt, bank=bank, w_t=w_t, x_t=x_t, wait_sc=wait_sc):
                        if mt == 0:
                            t.wait_ge(s_dg.h, w_t)
                            t.wait_ge(s_ds.h, x_t)
                        if wait_sc is not None:
                            t.wait_ge(s_sc.h, wait_sc)
                        for cc in range(NCC):
                            ins = t.matmul(bank[:, :],
                                           BW[:, 2048 * cc + 128 * mt:2048 * cc + 128 * (mt + 1)],
                                           B2[:, 512 * cc:512 * (cc + 1)],
                                           start=(cc == 0), stop=(cc == 15))
                        ins.then_inc(s_t.h, 1)
                    em("tensor", ftp)
                    mm_t = s_t.bump(1)

                    def fse(sc, mt=mt, bank=bank, mm_t=mm_t, act=act, x_t=x_t):
                        sc.wait_ge(s_t.h, mm_t)
                        if mt == 0:
                            sc.wait_ge(s_ds.h, x_t)
                        fn = AF.Silu if act == "silu" else AF.Copy
                        sc.activation(out=B2[:, 8192 + 512 * mt:8192 + 512 * (mt + 1)],
                                      in_=bank[:, :], func=fn).then_inc(s_sc.h, 1)
                    em("scalar", fse)
                    s_sc.bump(1)
                e_t = s_sc.v

                def fso(s, spname=spname, tau=tau, e_t=e_t):
                    s.wait_ge(s_sc.h, e_t)
                    s.dma_start(out=spl(spname, TAU * tau, TAU * (tau + 1)),
                                in_=s3(B2[:, 8192:16384])).then_inc(s_ds.h, 16)
                em("sync", fso)
                s_ds.bump(16)

        prev_t = s_t.v

        def flw2(g, prev_t=prev_t):
            g.wait_ge(s_t.h, prev_t)
            for cc in range(NCC):
                g.dma_start(out=BW[:, 2048 * cc:2048 * (cc + 1)],
                            in_=D["wv_t"][128 * cc:128 * (cc + 1), :]).then_inc(s_dg.h, 16)
        em("gpsimd", flw2)
        w_t = s_dg.bump(16 * NCC)

        for tau in range(ntau):
            prev_mm = s_t.v

            def flx(s, tau=tau, prev_mm=prev_mm):
                if tau > 0:
                    s.wait_ge(s_t.h, prev_mm)
                s.dma_start(out=s3(B2[:, 0:8192]),
                            in_=spl("xf_sp", TAU * tau, TAU * (tau + 1), 2)).then_inc(s_ds.h, 16)
            em("sync", flx)
            x_t = s_ds.bump(16)

            v0 = s_v.v
            prev_fso2 = s_ds.v
            for tt in range(4):
                for ac in range(4):
                    idx = tt * 4 + ac
                    bank = PB[idx % 4]
                    wait_v = v0 + idx - 3 if idx >= 4 else None

                    def ftv(t, tt=tt, ac=ac, bank=bank, w_t=w_t, x_t=x_t, wait_v=wait_v):
                        if tt == 0 and ac == 0:
                            t.wait_ge(s_dg.h, w_t)
                            t.wait_ge(s_ds.h, x_t)
                        if wait_v is not None:
                            t.wait_ge(s_v.h, wait_v)
                        for cc in range(NCC):
                            ins = t.matmul(bank[:, :],
                                           B2[:, 512 * cc + 128 * tt:512 * cc + 128 * (tt + 1)],
                                           BW[:, 2048 * cc + 512 * ac:2048 * cc + 512 * (ac + 1)],
                                           start=(cc == 0), stop=(cc == 15))
                        ins.then_inc(s_t.h, 1)
                    em("tensor", ftv)
                    mm_t = s_t.bump(1)

                    def fve(v, tt=tt, ac=ac, bank=bank, mm_t=mm_t, x_t=x_t):
                        v.wait_ge(s_t.h, mm_t)
                        if tt == 0 and ac == 0:
                            v.wait_ge(s_ds.h, x_t)
                        v.tensor_copy(out=B3[:, 2048 * tt + 512 * ac:2048 * tt + 512 * (ac + 1)],
                                      in_=bank[:, :])
                        v.drain()
                        v.nop().then_inc(s_v.h, 1)
                    em("vector", fve)
                    s_v.bump(1)
            v_done = s_v.v

            def fso2(s, tau=tau, v_done=v_done):
                s.wait_ge(s_v.h, v_done)
                for tt in range(4):
                    s.dma_start(out=D["vtm_sp"][4 * tau + tt],
                                in_=B3[:, 2048 * tt:2048 * (tt + 1)]).then_inc(s_ds.h, 16)
                for tt in range(4):
                    s.dma_start(out=D["vtm2_sp"][4 * tau + tt][0:64, :],
                                in_=B3[64:128, 2048 * tt:2048 * (tt + 1)]).then_inc(s_ds.h, 16)
                    s.dma_start(out=D["vtm2_sp"][4 * tau + tt][64:128, :],
                                in_=B3[0:64, 2048 * tt:2048 * (tt + 1)]).then_inc(s_ds.h, 16)
            em("sync", fso2)
            s_ds.bump(16 * 12)

        for tau in range(ntau):
            prev_t = s_t.v
            prev_b3 = s_ds.v
            prev_fsn = s_ds.v

            def flx(s, tau=tau, prev_t=prev_t):
                if tau > 0:
                    s.wait_ge(s_t.h, prev_t)
                s.dma_start(out=s3(B2[:, 0:8192]),
                            in_=spl("xf_sp", TAU * tau, TAU * (tau + 1), 0)).then_inc(s_ds.h, 16)
            em("sync", flx)
            x_t = s_ds.bump(16)

            def ftz(t, x_t=x_t):
                t.wait_ge(s_ds.h, x_t)
                for cc in range(NCC):
                    t.matmul(PB[4][0:64, :], SB["dw1"][:, cc, :],
                             B2[:, 512 * cc:512 * (cc + 1)],
                             start=(cc == 0), stop=(cc == 15))
                for cc in range(NCC):
                    ins = t.matmul(PB[4][64:128, :], SB["dw1"][:, cc, :],
                                   B2[:, 512 * cc:512 * (cc + 1)],
                                   start=(cc == 0), stop=(cc == 15),
                                   tile_position=(0, 64))
                ins.then_inc(s_t.h, 1)
            em("tensor", ftz)
            tz = s_t.bump(1)

            def fsz(sc, tz=tz, x_t=x_t):
                sc.wait_ge(s_t.h, tz)
                sc.wait_ge(s_ds.h, x_t)
                sc.activation(out=B3[:, 0:512], in_=PB[4][:, :],
                              func=AF.Tanh).then_inc(s_sc.h, 1)
            em("scalar", fsz)
            scz = s_sc.bump(1)

            v0 = s_v.v
            for cc in range(NCC):
                bank = PB[5 + (cc % 2)]
                wait_v = v0 + cc - 1 if cc >= 2 else None

                def ftd(t, cc=cc, bank=bank, scz=scz, wait_v=wait_v):
                    if cc == 0:
                        t.wait_ge(s_sc.h, scz)
                    if wait_v is not None:
                        t.wait_ge(s_v.h, wait_v)
                    zrow = 64 * (cc % 2)
                    t.matmul(bank[:, :], SB["dw2"][zrow:zrow + 64, cc // 2, :],
                             B3[zrow:zrow + 64, 0:512],
                             start=True, stop=True, tile_position=(zrow, 0)).then_inc(s_t.h, 1)
                em("tensor", ftd)
                td = s_t.bump(1)

                def fsd(sc, cc=cc, bank=bank, td=td, x_t=x_t):
                    sc.wait_ge(s_t.h, td)
                    if cc == 0:
                        sc.wait_ge(s_ds.h, x_t)
                    sc.activation(out=F1[:, 512 * cc:512 * (cc + 1)], in_=bank[:, :],
                                  func=AF.Exp, bias=vcol("td", cc)).then_inc(s_sc.h, 1)
                em("scalar", fsd)
                sd = s_sc.bump(1)

                def fvd(v, cc=cc, sd=sd):
                    v.wait_ge(s_sc.h, sd)
                    v.tensor_scalar_min(out=F1[:, 512 * cc:512 * (cc + 1)],
                                        in0=F1[:, 512 * cc:512 * (cc + 1)],
                                        scalar1=NU_CLAMP)
                    v.drain()
                    v.nop().then_inc(s_v.h, 1)
                em("vector", fvd)
                s_v.bump(1)
            vnu = s_v.v

            def fsn(s, tau=tau, vnu=vnu):
                s.wait_ge(s_v.h, vnu)
                s.dma_start(out=spl("nu_sp", TAU * tau, TAU * (tau + 1)),
                            in_=s3(F1[:, 0:8192])).then_inc(s_ds.h, 16)
            em("sync", fsn)
            s_ds.bump(16)

    FK8, FKA, FKF2, FKF3, FK32, FR32, FR8, FK0 = range(8)

    def bwsl(slot, cc, a=0, b=TAUC):
        return BW[:, 4096 * slot + 256 * cc + a:4096 * slot + 256 * cc + b]

    def ph_C():
        prev_oso = 0
        prev_sO_last = None
        for tc in range(ntc):
            prev_t = s_t.v
            prev_v = s_v.v

            def fl(s, tc=tc, prev_t=prev_t, prev_v=prev_v):
                if tc > 0:
                    s.wait_ge(s_t.h, prev_t)
                    s.wait_ge(s_v.h, prev_v)
                s.dma_start(out=s3(B2[:, 0:4096]),
                            in_=spl("k_sp", TAUC * tc, TAUC * (tc + 1))).then_inc(s_ds.h, 16)
                s.dma_start(out=s3(B2[:, 4096:8192]),
                            in_=spl("r_sp", TAUC * tc, TAUC * (tc + 1))).then_inc(s_ds.h, 16)
                s.dma_start(out=s3(F1[:, 0:4096]),
                            in_=spl("nu_sp", TAUC * tc, TAUC * (tc + 1))).then_inc(s_ds.h, 16)
                s.dma_start(out=B3[:, 0:2048], in_=D["vtm_sp"][2 * tc]).then_inc(s_ds.h, 16)
                s.dma_start(out=B3[:, 2048:4096], in_=D["vtm_sp"][2 * tc + 1]).then_inc(s_ds.h, 16)
                for b in range(2):
                    s.dma_start(out=B2[:, 8192 + 2048 * b:8192 + 2048 * (b + 1)],
                                in_=D["vtm2_sp"][2 * tc + b]).then_inc(s_ds.h, 16)
            em("sync", fl)
            ld_t = s_ds.bump(16 * 7)

            def scn_(cc):
                return F1[:, 4096 + 256 * cc:4096 + 256 * (cc + 1)]

            def nu_(cc):
                return F1[:, 256 * cc:256 * (cc + 1)]

            def k_(cc):
                return B2[:, 256 * cc:256 * (cc + 1)]

            def r_(cc):
                return B2[:, 4096 + 256 * cc:4096 + 256 * (cc + 1)]

            def scr_(cc):
                return F1[:, 8192 + 256 * cc:8192 + 256 * (cc + 1)]

            def k64_(cc, a=0, b=TAUC):
                return B2[:, 12288 + 256 * cc + a:12288 + 256 * cc + b]

            def sV1(v, ld_t=ld_t):
                v.wait_ge(s_ds.h, ld_t)
                for cc in range(NCC):
                    v.tensor_tensor_scan(out=scn_(cc), data0=SB["m8"][:, 0:TAUC],
                                         data1=nu_(cc), initial=0.0,
                                         op0=ALU.mult, op1=ALU.add)
                v.drain()
                v.nop().then_inc(s_v.h, 1)
            em("vector", sV1)
            s_v.bump(1)

            def sS1(sc, vw=s_v.v):
                sc.wait_ge(s_v.h, vw)
                for cc in range(NCC):
                    sc.activation(out=bwsl(FK8, cc), in_=scn_(cc), func=AF.Exp)
                    sc.activation(out=bwsl(FR8, cc, 1, TAUC), in_=scn_(cc)[:, 0:TAUC - 1],
                                  func=AF.Exp, scale=-1.0)
                    g8 = bass.AP(tensor=F1, offset=4096 + 256 * cc + 7,
                                 ap=[[12288, 128], [8, 32]])
                    ins = sc.activation(out=F3[:, 34 * cc:34 * cc + 32], in_=g8,
                                        func=AF.Exp, scale=-1.0)
                ins.then_inc(s_sc.h, 1)
            em("scalar", sS1)
            s_sc.bump(1)

            def sV23(v, sw=s_sc.v):
                v.wait_ge(s_sc.h, sw)
                for cc in range(NCC):
                    fix = bass.AP(tensor=BW, offset=4096 * FR8 + 256 * cc,
                                  ap=[[32768, 128], [8, 32]])
                    v.memset(fix, 1.0)
                    v.memset(F3[:, 34 * cc + 32:34 * cc + 34], 0.0)
                v.drain()
                for cc in range(NCC):
                    v.tensor_mul(out=bwsl(FK8, cc), in0=bwsl(FK8, cc), in1=k_(cc))
                    v.tensor_mul(out=bwsl(FR8, cc), in0=bwsl(FR8, cc), in1=r_(cc))
                v.drain()
                for cc in range(NCC):
                    e8b = bass.AP(tensor=F3, offset=34 * cc,
                                  ap=[[2048, 128], [1, 32], [0, 8]])
                    v.tensor_tensor(out=bwsl(FKA, cc), in0=bwsl(FK8, cc), in1=e8b,
                                    op=ALU.mult)
                v.drain()
                for cc in range(NCC):
                    e8b1 = bass.AP(tensor=F3, offset=34 * cc + 1,
                                   ap=[[2048, 128], [1, 32], [0, 8]])
                    v.tensor_tensor(out=bwsl(FKF2, cc), in0=bwsl(FKA, cc), in1=e8b1,
                                    op=ALU.mult)
                v.drain()
                for cc in range(NCC):
                    e8b2 = bass.AP(tensor=F3, offset=34 * cc + 2,
                                   ap=[[2048, 128], [1, 32], [0, 8]])
                    v.tensor_tensor(out=bwsl(FKF3, cc), in0=bwsl(FKF2, cc), in1=e8b2,
                                    op=ALU.mult)
                v.drain()
                for cc in range(NCC):
                    v.tensor_tensor_scan(out=scn_(cc), data0=SB["m32"][:, 0:TAUC],
                                         data1=nu_(cc), initial=0.0,
                                         op0=ALU.mult, op1=ALU.add)
                v.drain()
                v.nop().then_inc(s_v.h, 1)
            em("vector", sV23)
            s_v.bump(1)

            def sS2(sc, vw=s_v.v):
                sc.wait_ge(s_v.h, vw)
                for cc in range(NCC):
                    sc.activation(out=bwsl(FR32, cc, 1, TAUC), in_=scn_(cc)[:, 0:TAUC - 1],
                                  func=AF.Exp, scale=-1.0)
                    g32 = bass.AP(tensor=F1, offset=4096 + 256 * cc + 31,
                                  ap=[[12288, 128], [32, 8]])
                    ins = sc.activation(out=F3[:, 544 + 8 * cc:544 + 8 * cc + 8], in_=g32,
                                        func=AF.Copy)
                ins.then_inc(s_sc.h, 1)
            em("scalar", sS2)
            s_sc.bump(1)

            def sV4(v, sw=s_sc.v):
                v.wait_ge(s_sc.h, sw)
                for cc in range(NCC):
                    fix = bass.AP(tensor=BW, offset=4096 * FR32 + 256 * cc,
                                  ap=[[32768, 128], [32, 8]])
                    v.memset(fix, 1.0)
                v.drain()
                for cc in range(NCC):
                    v.tensor_mul(out=bwsl(FR32, cc), in0=bwsl(FR32, cc), in1=r_(cc))
                for cc in range(NCC):
                    b32b = bass.AP(tensor=F3, offset=544 + 8 * cc,
                                   ap=[[2048, 128], [1, 8], [0, 32]])
                    v.tensor_tensor(out=scr_(cc), in0=scn_(cc), in1=b32b,
                                    op=ALU.subtract)
                v.drain()
                v.nop().then_inc(s_v.h, 1)
            em("vector", sV4)
            s_v.bump(1)

            def sS3(sc, vw=s_v.v):
                sc.wait_ge(s_v.h, vw)
                for cc in range(NCC):
                    ins = sc.activation(out=bwsl(FK32, cc), in_=scr_(cc), func=AF.Exp)
                ins.then_inc(s_sc.h, 1)
            em("scalar", sS3)
            s_sc.bump(1)

            def sV5(v, sw=s_sc.v):
                v.wait_ge(s_sc.h, sw)
                for cc in range(NCC):
                    v.tensor_mul(out=bwsl(FK32, cc), in0=bwsl(FK32, cc), in1=k_(cc))
                for cc in range(NCC):
                    v.tensor_tensor_scan(out=scn_(cc), data0=SB["m64"][:, 0:TAUC],
                                         data1=nu_(cc), initial=0.0,
                                         op0=ALU.mult, op1=ALU.add)
                v.drain()
                v.nop().then_inc(s_v.h, 1)
            em("vector", sV5)
            s_v.bump(1)

            def sS4(sc, vw=s_v.v):
                sc.wait_ge(s_v.h, vw)
                for cc in range(NCC):
                    g64 = bass.AP(tensor=F1, offset=4096 + 256 * cc + 63,
                                  ap=[[12288, 128], [64, 4]])
                    sc.activation(out=F3[:, 736 + 4 * cc:736 + 4 * cc + 4], in_=g64,
                                  func=AF.Copy)
                    ins = sc.activation(out=F3[:, 672 + 4 * cc:672 + 4 * cc + 4], in_=g64,
                                        func=AF.Exp, scale=-1.0)
                ins.then_inc(s_sc.h, 1)
            em("scalar", sS4)
            s_sc.bump(1)

            def sV6(v, sw=s_sc.v):
                v.wait_ge(s_sc.h, sw)
                for cc in range(NCC):
                    b64b = bass.AP(tensor=F3, offset=736 + 4 * cc,
                                   ap=[[2048, 128], [1, 4], [0, 64]])
                    v.tensor_tensor(out=scr_(cc), in0=scn_(cc), in1=b64b,
                                    op=ALU.subtract)
                for cc in range(NCC):
                    v.tensor_scalar_mul(out=bwsl(FK0, cc), in0=k_(cc),
                                        scalar1=vcol("u", cc))
                v.drain()
                v.nop().then_inc(s_v.h, 1)
            em("vector", sV6)
            s_v.bump(1)

            def sS5(sc, vw=s_v.v):
                sc.wait_ge(s_v.h, vw)
                for cc in range(NCC):
                    ins = sc.activation(out=k64_(cc), in_=scr_(cc), func=AF.Exp)
                ins.then_inc(s_sc.h, 1)
            em("scalar", sS5)
            s_sc.bump(1)

            def sV7(v, sw=s_sc.v):
                v.wait_ge(s_sc.h, sw)
                for cc in range(NCC):
                    v.tensor_mul(out=k64_(cc), in0=k64_(cc), in1=k_(cc))
                v.drain()
                v.nop().then_inc(s_v.h, 1)
            em("vector", sV7)
            s_v.bump(1)

            def sT1(t, vw=s_v.v):
                t.wait_ge(s_v.h, vw)
                for cc in range(NCC):
                    bank, boff = PB[cc // 4], 128 * (cc % 4)
                    for b in range(2):
                        ins = t.transpose(
                            bank[:, boff + 64 * b:boff + 64 * (b + 1)].bitcast(BF16),
                            B2[:, 12288 + 256 * cc + 128 * b:12288 + 256 * cc + 128 * (b + 1)],
                            SB["ident"][:])
                ins.then_inc(s_t.h, 1)
            em("tensor", sT1)
            tT1 = s_t.bump(1)

            def sV8(v, tT1=tT1):
                v.wait_ge(s_t.h, tT1)
                for cc in range(NCC):
                    bank, boff = PB[cc // 4], 128 * (cc % 4)
                    v.tensor_copy(out=B3[:, 4096 + 256 * cc:4096 + 256 * (cc + 1)],
                                  in_=bank[:, boff:boff + 128].bitcast(BF16))
                v.drain()
                v.nop().then_inc(s_v.h, 1)
            em("vector", sV8)
            s_v.bump(1)

            def sS6(sc, tT1=tT1):
                sc.wait_ge(s_t.h, tT1)
                for cc in range(NCC):
                    ins = sc.activation(out=k64_(cc, 1, TAUC), in_=scn_(cc)[:, 0:TAUC - 1],
                                        func=AF.Exp, scale=-1.0)
                ins.then_inc(s_sc.h, 1)
            em("scalar", sS6)
            s_sc.bump(1)

            def sV9(v, sw=s_sc.v):
                v.wait_ge(s_sc.h, sw)
                for cc in range(NCC):
                    fix = bass.AP(tensor=B2, offset=12288 + 256 * cc,
                                  ap=[[16384, 128], [64, 4]])
                    v.memset(fix, 1.0)
                v.drain()
                for cc in range(NCC):
                    v.tensor_mul(out=k64_(cc), in0=k64_(cc), in1=r_(cc))
                v.drain()
                v.nop().then_inc(s_v.h, 1)
            em("vector", sV9)
            fac_done = s_v.bump(1)

            nit = 2 * nch_tc

            def pcl_(i, po=0, fo=0, p=128, w=512):
                if i % 2 == 0:
                    return SB["Pcl"][po:po + p, fo:fo + w]
                return F3[po:po + p, 1536 + fo // 2:1536 + (fo + w) // 2].bitcast(BF16)

            def mk_ftp(i, wv):
                ch, half = divmod(i, 2)
                t0 = LC * ch

                def ftp(t, t0=t0, half=half, wv=wv):
                    t.wait_ge(s_v.h, wv)
                    for ti, cc in enumerate(range(8 * half, 8 * (half + 1))):
                        for hh in range(2):
                            po, fo = 64 * hh, 64 * ti
                            tp = (64 * hh, 64 * hh)
                            for bank, slot in [(PB[0], FK8), (PB[1], FKA),
                                               (PB[2], FKF2), (PB[3], FKF3),
                                               (PB[4], FK32)]:
                                rslot = FR32 if slot == FK32 else FR8
                                t.matmul(bank[po:po + 64, fo:fo + 64],
                                         bwsl(slot, cc, t0, t0 + LC)[64 * hh:64 * hh + 64, :],
                                         bwsl(rslot, cc, t0, t0 + LC)[64 * hh:64 * hh + 64, :],
                                         start=True, stop=True, tile_position=tp)
                            ins = t.matmul(
                                PB[5][po:po + 64, fo:fo + 64],
                                bwsl(FK0, cc, t0, t0 + LC)[64 * hh:64 * hh + 64, :],
                                B2[64 * hh:64 * hh + 64, 4096 + 256 * cc + t0:4096 + 256 * cc + t0 + LC],
                                start=True, stop=True, tile_position=tp)
                    ins.then_inc(s_t.h, 1)
                return ftp

            def mk_fvc(i, wt):
                def fvc(v, i=i, wt=wt):
                    v.wait_ge(s_t.h, wt)
                    v.tensor_tensor(out=pcl_(i), in0=PB[0][:, :], in1=SB["MA"][:],
                                    op=ALU.mult)
                    v.drain()
                    for mask, bank in [("MB", PB[1]), ("MC", PB[2]), ("MD", PB[3]),
                                       ("ME", PB[4]), ("MF", PB[5])]:
                        v.select(out=pcl_(i), mask=SB[mask][:], on_true=bank[:, :],
                                 on_false=pcl_(i), add_drain=True)
                        v.drain()
                    v.nop().then_inc(s_v.h, 1)
                return fvc

            def mk_fti(i, wv, wsc):
                ch, half = divmod(i, 2)
                t0 = LC * ch

                def fti(t, i=i, t0=t0, half=half, wv=wv, wsc=wsc):
                    t.wait_ge(s_v.h, wv)
                    if wsc is not None:
                        t.wait_ge(s_sc.h, wsc)
                    vb = t0 // 128
                    vrow = 64 * ((t0 // 64) % 2)
                    for ti, cc in enumerate(range(8 * half, 8 * (half + 1))):
                        for hh in range(2):
                            po, fo = 64 * hh, 64 * ti
                            hd = 2 * cc + hh
                            t.matmul(PB[6][po:po + 64, fo:fo + 64],
                                     SB["Sb"][64 * hh:64 * hh + 64, cc, :],
                                     B2[64 * hh:64 * hh + 64, 12288 + 256 * cc + t0:12288 + 256 * cc + t0 + LC],
                                     start=True, stop=False, tile_position=(64 * hh, po))
                            if po == vrow:
                                vsrc = B3[po:po + 64, 2048 * vb + 64 * hd:2048 * vb + 64 * hd + 64]
                            else:
                                vsrc = B2[po:po + 64, 8192 + 2048 * vb + 64 * hd:8192 + 2048 * vb + 64 * hd + 64]
                            t.matmul(PB[6][po:po + 64, fo:fo + 64], vsrc,
                                     pcl_(i, po, fo, 64, 64),
                                     start=False, stop=True, tile_position=(po, po))
                            ins = t.matmul(
                                PB[7][po:po + 64, fo:fo + 64],
                                B3[vrow:vrow + 64, 4096 + 256 * cc + 128 * vb + 64 * hh:4096 + 256 * cc + 128 * vb + 64 * hh + 64],
                                B3[vrow:vrow + 64, 2048 * vb + 64 * hd:2048 * vb + 64 * hd + 64],
                                start=True, stop=True, tile_position=(vrow, po))
                    ins.then_inc(s_t.h, 1)
                return fti

            def mk_fve(i, wt):
                ch, half = divmod(i, 2)

                def fve(v, half=half, ch=ch, wt=wt):
                    v.wait_ge(s_t.h, wt)
                    sap = bass.AP(tensor=SB["S"], offset=64 * 8 * half,
                                  ap=[[1024, 128], [64, 8], [1, 64]])
                    d64 = bass.AP(tensor=F3, offset=672 + 4 * 8 * half + ch,
                                  ap=[[2048, 128], [4, 8], [0, 64]])
                    v.tensor_tensor(out=F3[:, 1024:1536], in0=sap, in1=d64, op=ALU.mult)
                    v.drain()
                    v.tensor_add(out=sap, in0=F3[:, 1024:1536], in1=PB[7][:, :])
                    v.drain()
                    sbap = bass.AP(tensor=SB["Sb"], offset=64 * 8 * half,
                                   ap=[[1024, 128], [64, 8], [1, 64]])
                    v.tensor_copy(out=sbap, in_=sap)
                    v.drain()
                    v.nop().then_inc(s_v.h, 1)
                return fve

            def mk_foe(i, wt):
                ch, half = divmod(i, 2)
                t0 = LC * ch

                def foe(sc, half=half, t0=t0, wt=wt):
                    sc.wait_ge(s_t.h, wt)
                    oap = bass.AP(tensor=F1, offset=8192 + 256 * 8 * half + t0,
                                  ap=[[12288, 128], [256, 8], [1, 64]])
                    sc.activation(out=oap, in_=PB[6][:, :],
                                  func=AF.Copy).then_inc(s_sc.h, 1)
                return foe

            tPl = [None] * nit
            vCl = [None] * nit
            tIl = [None] * nit
            vEl = [None] * nit
            sOl = [None] * nit
            for i in range(nit):
                em("tensor", mk_ftp(i, vCl[i - 1] if i > 0 else fac_done))
                tPl[i] = s_t.bump(1)
                em("vector", mk_fvc(i, tPl[i]))
                vCl[i] = s_v.bump(1)
                if i >= 1:
                    j = i - 1
                    wv = vEl[j - 1] if j >= 1 else vCl[0]
                    wsc = sOl[j - 1] if j >= 1 else prev_sO_last
                    em("tensor", mk_fti(j, wv, wsc))
                    tIl[j] = s_t.bump(1)
                    em("vector", mk_fve(j, tIl[j]))
                    vEl[j] = s_v.bump(1)
                    em("scalar", mk_foe(j, tIl[j]))
                    sOl[j] = s_sc.bump(1)
            j = nit - 1
            em("tensor", mk_fti(j, vEl[j - 1], sOl[j - 1]))
            tIl[j] = s_t.bump(1)
            em("vector", mk_fve(j, tIl[j]))
            vEl[j] = s_v.bump(1)
            em("scalar", mk_foe(j, tIl[j]))
            sOl[j] = s_sc.bump(1)
            prev_sO_last = sOl[j]

            def fso(s, tc=tc, wsc=sOl[nit - 1]):
                s.wait_ge(s_sc.h, wsc)
                s.dma_start(out=spl("o_sp", TAUC * tc, TAUC * (tc + 1)),
                            in_=s3(F1[:, 8192:12288])).then_inc(s_ds.h, 16)
            em("sync", fso)
            prev_oso = s_ds.bump(16)

            def fvg(v):
                for ch in range(nch_tc):
                    d64 = bass.AP(tensor=F3, offset=672 + ch, ap=[[2048, 128], [4, NCC]])
                    v.tensor_tensor(out=SB["Gfin"][:], in0=SB["Gfin"][:], in1=d64,
                                    op=ALU.mult)
                    v.drain()
                v.nop().then_inc(s_v.h, 1)
            em("vector", fvg)
            s_v.bump(1)

    def ph_AG():
        vS = s_v.v
        dg0 = s_dg.v

        def fg(g, vS=vS, dg0=dg0):
            g.wait_ge(s_v.h, vS)
            g.dma_start(out=D["cc_in"][:, 0:1024],
                        in_=SB["S"][:].rearrange("p c m -> p (c m)")).then_inc(s_dg.h, 16)
            g.dma_start(out=D["cc_in"][:, 1024:1040], in_=SB["Gfin"][:]).then_inc(s_dg.h, 16)
            g.wait_ge(s_dg.h, dg0 + 32)
            g.collective_compute(
                "AllGather", ALU.bypass, replica_groups=[list(range(NCORES))],
                ins=[D["cc_in"]], outs=[D["cc_out"]],
            ).then_inc(s_g.h, 1)
        em("gpsimd", fg)
        s_dg.bump(32)
        gT = s_g.bump(1)

        def fl(s, gT=gT):
            s.wait_ge(s_g.h, gT)
            s.dma_start(out=F1[:, 0:8320].rearrange("p (r f) -> p r f", r=NCORES),
                        in_=D["cc_out"].transpose([1, 0, 2])).then_inc(s_ds.h, 16)
        em("sync", fl)
        ld = s_ds.bump(16)

        def fv(v, ld=ld):
            v.wait_ge(s_ds.h, ld)
            v.memset(F1[:, 8320:9344], 0.0)
            v.memset(F3[:, 0:NCC], 1.0)
            v.drain()
            for j in range(NCORES - 1, -1, -1):
                v.tensor_scalar_mul(out=F3[:, 16:32], in0=F3[:, 0:NCC],
                                    scalar1=SB["rankmask"][:, j:j + 1])
                v.drain()
                facb = bass.AP(tensor=F3, offset=16, ap=[[2048, 128], [1, NCC], [0, HS]])
                v.tensor_tensor(out=F3[:, 1024:2048],
                                in0=F1[:, 1040 * j:1040 * j + 1024], in1=facb, op=ALU.mult)
                v.drain()
                v.tensor_add(out=F1[:, 8320:9344], in0=F1[:, 8320:9344],
                             in1=F3[:, 1024:2048])
                v.drain()
                v.tensor_single_scalar(out=F3[:, 32:48],
                                       in_=F1[:, 1040 * j + 1024:1040 * j + 1040],
                                       scalar=1.0, op=ALU.subtract)
                v.drain()
                v.tensor_scalar(out=F3[:, 32:48], in0=F3[:, 32:48],
                                scalar1=SB["rankmask"][:, j:j + 1], scalar2=1.0,
                                op0=ALU.mult, op1=ALU.add)
                v.drain()
                v.tensor_mul(out=F3[:, 0:NCC], in0=F3[:, 0:NCC], in1=F3[:, 32:48])
                v.drain()
            v.tensor_copy(out=SB["Sinb"][:].rearrange("p c m -> p (c m)"),
                          in_=F1[:, 8320:9344])
            v.drain()
            v.memset(F3[:, 2044:2048], 0.0)
            v.drain()
            v.nop().then_inc(s_v.h, 1)
        em("vector", fv)
        vcomb = s_v.bump(1)

        if "ag_dbg" in dbg:
            D["ag_dbg"] = nc.dram_tensor("ag_dbg", [NCORES, 128, 1040], F32,
                                         kind="ExternalOutput").ap()
            D["sin_dbg"] = nc.dram_tensor("sin_dbg", [128, NCC, HS], BF16,
                                          kind="ExternalOutput").ap()

            def fdbg(s, vcomb=vcomb):
                s.wait_ge(s_v.h, vcomb)
                s.dma_start(out=D["ag_dbg"], in_=D["cc_out"]).then_inc(s_ds.h, 16)
                s.dma_start(out=D["sin_dbg"], in_=SB["Sinb"][:]).then_inc(s_ds.h, 16)
            em("sync", fdbg)
            s_ds.bump(32)

    def ph_C2():
        prev_fsg = 0
        for tc in range(ntc):
            prev_t = s_t.v
            prev_v = s_v.v

            def fl(s, tc=tc, prev_t=prev_t, prev_v=prev_v):
                if tc > 0:
                    s.wait_ge(s_t.h, prev_t)
                    s.wait_ge(s_v.h, prev_v)
                s.dma_start(out=s3(F1[:, 8192:12288]),
                            in_=spl("o_sp", TAUC * tc, TAUC * (tc + 1))).then_inc(s_ds.h, 16)
                s.dma_start(out=s3(F1[:, 0:4096]),
                            in_=spl("nu_sp", TAUC * tc, TAUC * (tc + 1))).then_inc(s_ds.h, 16)
                s.dma_start(out=s3(B2[:, 4096:8192]),
                            in_=spl("r_sp", TAUC * tc, TAUC * (tc + 1))).then_inc(s_ds.h, 16)
                s.dma_start(out=s3(B2[:, 0:4096]),
                            in_=spl("g_sp", TAUC * tc, TAUC * (tc + 1))).then_inc(s_ds.h, 16)
            em("sync", fl)
            ld = s_ds.bump(16 * 4)

            def scn2_(cc):
                return F1[:, 4096 + 256 * cc:4096 + 256 * (cc + 1)]

            def nu2_(cc):
                return F1[:, 256 * cc:256 * (cc + 1)]

            def cV1(v, ld=ld):
                v.wait_ge(s_ds.h, ld)
                for cc in range(NCC):
                    v.tensor_tensor_scan(out=scn2_(cc), data0=SB["m64"][:, 0:TAUC],
                                         data1=nu2_(cc), initial=0.0,
                                         op0=ALU.mult, op1=ALU.add)
                v.drain()
                v.nop().then_inc(s_v.h, 1)
            em("vector", cV1)
            s_v.bump(1)

            def cS1(sc, vw=s_v.v):
                sc.wait_ge(s_v.h, vw)
                for cc in range(NCC):
                    sc.activation(out=B3[:, 256 * cc + 1:256 * (cc + 1)],
                                  in_=scn2_(cc)[:, 0:TAUC - 1], func=AF.Exp, scale=-1.0)
                    g64 = bass.AP(tensor=F1, offset=4096 + 256 * cc + 63,
                                  ap=[[12288, 128], [64, 4]])
                    ins = sc.activation(out=F3[:, 672 + 4 * cc:672 + 4 * cc + 4], in_=g64,
                                        func=AF.Exp, scale=-1.0)
                ins.then_inc(s_sc.h, 1)
            em("scalar", cS1)
            s_sc.bump(1)

            def cV2(v, sw=s_sc.v, tc=tc):
                v.wait_ge(s_sc.h, sw)
                for cc in range(NCC):
                    fix = bass.AP(tensor=B3, offset=256 * cc, ap=[[8192, 128], [64, 4]])
                    v.memset(fix, 1.0)
                    if tc == 0:
                        v.memset(F3[:, 1792 + cc:1792 + cc + 1], 1.0)
                v.drain()
                for cc in range(NCC):
                    v.tensor_tensor_scan(out=F3[:, 832 + 4 * cc:832 + 4 * cc + 4],
                                         data0=F3[:, 672 + 4 * cc:672 + 4 * cc + 4],
                                         data1=F3[:, 2044:2048],
                                         initial=F3[:, 1792 + cc:1792 + cc + 1],
                                         op0=ALU.mult, op1=ALU.add)
                v.drain()
                for cc in range(NCC):
                    v.tensor_copy(out=F3[:, 896 + 4 * cc + 1:896 + 4 * cc + 4],
                                  in_=F3[:, 832 + 4 * cc:832 + 4 * cc + 3])
                    v.tensor_copy(out=F3[:, 896 + 4 * cc:896 + 4 * cc + 1],
                                  in_=F3[:, 1792 + cc:1792 + cc + 1])
                v.drain()
                for cc in range(NCC):
                    v.tensor_copy(out=F3[:, 1792 + cc:1792 + cc + 1],
                                  in_=F3[:, 832 + 4 * cc + 3:832 + 4 * cc + 4])
                v.drain()
                for cc in range(NCC):
                    v.tensor_mul(out=B3[:, 256 * cc:256 * (cc + 1)],
                                 in0=B3[:, 256 * cc:256 * (cc + 1)],
                                 in1=B2[:, 4096 + 256 * cc:4096 + 256 * (cc + 1)])
                v.drain()
                for cc in range(NCC):
                    gpb = bass.AP(tensor=F3, offset=896 + 4 * cc,
                                  ap=[[2048, 128], [1, 4], [0, 64]])
                    v.tensor_tensor(out=B3[:, 256 * cc:256 * (cc + 1)],
                                    in0=B3[:, 256 * cc:256 * (cc + 1)], in1=gpb,
                                    op=ALU.mult)
                v.drain()
                v.nop().then_inc(s_v.h, 1)
            em("vector", cV2)
            vR = s_v.bump(1)

            def cT1(t, vw=vR):
                t.wait_ge(s_v.h, vw)
                for cc in range(NCC):
                    bank, off = PB[cc // 2], 256 * (cc % 2)
                    for hh in range(2):
                        ins = t.matmul(bank[64 * hh:64 * hh + 64, off:off + 256],
                                       SB["Sinb"][64 * hh:64 * hh + 64, cc, :],
                                       B3[64 * hh:64 * hh + 64, 256 * cc:256 * (cc + 1)],
                                       start=True, stop=True,
                                       tile_position=(64 * hh, 64 * hh))
                ins.then_inc(s_t.h, 1)
            em("tensor", cT1)
            tC = s_t.bump(1)

            def cVc(v, tC=tC):
                v.wait_ge(s_t.h, tC)
                for cc in range(NCC):
                    off = 256 * (cc % 2)
                    v.tensor_add(out=F1[:, 8192 + 256 * cc:8192 + 256 * (cc + 1)],
                                 in0=F1[:, 8192 + 256 * cc:8192 + 256 * (cc + 1)],
                                 in1=PB[cc // 2][:, off:off + 256])
                v.drain()
                v.nop().then_inc(s_v.h, 1)
            em("vector", cVc)
            vO = s_v.bump(1)

            def fsq(sc, vO=vO):
                sc.wait_ge(s_v.h, vO)
                for cc in range(NCC):
                    sc.activation(out=B2[:, 8192 + 256 * cc:8192 + 256 * (cc + 1)],
                                  in_=F1[:, 8192 + 256 * cc:8192 + 256 * (cc + 1)],
                                  func=AF.Square)
                for cc in range(NCC):
                    ins = sc.activation(out=B2[:, 4096 + 256 * cc:4096 + 256 * (cc + 1)],
                                        in_=F1[:, 8192 + 256 * cc:8192 + 256 * (cc + 1)],
                                        func=AF.Copy)
                ins.then_inc(s_sc.h, 1)
            em("scalar", fsq)
            sq = s_sc.bump(1)

            def fts(t, sq=sq):
                t.wait_ge(s_sc.h, sq)
                for cc in range(NCC):
                    t.matmul(PB[2][0:32, 0:256], SB["gn_ones_bf"][:, cc, :],
                             B2[:, 4096 + 256 * cc:4096 + 256 * (cc + 1)],
                             start=(cc == 0), stop=(cc == 15))
                for cc in range(NCC):
                    ins = t.matmul(PB[3][0:32, 0:256], SB["gn_ones_bf"][:, cc, :],
                                   B2[:, 8192 + 256 * cc:8192 + 256 * (cc + 1)],
                                   start=(cc == 0), stop=(cc == 15))
                ins.then_inc(s_t.h, 1)
            em("tensor", fts)
            tS = s_t.bump(1)

            def fvs(v, tS=tS):
                v.wait_ge(s_t.h, tS)
                v.tensor_scalar_mul(out=F3[0:32, 1024:1280], in0=PB[2][0:32, 0:256],
                                    scalar1=1.0 / HS)
                v.drain()
                v.tensor_scalar_mul(out=F3[0:32, 1280:1536], in0=PB[3][0:32, 0:256],
                                    scalar1=1.0 / HS)
                v.drain()
                v.tensor_mul(out=F3[0:32, 1536:1792], in0=F3[0:32, 1024:1280],
                             in1=F3[0:32, 1024:1280])
                v.drain()
                v.tensor_sub(out=F3[0:32, 1280:1536], in0=F3[0:32, 1280:1536],
                             in1=F3[0:32, 1536:1792])
                v.drain()
                v.tensor_copy(out=SB["Pcl"][0:32, 0:256], in_=F3[0:32, 1024:1280])
                v.drain()
                v.nop().then_inc(s_v.h, 1)
            em("vector", fvs)
            vs = s_v.bump(1)

            def fss(sc, vs=vs):
                sc.wait_ge(s_v.h, vs)
                sc.activation(out=F3[0:32, 1536:1792], in_=F3[0:32, 1280:1536],
                              func=AF.Ln, bias=SB["eps"][0:32, 0:1])
                sc.drain()
                sc.activation(out=SB["Pcl"][0:32, 256:512], in_=F3[0:32, 1536:1792],
                              func=AF.Exp, scale=-0.5).then_inc(s_sc.h, 1)
            em("scalar", fss)
            ss = s_sc.bump(1)

            for half in range(2):
                def gT(t, half=half, ss=ss, pv=s_v.v):
                    if half == 0:
                        t.wait_ge(s_sc.h, ss)
                    else:
                        t.wait_ge(s_v.h, pv)
                    for i in range(8):
                        cc = 8 * half + i
                        gsl = SB["gn_sel"][:, cc, :]
                        off = 256 * (i % 2)
                        t.matmul(PB[i // 2][:, off:off + 256], gsl,
                                 SB["Pcl"][0:32, 0:256], start=True, stop=True)
                        ins = t.matmul(PB[4 + i // 2][:, off:off + 256], gsl,
                                       SB["Pcl"][0:32, 256:512],
                                       start=True, stop=True)
                    ins.then_inc(s_t.h, 1)
                em("tensor", gT)
                tb = s_t.bump(1)

                def gV(v, half=half, tb=tb, prev_fsg=prev_fsg):
                    v.wait_ge(s_t.h, tb)
                    if half == 0 and prev_fsg > 0:
                        v.wait_ge(s_dx.h, prev_fsg)
                    for i in range(8):
                        cc = 8 * half + i
                        off = 256 * (i % 2)
                        v.tensor_sub(out=F1[:, 4096 + 256 * cc:4096 + 256 * (cc + 1)],
                                     in0=F1[:, 8192 + 256 * cc:8192 + 256 * (cc + 1)],
                                     in1=PB[i // 2][:, off:off + 256])
                    v.drain()
                    for i in range(8):
                        cc = 8 * half + i
                        off = 256 * (i % 2)
                        v.scalar_tensor_tensor(out=F1[:, 4096 + 256 * cc:4096 + 256 * (cc + 1)],
                                               in0=F1[:, 4096 + 256 * cc:4096 + 256 * (cc + 1)],
                                               scalar=vcol("ln_g", cc),
                                               in1=PB[4 + i // 2][:, off:off + 256],
                                               op0=ALU.mult, op1=ALU.mult)
                    v.drain()
                    for i in range(8):
                        cc = 8 * half + i
                        v.scalar_tensor_tensor(out=B3[:, 4096 + 256 * cc:4096 + 256 * (cc + 1)],
                                               in0=F1[:, 4096 + 256 * cc:4096 + 256 * (cc + 1)],
                                               scalar=vcol("ln_b", cc),
                                               in1=B2[:, 256 * cc:256 * (cc + 1)],
                                               op0=ALU.add, op1=ALU.mult)
                    v.drain()
                    v.nop().then_inc(s_v.h, 1)
                em("vector", gV)
                s_v.bump(1)
            vgn = s_v.v

            def fsg(s, tc=tc, vgn=vgn):
                s.wait_ge(s_v.h, vgn)
                s.dma_start(out=spl("ogn_sp", TAUC * tc, TAUC * (tc + 1)),
                            in_=s3(B3[:, 4096:8192])).then_inc(s_dx.h, 16)
            em("sync", fsg)
            prev_fsg = s_dx.bump(16)

    def ph_D():
        def flw(g):
            for cc in range(NCC):
                g.dma_start(out=BW[:, 2048 * cc:2048 * (cc + 1)],
                            in_=D["wo_t"][128 * cc:128 * (cc + 1), :]).then_inc(s_dg.h, 16)
        em("gpsimd", flw)
        w_t = s_dg.bump(16 * NCC)

        for tau in range(ntau):
            prev_v = s_v.v
            prev_st = s_ds.v

            def flx(s, tau=tau, prev_v=prev_v, prev_st=prev_st):
                if tau > 0:
                    s.wait_ge(s_v.h, prev_v)
                    s.wait_ge(s_ds.h, prev_st)
                s.dma_start(out=s3(B2[:, 0:8192]),
                            in_=spl("ogn_sp", TAU * tau, TAU * (tau + 1))).then_inc(s_ds.h, 16)
                for cc in range(NCC):
                    s.dma_start(out=F1[:, 512 * cc:512 * (cc + 1)],
                                in_=D["hid_fm"][cc, :, 1 + TAU * tau:1 + TAU * (tau + 1)]).then_inc(s_ds.h, 16)
            em("sync", flx)
            x_t = s_ds.bump(16 * 17)

            v0 = s_v.v
            for mt in range(NCC):
                bank = PB[mt % 4]
                wait_v = v0 + mt - 3 if mt >= 4 else None

                def ftp(t, mt=mt, bank=bank, w_t=w_t, x_t=x_t, wait_v=wait_v):
                    if mt == 0:
                        t.wait_ge(s_dg.h, w_t)
                        t.wait_ge(s_ds.h, x_t)
                    if wait_v is not None:
                        t.wait_ge(s_v.h, wait_v)
                    for cc in range(NCC):
                        ins = t.matmul(bank[:, :],
                                       BW[:, 2048 * cc + 128 * mt:2048 * cc + 128 * (mt + 1)],
                                       B2[:, 512 * cc:512 * (cc + 1)],
                                       start=(cc == 0), stop=(cc == 15))
                    ins.then_inc(s_t.h, 1)
                em("tensor", ftp)
                mm_t = s_t.bump(1)

                def fvr(v, mt=mt, bank=bank, mm_t=mm_t):
                    v.wait_ge(s_t.h, mm_t)
                    v.tensor_add(out=F1[:, 512 * mt:512 * (mt + 1)],
                                 in0=F1[:, 512 * mt:512 * (mt + 1)], in1=bank[:, :])
                    v.drain()
                    v.nop().then_inc(s_v.h, 1)
                em("vector", fvr)
                s_v.bump(1)
            vr = s_v.v

            def fsq(sc, vr=vr):
                sc.wait_ge(s_v.h, vr)
                for cc in range(NCC):
                    ins = sc.activation(out=B2[:, 8192 + 512 * cc:8192 + 512 * (cc + 1)],
                                        in_=F1[:, 512 * cc:512 * (cc + 1)], func=AF.Square)
                ins.then_inc(s_sc.h, 1)
            em("scalar", fsq)
            sq = s_sc.bump(1)

            def fts(t, sq=sq):
                t.wait_ge(s_sc.h, sq)
                for cc in range(NCC):
                    t.matmul(PB[4][0:1, :], SB["ones_col"][:], F1[:, 512 * cc:512 * (cc + 1)],
                             start=(cc == 0), stop=(cc == 15))
                for cc in range(NCC):
                    ins = t.matmul(PB[5][0:1, :], SB["ones_col_bf"][:],
                                   B2[:, 8192 + 512 * cc:8192 + 512 * (cc + 1)],
                                   start=(cc == 0), stop=(cc == 15))
                ins.then_inc(s_t.h, 1)
            em("tensor", fts)
            tS = s_t.bump(1)

            def fvs(v, tS=tS):
                v.wait_ge(s_t.h, tS)
                v.tensor_scalar_mul(out=F3[0:1, 0:512], in0=PB[4][0:1, :], scalar1=1.0 / C)
                v.drain()
                v.tensor_scalar_mul(out=F3[0:1, 512:1024], in0=PB[5][0:1, :], scalar1=1.0 / C)
                v.drain()
                v.tensor_mul(out=F3[0:1, 1024:1536], in0=F3[0:1, 0:512], in1=F3[0:1, 0:512])
                v.drain()
                v.tensor_sub(out=F3[0:1, 512:1024], in0=F3[0:1, 512:1024],
                             in1=F3[0:1, 1024:1536])
                v.drain()
                v.nop().then_inc(s_v.h, 1)
            em("vector", fvs)
            vs = s_v.bump(1)

            def fss(sc, vs=vs):
                sc.wait_ge(s_v.h, vs)
                sc.activation(out=F3[0:1, 1024:1536], in_=F3[0:1, 512:1024],
                              func=AF.Ln, bias=SB["eps"][0:1, 1:2])
                sc.drain()
                sc.activation(out=F3[0:1, 512:1024], in_=F3[0:1, 1024:1536],
                              func=AF.Exp, scale=-0.5).then_inc(s_sc.h, 1)
            em("scalar", fss)
            ss = s_sc.bump(1)

            def ftb(t, ss=ss):
                t.wait_ge(s_sc.h, ss)
                t.matmul(PB[6][:, :], SB["ones_row"][:], F3[0:1, 0:512],
                         start=True, stop=True)
                t.matmul(PB[7][:, :], SB["ones_row"][:], F3[0:1, 512:1024],
                         start=True, stop=True).then_inc(s_t.h, 1)
            em("tensor", ftb)
            tb = s_t.bump(1)

            for cc in range(NCC):
                def fva(v, cc=cc, tb=tb):
                    if cc == 0:
                        v.wait_ge(s_t.h, tb)
                    v.tensor_sub(out=F1[:, 512 * cc:512 * (cc + 1)],
                                 in0=F1[:, 512 * cc:512 * (cc + 1)], in1=PB[6][:, :])
                    v.drain()
                    v.tensor_mul(out=F1[:, 512 * cc:512 * (cc + 1)],
                                 in0=F1[:, 512 * cc:512 * (cc + 1)], in1=PB[7][:, :])
                    v.drain()
                    v.tensor_copy(out=B2[:, 512 * cc:512 * (cc + 1)],
                                  in_=F1[:, 512 * cc:512 * (cc + 1)])
                    v.drain()
                    v.nop().then_inc(s_v.h, 1)
                em("vector", fva)
                s_v.bump(1)
            va = s_v.v

            def fso(s, tau=tau, va=va):
                s.wait_ge(s_v.h, va)
                s.dma_start(out=spl("x1f_sp", TAU * tau, TAU * (tau + 1)),
                            in_=s3(F1[:, 0:8192])).then_inc(s_ds.h, 16)
                s.dma_start(out=spl("x1b_sp", TAU * tau, TAU * (tau + 1)),
                            in_=s3(B2[:, 0:8192])).then_inc(s_ds.h, 16)
            em("sync", fso)
            s_ds.bump(32)

    def ph_E():
        def flx(s):
            ins = s.dma_start(
                out=BW[:, 0:NCC * t_core].rearrange("p (c t) -> p c t", c=NCC),
                in_=D["x1b_sp"].transpose([1, 0, 2]))
            ins.then_inc(s_dx.h, 16)
        em("sync", flx)
        x_t = s_dx.bump(16)

        hs_ready = s_g.v

        prev_eg_mm = 0
        for eg in range(16):
            e0 = 512 * eg
            for wi, wname in [(0, "wu_t"), (1, "wgate_t")]:
                prev_sc = s_sc.v
                prev_fqv = s_v.v

                def flw(s, wname=wname, e0=e0, prev_sc=prev_sc, first=(eg == 0 and wi == 0)):
                    if not first:
                        s.wait_ge(s_sc.h, prev_sc)
                    for cc in range(NCC):
                        s.dma_start(out=F1[:, 512 * cc:512 * (cc + 1)],
                                    in_=D[wname][128 * cc:128 * (cc + 1), e0:e0 + 512]).then_inc(s_ds.h, 16)
                em("sync", flw)
                lw = s_ds.bump(16 * NCC)

                scr = F1[:, 8192:12288].bitcast(BF16)

                def fq(sc, wi=wi, lw=lw, scr=scr, hs_ready=hs_ready,
                       prev_fqv=prev_fqv, prev_eg_mm=prev_eg_mm,
                       first=(eg == 0 and wi == 0)):
                    sc.wait_ge(s_ds.h, lw)
                    if first:
                        sc.wait_ge(s_g.h, hs_ready)
                    if prev_fqv > 0:
                        sc.wait_ge(s_v.h, prev_fqv)
                    if prev_eg_mm > 0:
                        sc.wait_ge(s_t.h, prev_eg_mm)
                    sc.sign(out=scr, in_=F1[:, 0:8192],
                            bias=SB["hs"][:, 4 + wi:5 + wi])
                    sc.drain()
                    sc.sign(out=B2[:, wi * 8192:wi * 8192 + 8192], in_=F1[:, 0:8192],
                            bias=SB["hs"][:, wi:wi + 1]).then_inc(s_sc.h, 1)
                em("scalar", fq)
                qs = s_sc.bump(1)

                def fqv(v, wi=wi, qs=qs, scr=scr):
                    v.wait_ge(s_sc.h, qs)
                    v.tensor_add(out=B2[:, wi * 8192:wi * 8192 + 8192],
                                 in0=B2[:, wi * 8192:wi * 8192 + 8192], in1=scr)
                    v.drain()
                    v.nop().then_inc(s_v.h, 1)
                em("vector", fqv)
                s_v.bump(1)
            q_done = s_v.v

            sc0 = s_sc.v
            vE0 = s_v.v
            prev_fsh = getattr(ph_E, "_pf", 0)
            nslot = 0
            for et in range(4):
                for tau in range(ntau):
                    upb = PB[(nslot % 2) * 2]
                    gtb = PB[(nslot % 2) * 2 + 1]
                    wait_sc = sc0 + (nslot - 1) if nslot >= 2 else None
                    wait_ve = vE0 + (nslot - 1) if nslot >= 2 else None

                    def ftm(t, et=et, tau=tau, upb=upb, gtb=gtb, q_done=q_done,
                            x_t=x_t, wait_sc=wait_sc, wait_ve=wait_ve,
                            first=(nslot == 0)):
                        if first:
                            t.wait_ge(s_v.h, q_done)
                            t.wait_ge(s_dx.h, x_t)
                        if wait_sc is not None:
                            t.wait_ge(s_sc.h, wait_sc)
                        if wait_ve is not None:
                            t.wait_ge(s_v.h, wait_ve)
                        for cc in range(NCC):
                            t.matmul(upb[:, :],
                                     B2[:, 512 * cc + 128 * et:512 * cc + 128 * (et + 1)],
                                     BW[:, t_core * cc + 512 * tau:t_core * cc + 512 * (tau + 1)],
                                     start=(cc == 0), stop=(cc == 15))
                        for cc in range(NCC):
                            ins = t.matmul(gtb[:, :],
                                           B2[:, 8192 + 512 * cc + 128 * et:8192 + 512 * cc + 128 * (et + 1)],
                                           BW[:, t_core * cc + 512 * tau:t_core * cc + 512 * (tau + 1)],
                                           start=(cc == 0), stop=(cc == 15))
                        ins.then_inc(s_t.h, 1)
                    em("tensor", ftm)
                    mm_t = s_t.bump(1)

                    def fup(sc, upb=upb, mm_t=mm_t,
                            wv=(vE0 + nslot if nslot >= 1 else None)):
                        sc.wait_ge(s_t.h, mm_t)
                        if wv is not None:
                            sc.wait_ge(s_v.h, wv)
                        sc.activation(out=SB["Pcl"][:], in_=upb[:, :], func=AF.Silu,
                                      scale=SB["hs"][:, 0:1]).then_inc(s_sc.h, 1)
                    em("scalar", fup)
                    up_t = s_sc.bump(1)

                    def fh(v, et=et, tau=tau, gtb=gtb, up_t=up_t):
                        v.wait_ge(s_sc.h, up_t)
                        v.scalar_tensor_tensor(
                            out=B3[:, 512 * ntau * et + 512 * tau:512 * ntau * et + 512 * (tau + 1)],
                            in0=gtb[:, :], scalar=SB["hs"][:, 1:2], in1=SB["Pcl"][:],
                            op0=ALU.mult, op1=ALU.mult)
                        v.drain()
                        v.nop().then_inc(s_v.h, 1)
                    em("vector", fh)
                    s_v.bump(1)
                    nslot += 1
            h_done = s_v.v
            prev_eg_mm = s_t.v

            def fsh(s, eg=eg, h_done=h_done):
                s.wait_ge(s_v.h, h_done)
                for et in range(4):
                    s.dma_start(out=D["h_sp"][4 * eg + et],
                                in_=B3[:, 512 * ntau * et:512 * ntau * (et + 1)]).then_inc(s_ds.h, 16)
            em("sync", fsh)
            ph_E._pf = s_ds.bump(16 * 4)

        h_stores = s_ds.v
        ph_E._mm_end = s_t.v
        ph_E._h_sp_end = h_stores
        qsl = [None] * 128
        spl_l = [None] * 128
        dg0 = s_dg.v
        for idx in range(128):
            sl, hf = divmod(idx, 2)

            def flw(g, sl=sl, hf=hf, idx=idx, pq=(qsl[idx - 2] if idx >= 2 else None)):
                if pq is not None:
                    g.wait_ge(s_sc.h, pq)
                g.dma_start(out=F3[:, 1024 * (idx % 2):1024 * (idx % 2) + 1024],
                            in_=D["wd_t"][128 * sl:128 * (sl + 1), 1024 * hf:1024 * (hf + 1)]).then_inc(s_dg.h, 16)
            em("gpsimd", flw)
            lw = s_dg.bump(16)

            def fq(sc, idx=idx, lw=dg0 + 16 * (min(idx + 1, 127) + 1), h_stores=h_stores,
                   pspill=(spl_l[idx - 2] if idx >= 2 else None)):
                sc.wait_ge(s_dg.h, lw)
                if idx < 2:
                    sc.wait_ge(s_ds.h, h_stores)
                if pspill is not None:
                    sc.wait_ge(s_ds.h if idx % 2 == 0 else s_dx.h, pspill)
                fsl = F3[:, 1024 * (idx % 2):1024 * (idx % 2) + 1024]
                sc.sign(out=B3[:, 1024 * (idx % 2):1024 * (idx % 2) + 1024],
                        in_=fsl, bias=SB["hs"][:, 6:7])
                sc.sign(out=B3[:, 2048 + 1024 * (idx % 2):2048 + 1024 * (idx % 2) + 1024],
                        in_=fsl, bias=SB["hs"][:, 2:3]).then_inc(s_sc.h, 1)
            em("scalar", fq)
            qsl[idx] = s_sc.bump(1)

            def fqv(v, idx=idx, qs=qsl[idx]):
                v.wait_ge(s_sc.h, qs)
                v.tensor_add(out=B3[:, 1024 * (idx % 2):1024 * (idx % 2) + 1024],
                             in0=B3[:, 1024 * (idx % 2):1024 * (idx % 2) + 1024],
                             in1=B3[:, 2048 + 1024 * (idx % 2):2048 + 1024 * (idx % 2) + 1024])
                v.drain()
                v.nop().then_inc(s_v.h, 1)
            em("vector", fqv)
            qv = s_v.bump(1)

            def fso(s, sl=sl, hf=hf, idx=idx, qv=qv):
                s.wait_ge(s_v.h, qv)
                sem = s_ds.h if idx % 2 == 0 else s_dx.h
                for d in range(8):
                    dt = 8 * hf + d
                    s.dma_start(out=D["wdq_sp"][dt, :, sl, :],
                                in_=B3[:, 1024 * (idx % 2) + 128 * d:1024 * (idx % 2) + 128 * (d + 1)]).then_inc(sem, 16)
            em("sync", fso)
            spl_l[idx] = (s_ds if idx % 2 == 0 else s_dx).bump(16 * 8)
        ph_E._wd_end = (s_ds.v, s_dx.v)

    def ph_F():
        mm_end = ph_E._mm_end
        h_sp_end = ph_E._h_sp_end
        wd_end = ph_E._wd_end
        prev_out = 0
        NG = ntau * NCC
        lwg = [None] * NG
        mmg = [None] * NG
        vrg = [None] * NG
        sqg = [None] * NG
        for tau in range(ntau):
            prev_t = s_t.v

            def flh(s, tau=tau, prev_t=prev_t):
                if tau > 0:
                    s.wait_ge(s_t.h, prev_t)
                else:
                    s.wait_ge(s_t.h, mm_end)
                    s.wait_ge(s_ds.h, h_sp_end)
                for q in range(4):
                    s.dma_start(
                        out=BW[:, 8192 * q:8192 * (q + 1)].rearrange("p (e t) -> p e t", e=16),
                        in_=D["h_sp"][16 * q:16 * (q + 1), :, TAU * tau:TAU * (tau + 1)].transpose([1, 0, 2])).then_inc(s_dh.h, 16)
            em("sync", flh)
            h_t = s_dh.bump(16 * 4)

            for dt in range(NCC):
                g = NCC * tau + dt

                def flw(s, tau=tau, dt=dt, g=g,
                        wt=(mmg[g - 2] if g >= 2 else mm_end),
                        wv=(vrg[g - 2] if g >= 2 else None),
                        first=(g < 2)):
                    s.wait_ge(s_t.h, wt)
                    if wv is not None:
                        s.wait_ge(s_v.h, wv)
                    if first:
                        s.wait_ge(s_ds.h, wd_end[0])
                        s.wait_ge(s_dx.h, wd_end[1])
                    sem = s_ds.h if g % 2 == 0 else s_dx.h
                    s.dma_start(out=B2[:, 8192 * (dt % 2):8192 * (dt % 2) + 8192],
                                in_=D["wdq_sp"][dt].rearrange("p ec d -> p (ec d)")).then_inc(sem, 16)
                    s.dma_start(out=F3[:, 512 * (dt % 2):512 * (dt % 2) + 512],
                                in_=D["x1f_sp"][dt, :, TAU * tau:TAU * (tau + 1)]).then_inc(sem, 16)
                em("sync", flw)
                lwg[g] = (s_ds if g % 2 == 0 else s_dx).bump(32)

                def ftm(t, dt=dt, g=g, lw=lwg[g], h_t=h_t,
                        wv=(vrg[g - 2] if g >= 2 else None), hwait=(dt < 2)):
                    t.wait_ge(s_ds.h if g % 2 == 0 else s_dx.h, lw)
                    if hwait:
                        t.wait_ge(s_dh.h, h_t)
                    if wv is not None:
                        t.wait_ge(s_v.h, wv)
                    bank = PB[dt % 2]
                    for ec in range(NET):
                        ins = t.matmul(bank[:, :],
                                       B2[:, 8192 * (dt % 2) + 128 * ec:8192 * (dt % 2) + 128 * (ec + 1)],
                                       BW[:, 512 * ec:512 * (ec + 1)],
                                       start=(ec == 0), stop=(ec == NET - 1))
                    ins.then_inc(s_t.h, 1)
                em("tensor", ftm)
                mmg[g] = s_t.bump(1)

                def fvr(v, dt=dt, g=g, mm_t=mmg[g], prev_out=prev_out,
                        first=(dt == 0)):
                    v.wait_ge(s_t.h, mm_t)
                    if first and prev_out > 0:
                        v.wait_ge(s_ds.h, prev_out)
                    v.scalar_tensor_tensor(out=F1[:, 512 * dt:512 * (dt + 1)],
                                           in0=PB[dt % 2][:, :], scalar=SB["hs"][:, 2:3],
                                           in1=F3[:, 512 * (dt % 2):512 * (dt % 2) + 512],
                                           op0=ALU.mult, op1=ALU.add)
                    v.drain()
                    v.nop().then_inc(s_v.h, 1)
                em("vector", fvr)
                vrg[g] = s_v.bump(1)

                def fsq(sc, dt=dt, g=g, vr=vrg[g]):
                    sc.wait_ge(s_v.h, vr)
                    sc.activation(out=B3[:, 512 * dt:512 * (dt + 1)],
                                  in_=F1[:, 512 * dt:512 * (dt + 1)],
                                  func=AF.Square).then_inc(s_sc.h, 1)
                em("scalar", fsq)
                sqg[g] = s_sc.bump(1)

            def fts(t, sq=sqg[NCC * tau + NCC - 1]):
                t.wait_ge(s_sc.h, sq)
                for cc in range(NCC):
                    t.matmul(PB[4][0:1, :], SB["ones_col"][:], F1[:, 512 * cc:512 * (cc + 1)],
                             start=(cc == 0), stop=(cc == 15))
                for cc in range(NCC):
                    ins = t.matmul(PB[5][0:1, :], SB["ones_col_bf"][:],
                                   B3[:, 512 * cc:512 * (cc + 1)],
                                   start=(cc == 0), stop=(cc == 15))
                ins.then_inc(s_t.h, 1)
            em("tensor", fts)
            tS = s_t.bump(1)

            def fvs(v, tS=tS):
                v.wait_ge(s_t.h, tS)
                v.tensor_scalar_mul(out=F1[0:1, 8192:8704], in0=PB[4][0:1, :],
                                    scalar1=1.0 / C)
                v.drain()
                v.tensor_scalar_mul(out=F1[0:1, 8704:9216], in0=PB[5][0:1, :],
                                    scalar1=1.0 / C)
                v.drain()
                v.tensor_mul(out=F1[0:1, 9216:9728], in0=F1[0:1, 8192:8704],
                             in1=F1[0:1, 8192:8704])
                v.drain()
                v.tensor_sub(out=F1[0:1, 8704:9216], in0=F1[0:1, 8704:9216],
                             in1=F1[0:1, 9216:9728])
                v.drain()
                v.nop().then_inc(s_v.h, 1)
            em("vector", fvs)
            vs = s_v.bump(1)

            def fss(sc, vs=vs):
                sc.wait_ge(s_v.h, vs)
                sc.activation(out=F1[0:1, 9216:9728], in_=F1[0:1, 8704:9216],
                              func=AF.Ln, bias=SB["eps"][0:1, 1:2])
                sc.drain()
                sc.activation(out=F1[0:1, 8704:9216], in_=F1[0:1, 9216:9728],
                              func=AF.Exp, scale=-0.5).then_inc(s_sc.h, 1)
            em("scalar", fss)
            ss = s_sc.bump(1)

            def ftb(t, ss=ss):
                t.wait_ge(s_sc.h, ss)
                t.matmul(PB[6][:, :], SB["ones_row"][:], F1[0:1, 8192:8704],
                         start=True, stop=True)
                t.matmul(PB[7][:, :], SB["ones_row"][:], F1[0:1, 8704:9216],
                         start=True, stop=True).then_inc(s_t.h, 1)
            em("tensor", ftb)
            tb = s_t.bump(1)

            def fva(v, tb=tb):
                v.wait_ge(s_t.h, tb)
                for cc in range(NCC):
                    v.tensor_sub(out=F1[:, 512 * cc:512 * (cc + 1)],
                                 in0=F1[:, 512 * cc:512 * (cc + 1)], in1=PB[6][:, :])
                v.drain()
                for cc in range(NCC):
                    v.tensor_mul(out=F1[:, 512 * cc:512 * (cc + 1)],
                                 in0=F1[:, 512 * cc:512 * (cc + 1)], in1=PB[7][:, :])
                v.drain()
                v.nop().then_inc(s_v.h, 1)
            em("vector", fva)
            va = s_v.bump(1)

            def fso(s, tau=tau, va=va):
                s.wait_ge(s_v.h, va)
                s.dma_start(out=spl("out_fm", TAU * tau, TAU * (tau + 1)),
                            in_=s3(F1[:, 0:8192])).then_inc(s_ds.h, 16)
            em("sync", fso)
            prev_out = s_ds.bump(16)

    allph = [("setup", ph_setup), ("scale", ph_scale), ("A", ph_A), ("B", ph_B),
             ("C", ph_C), ("AG", ph_AG), ("C2", ph_C2), ("D", ph_D), ("E", ph_E),
             ("F", ph_F)]
    NOFENCE = {"E"}
    for name, fn in allph:
        fn()
        if name not in NOFENCE or name == upto:
            fence()
        if upto is not None and name == upto:
            break

    blk = ctx.enter_context(nc.Block())

    def runner(engine_name):
        def run(eng):
            for fn in steps[engine_name]:
                fn(eng)
        return run

    blk.sync(runner("sync"))
    blk.gpsimd(runner("gpsimd"))
    blk.vector(runner("vector"))
    blk.scalar(runner("scalar"))
    blk.tensor(runner("tensor"))
    ctx.close()
    return nc, D


def kernel(**inputs):
    in_maps, t_core = prep_inputs(inputs)
    nc, D = build_program(t_core)
    res = run_bass_kernel_spmd(nc, in_maps, core_ids=list(range(NCORES)))
    outs = []
    for r in range(NCORES):
        o = np.asarray(res.results[r]["out_fm"])
        outs.append(o.reshape(C, t_core).T)
    full = np.concatenate(outs, axis=0)
    return np.ascontiguousarray(full.reshape(1, NCORES * t_core, C)).astype(np.float32)


# version 10 (final)
# speedup vs baseline: 1.2215x; 1.0434x over previous
import numpy as np
import ml_dtypes
from contextlib import ExitStack

import concourse.bass as bass
import concourse.mybir as mybir
from concourse.bass_utils import run_bass_kernel_spmd

F32 = mybir.dt.float32
BF16 = mybir.dt.bfloat16
ALU = mybir.AluOpType
AF = mybir.ActivationFunctionType

NCORES = 8
C = 2048
NCC = 16
H = 32
HS = 64
EXP = 8192
NET = EXP // 128
LC = 64
TAU = 512
TAUC = 256
NU_CLAMP = 11.0
GN_EPS = 1e-5 * 64.0
LN_EPS = 1e-5
DMIX = 32
DDEC = 64

ENGINES = ("sync", "gpsimd", "vector", "scalar", "tensor")



def _vec_fm(v):
    return np.ascontiguousarray(np.asarray(v, np.float32).reshape(NCC, 128).T)


def _host_constants():
    t = np.arange(TAU)
    consts = {
        "m8": np.broadcast_to((t[:TAUC] % 8 != 0).astype(np.float32), (128, TAUC)).copy(),
        "m32": np.broadcast_to((t[:TAUC] % 32 != 0).astype(np.float32), (128, TAUC)).copy(),
        "m64": np.broadcast_to((t[:TAUC] % LC != 0).astype(np.float32), (128, TAUC)).copy(),
    }
    s = np.arange(LC)[:, None]
    tt = np.arange(LC)[None, :]
    bs, bt = s // 8, tt // 8
    pats = {
        "MA": (bs == bt) & (s < tt),
        "MB": (bt - bs) == 1,
        "MC": (bt - bs) == 2,
        "MD": (bt - bs) == 3,
        "ME": ((bt - bs) >= 4) & (s < 32) & (tt >= 32),
        "MF": (s == tt),
    }
    valid = np.zeros((LC, LC), bool)
    for name, m in pats.items():
        if name == "MA":
            consts[name] = np.tile(m.astype(np.float32), (2, 8)).astype(ml_dtypes.bfloat16)
        else:
            consts[name] = np.tile(m.astype(np.uint8), (2, 8))
        valid |= m
    consts["ident"] = np.eye(128).astype(ml_dtypes.bfloat16)
    go = np.zeros((128, NCC, 32), np.float32)
    for cc in range(NCC):
        go[0:64, cc, 2 * cc] = 1.0
        go[64:128, cc, 2 * cc + 1] = 1.0
    consts["gn_ones"] = go
    consts["gn_ones_bf"] = go.astype(ml_dtypes.bfloat16)
    gs = np.zeros((32, NCC, 128), np.float32)
    for cc in range(NCC):
        gs[2 * cc, cc, 0:64] = 1.0
        gs[2 * cc + 1, cc, 64:128] = 1.0
    consts["gn_sel"] = gs.astype(ml_dtypes.bfloat16)
    consts["ones_col"] = np.ones((128, 1), np.float32)
    consts["ones_col_bf"] = np.ones((128, 1), np.float32).astype(ml_dtypes.bfloat16)
    consts["ones_row"] = np.ones((1, 128), np.float32)
    return consts


def prep_inputs(inp):
    x = np.asarray(inp["L0_attn_hidden"], np.float32).reshape(-1, C)
    t_total = x.shape[0]
    t_core = t_total // NCORES
    consts = _host_constants()

    def T(a):
        return np.ascontiguousarray(np.asarray(a, np.float32).T)

    wu_t = T(inp["L0_ffn_W_up"])
    wgate_t = T(inp["L0_ffn_W_gate"])
    wd_t = T(inp["L0_ffn_W_down"])

    shared = {
        "wr_t": T(inp["L0_attn_W_r"]), "wk_t": T(inp["L0_attn_W_k"]),
        "wv_t": T(inp["L0_attn_W_v"]), "wg_t": T(inp["L0_attn_W_g"]),
        "wo_t": T(inp["L0_attn_W_o"]),
        "wu_t": wu_t, "wgate_t": wgate_t, "wd_t": wd_t,
        "w1": np.ascontiguousarray(np.asarray(inp["L0_attn_time_maa_w1"], np.float32)),
        "w2": np.ascontiguousarray(np.asarray(inp["L0_attn_time_maa_w2"], np.float32)),
        "w2p": np.ascontiguousarray(
            np.asarray(inp["L0_attn_time_maa_w2"], np.float32)[0:4]
            .reshape(4, DMIX, NCC, 128).transpose(0, 1, 2, 3).reshape(128, NCC, 128)),
        "dw1": np.ascontiguousarray(np.asarray(inp["L0_attn_time_decay_w1"], np.float32)),
        "dw2": np.ascontiguousarray(
            np.asarray(inp["L0_attn_time_decay_w2"], np.float32)
            .reshape(DDEC, 8, 2, 128).transpose(2, 0, 1, 3).reshape(128, 8, 128)),
        "vecs": np.stack([
            _vec_fm(np.reshape(np.asarray(inp["L0_attn_time_maa_w"], np.float32), -1)),
            _vec_fm(np.reshape(np.asarray(inp["L0_attn_time_maa_k"], np.float32), -1)),
            _vec_fm(np.reshape(np.asarray(inp["L0_attn_time_maa_v"], np.float32), -1)),
            _vec_fm(np.reshape(np.asarray(inp["L0_attn_time_maa_r"], np.float32), -1)),
            _vec_fm(np.reshape(np.asarray(inp["L0_attn_time_maa_g"], np.float32), -1)),
            _vec_fm(np.reshape(np.asarray(inp["L0_attn_time_maa_x"], np.float32), -1)),
            _vec_fm(np.reshape(np.asarray(inp["L0_attn_time_decay"], np.float32), -1)),
            _vec_fm(np.reshape(np.asarray(inp["L0_attn_time_faaaa"], np.float32), -1)),
            _vec_fm(np.reshape(np.asarray(inp["L0_attn_ln_gamma"], np.float32), -1)),
            _vec_fm(np.reshape(np.asarray(inp["L0_attn_ln_beta"], np.float32), -1)),
        ], axis=0),
    }
    shared.update(consts)

    in_maps = []
    rows_ug = C // NCORES
    rows_d = EXP // NCORES
    for r in range(NCORES):
        d = dict(shared)
        blk = x[r * t_core:(r + 1) * t_core]
        halo = x[r * t_core - 1] if r > 0 else np.zeros(C, np.float32)
        fm = np.concatenate([halo[:, None], blk.T], axis=1)
        d["hid_fm"] = np.ascontiguousarray(fm.reshape(NCC, 128, t_core + 1))
        rm = np.zeros((128, 8), np.float32)
        rm[:, :r] = 1.0
        d["rankmask"] = rm
        d["qs_up"] = np.ascontiguousarray(wu_t[r * rows_ug:(r + 1) * rows_ug])
        d["qs_gate"] = np.ascontiguousarray(wgate_t[r * rows_ug:(r + 1) * rows_ug])
        d["qs_down"] = np.ascontiguousarray(wd_t[r * rows_d:(r + 1) * rows_d])
        in_maps.append(d)
    return in_maps, t_core



class SemCtr:
    def __init__(self, h):
        self.h = h
        self.v = 0

    def bump(self, n=1):
        self.v += n
        return self.v


def build_program(t_core, upto=None, dbg=()):
    nc = bass.Bass("TRN2", target_bir_lowering=False, debug=False)
    ntau = t_core // TAU
    ntc = t_core // TAUC
    nch_tc = TAUC // LC

    D = {}

    def dram(name, shape, dtype, kind="Internal"):
        if kind == "Internal" and name in dbg:
            kind = "ExternalOutput"
        D[name] = nc.dram_tensor(name, shape, dtype, kind=kind).ap()
        return D[name]

    dram("hid_fm", [NCC, 128, t_core + 1], F32, "ExternalInput")
    for n, shp in [("wr_t", [C, C]), ("wk_t", [C, C]), ("wv_t", [C, C]),
                   ("wg_t", [C, C]), ("wo_t", [C, C]),
                   ("wu_t", [C, EXP]), ("wgate_t", [C, EXP]), ("wd_t", [EXP, C]),
                   ("w1", [C, 160]), ("w2", [5, DMIX, C]), ("w2p", [128, NCC, 128]),
                   ("dw1", [C, DDEC]), ("dw2", [128, 8, 128]),
                   ("vecs", [10, 128, NCC]),
                   ("m8", [128, TAUC]), ("m32", [128, TAUC]), ("m64", [128, TAUC]),
                   ("gn_ones", [128, NCC, 32]),
                   ("ones_col", [128, 1]), ("ones_row", [1, 128]),
                   ("rankmask", [128, 8]),
                   ("qs_up", [C // NCORES, EXP]), ("qs_gate", [C // NCORES, EXP]),
                   ("qs_down", [EXP // NCORES, C])]:
        dram(n, shp, F32, "ExternalInput")
    dram("MA", [128, 512], BF16, "ExternalInput")
    for n in ["MB", "MC", "MD", "ME", "MF"]:
        dram(n, [128, 512], mybir.dt.uint8, "ExternalInput")
    dram("ident", [128, 128], BF16, "ExternalInput")
    dram("gn_sel", [32, NCC, 128], BF16, "ExternalInput")
    dram("gn_ones_bf", [128, NCC, 32], BF16, "ExternalInput")
    dram("ones_col_bf", [128, 1], BF16, "ExternalInput")

    dram("out_fm", [NCC, 128, t_core], F32, "ExternalOutput")

    dram("xf_sp", [5, NCC, 128, t_core], BF16)
    dram("r_sp", [NCC, 128, t_core], BF16)
    dram("k_sp", [NCC, 128, t_core], BF16)
    dram("g_sp", [NCC, 128, t_core], BF16)
    dram("vtm_sp", [t_core // 128, 128, C], BF16)
    dram("vtm2_sp", [t_core // 128, 128, C], BF16)
    dram("nu_sp", [NCC, 128, t_core], F32)
    dram("o_sp", [NCC, 128, t_core], F32)
    dram("ogn_sp", [NCC, 128, t_core], BF16)
    dram("x1f_sp", [NCC, 128, t_core], F32)
    dram("x1b_sp", [NCC, 128, t_core], BF16)
    dram("h_sp", [NET, 128, t_core], BF16)
    dram("wdq_sp", [NCC, 128, EXP // 128, 128], BF16)
    dram("sc_in", [1, 4], F32)
    dram("sc_out", [NCORES, 1, 4], F32)
    dram("cc_in", [128, 1040], F32)
    dram("cc_out", [NCORES, 128, 1040], F32)

    ctx = ExitStack()

    def sbuf(name, shape, dtype):
        return ctx.enter_context(nc.sbuf_tensor(name, shape, dtype))

    def psumt(name, shape):
        return ctx.enter_context(nc.psum_tensor(name, shape, F32))

    def mksem(name):
        return SemCtr(ctx.enter_context(nc.semaphore(name)))

    SB = {}
    SB["vecs"] = sbuf("sb_vecs", [128, 10, NCC], F32)
    for n in ["m8", "m32", "m64"]:
        SB[n] = sbuf("sb_" + n, [128, TAUC], F32)
    SB["MA"] = sbuf("sb_MA", [128, 512], BF16)
    for n in ["MB", "MC", "MD", "ME", "MF"]:
        SB[n] = sbuf("sb_" + n, [128, 512], mybir.dt.uint8)
    SB["ident"] = sbuf("sb_ident", [128, 128], BF16)
    SB["gn_ones_bf"] = sbuf("sb_gnonesb", [128, NCC, 32], BF16)
    SB["gn_sel"] = sbuf("sb_gnsel", [DMIX, NCC, 128], BF16)
    SB["ones_col"] = sbuf("sb_onescol", [128, 1], F32)
    SB["ones_col_bf"] = sbuf("sb_onescolb", [128, 1], BF16)
    SB["ones_row"] = sbuf("sb_onesrow", [1, 128], F32)
    SB["rankmask"] = sbuf("sb_rankmask", [128, 8], F32)
    SB["w1"] = sbuf("sb_w1", [128, NCC, 160], BF16)
    SB["w2"] = sbuf("sb_w2", [128, NCC, 128], BF16)
    SB["w2g"] = sbuf("sb_w2g", [DMIX, NCC, 128], BF16)
    SB["dw1"] = sbuf("sb_dw1", [128, NCC, DDEC], BF16)
    SB["dw2"] = sbuf("sb_dw2", [128, NCC // 2, 128], BF16)
    SB["S"] = sbuf("sb_S", [128, NCC, HS], F32)
    SB["Sb"] = sbuf("sb_Sb", [128, NCC, HS], BF16)
    SB["Sinb"] = sbuf("sb_Sinb", [128, NCC, HS], BF16)
    SB["hs"] = sbuf("sb_hs", [128, 8], F32)
    SB["Gfin"] = sbuf("sb_Gfin", [128, NCC], F32)
    SB["Pcl"] = sbuf("sb_Pcl", [128, 512], BF16)
    SB["scl"] = sbuf("sb_scl", [1, 40], F32)
    SB["eps"] = sbuf("sb_eps", [128, 2], F32)

    BW = sbuf("ar_BW", [128, 32768], BF16)
    B2 = sbuf("ar_B2", [128, 16384], BF16)
    B3 = sbuf("ar_B3", [128, 8192], BF16)
    F1 = sbuf("ar_F1", [128, 12288], F32)
    F3 = sbuf("ar_F3", [128, 2048], F32)

    PB = [psumt(f"pb{i}", [128, 512]) for i in range(8)]

    s_ds = mksem("s_ds")
    s_dx = mksem("s_dx")
    s_dg = mksem("s_dg")
    s_v = mksem("s_v")
    s_sc = mksem("s_sc")
    s_t = mksem("s_t")
    s_g = mksem("s_g")
    s_dh = mksem("s_dh")
    s_dg2 = mksem("s_dg2")
    s_dg3 = mksem("s_dg3")
    s_do = mksem("s_do")
    ALLSEMS = [s_ds, s_dx, s_dg, s_v, s_sc, s_t, s_g, s_dh, s_dg2, s_dg3, s_do]

    steps = {e: [] for e in ENGINES}

    def em(engine, fn):
        steps[engine].append(fn)

    def fence():
        tgt = [(s.h, s.v) for s in ALLSEMS if s.v > 0]
        for e in ENGINES:
            def f(eng, tgt=tuple(tgt)):
                for h, v in tgt:
                    eng.wait_ge(h, v)
            em(e, f)

    VEC = {"maa_w": 0, "maa_k": 1, "maa_v": 2, "maa_r": 3, "maa_g": 4,
           "maa_x": 5, "td": 6, "u": 7, "ln_g": 8, "ln_b": 9}

    def vcol(name, cc):
        return SB["vecs"][:, VEC[name], cc:cc + 1]

    def spl(name, a, b, f=None):
        if f is None:
            return D[name][:, :, a:b].transpose([1, 0, 2])
        return D[name][f, :, :, a:b].transpose([1, 0, 2])

    def s3(ap, t=None):
        n = ap.shape[1]
        t = t if t is not None else n // NCC
        return ap.rearrange("p (c t) -> p c t", t=t)

    def ph_setup():
        def f(s):
            s.dma_start(out=SB["vecs"][:], in_=D["vecs"].rearrange("v p c -> p v c")).then_inc(s_ds.h, 16)
            for n in ["m8", "m32", "m64", "MA", "MB", "MC", "MD", "ME", "MF",
                      "ident", "ones_col", "ones_row", "rankmask", "gn_sel"]:
                s.dma_start(out=SB[n][:], in_=D[n]).then_inc(s_ds.h, 16)
            s.dma_start(out=SB["gn_ones_bf"][:], in_=D["gn_ones_bf"]).then_inc(s_ds.h, 16)
            s.dma_start(out=SB["ones_col_bf"][:], in_=D["ones_col_bf"]).then_inc(s_ds.h, 16)
        em("sync", f)
        s_ds.bump(16 * 17)

        def g(g):
            g.dma_start(out=SB["w1"][:], in_=D["w1"].rearrange("(cc p) f -> p cc f", p=128)).then_inc(s_dg.h, 16)
            g.dma_start(out=SB["w2"][:], in_=D["w2p"]).then_inc(s_dg.h, 16)
            g.dma_start(out=SB["w2g"][:], in_=D["w2"][4].rearrange("d (cc m) -> d cc m", m=128)).then_inc(s_dg.h, 16)
            g.dma_start(out=SB["dw1"][:], in_=D["dw1"].rearrange("(cc p) f -> p cc f", p=128)).then_inc(s_dg.h, 16)
            g.dma_start(out=SB["dw2"][:], in_=D["dw2"]).then_inc(s_dg.h, 16)
        em("gpsimd", g)
        s_dg.bump(16 * 5)

        def v(v):
            v.memset(SB["S"][:], 0.0)
            v.memset(SB["Sb"][:], 0.0)
            v.memset(SB["Sinb"][:], 0.0)
            v.memset(SB["Gfin"][:], 1.0)
            v.memset(SB["scl"][:], 0.0)
            v.memset(F3[:, 2044:2048], 0.0)
            v.memset(SB["eps"][:, 0:1], GN_EPS)
            v.memset(SB["eps"][:, 1:2], LN_EPS)
            v.drain()
            v.nop().then_inc(s_v.h, 1)
        em("vector", v)
        s_v.bump(1)

    def ph_scale():
        slabs = [("qs_up", 0, 2, 8192), ("qs_gate", 1, 2, 8192), ("qs_down", 2, 8, 2048)]
        for wname, wi, nsl, cols in slabs:
            for sl in range(nsl):
                prev_sc = s_sc.v

                def fl(s, wname=wname, sl=sl, cols=cols, prev_sc=prev_sc):
                    if prev_sc > 0:
                        s.wait_ge(s_sc.h, prev_sc)
                    s.dma_start(out=F1[:, 0:cols],
                                in_=D[wname][128 * sl:128 * (sl + 1), :]).then_inc(s_ds.h, 16)
                em("sync", fl)
                ld_t = s_ds.bump(16)

                prev_fv = s_v.v

                def fs(sc, wi=wi, sl=sl, cols=cols, ld_t=ld_t, prev_fv=prev_fv):
                    sc.wait_ge(s_ds.h, ld_t)
                    if prev_fv > 0:
                        sc.wait_ge(s_v.h, prev_fv)
                    sc.activation(out=F1[:, 0:cols], in_=F1[:, 0:cols], func=AF.Abs,
                                  accum_out=F3[:, 4 + wi:5 + wi]).then_inc(s_sc.h, 1)
                em("scalar", fs)
                sc_t = s_sc.bump(1)

                def fv(v, wi=wi, sl=sl, sc_t=sc_t):
                    v.wait_ge(s_sc.h, sc_t)
                    if sl == 0:
                        v.tensor_copy(out=F3[:, wi:wi + 1], in_=F3[:, 4 + wi:5 + wi])
                    else:
                        v.tensor_add(out=F3[:, wi:wi + 1], in0=F3[:, wi:wi + 1],
                                     in1=F3[:, 4 + wi:5 + wi])
                    v.drain()
                    v.nop().then_inc(s_v.h, 1)
                em("vector", fv)
                s_v.bump(1)
        vt = s_v.v

        def ft(t, vt=vt):
            t.wait_ge(s_v.h, vt)
            t.matmul(PB[0][0:1, 0:3], SB["ones_col"][:], F3[:, 0:3],
                     start=True, stop=True).then_inc(s_t.h, 1)
        em("tensor", ft)
        tt = s_t.bump(1)

        def fv2(v, tt=tt):
            v.wait_ge(s_t.h, tt)
            v.tensor_copy(out=SB["scl"][0:1, 0:3], in_=PB[0][0:1, 0:3])
            v.drain()
            v.nop().then_inc(s_v.h, 1)
        em("vector", fv2)
        vt2 = s_v.bump(1)

        dg0 = s_dg.v

        GWAIT1 = s_g.v + 1

        def fg(g, vt2=vt2, dg0=dg0, GWAIT1=GWAIT1):
            g.wait_ge(s_v.h, vt2)
            g.dma_start(out=D["sc_in"], in_=SB["scl"][0:1, 0:4]).then_inc(s_dg.h, 16)
            g.wait_ge(s_dg.h, dg0 + 16)
            g.collective_compute(
                "AllGather", ALU.bypass, replica_groups=[list(range(NCORES))],
                ins=[D["sc_in"]], outs=[D["sc_out"]],
            ).then_inc(s_g.h, 1)
            g.wait_ge(s_g.h, GWAIT1)
            g.dma_start(out=SB["scl"][0:1, 8:40].rearrange("p (r f) -> p r f", r=NCORES),
                        in_=D["sc_out"].transpose([1, 0, 2])).then_inc(s_dg.h, 16)
        em("gpsimd", fg)
        s_dg.bump(32)
        s_g.bump(1)
        dg_t = s_dg.v

        def fv3(v, dg_t=dg_t):
            v.wait_ge(s_dg.h, dg_t)
            for wi, numel in [(0, C * EXP), (1, C * EXP), (2, C * EXP)]:
                src = bass.AP(tensor=SB["scl"], offset=8 + wi, ap=[[40, 1], [4, 8]])
                v.tensor_reduce(out=SB["scl"][0:1, 4 + wi:5 + wi], in_=src,
                                axis=mybir.AxisListType.X, op=ALU.add)
                v.drain()
            v.tensor_scalar(out=SB["scl"][0:1, 0:3], in0=SB["scl"][0:1, 4:7],
                            scalar1=0.5 / float(C * EXP), scalar2=0.5e-8,
                            op0=ALU.mult, op1=ALU.add)
            v.drain()
            v.tensor_scalar_mul(out=SB["scl"][0:1, 4:7], in0=SB["scl"][0:1, 0:3],
                                scalar1=-1.0)
            v.drain()
            v.nop().then_inc(s_v.h, 1)
        em("vector", fv3)
        vt3 = s_v.bump(1)

        def ft2(t, vt3=vt3):
            t.wait_ge(s_v.h, vt3)
            t.matmul(PB[1][:, 0:8], SB["ones_row"][:], SB["scl"][0:1, 0:8],
                     start=True, stop=True).then_inc(s_t.h, 1)
        em("tensor", ft2)
        tt2 = s_t.bump(1)

        def fv4(v, tt2=tt2):
            v.wait_ge(s_t.h, tt2)
            v.tensor_copy(out=SB["hs"][:, 0:8], in_=PB[1][:, 0:8])
            v.drain()
            v.nop().then_inc(s_g.h, 1)
        em("vector", fv4)
        s_g.bump(1)

    def ph_A():
        hid = F1
        prev_st = 0
        for tau in range(ntau):
            prev_v = s_v.v

            def fl(s, tau=tau, prev_v=prev_v):
                if tau > 0:
                    s.wait_ge(s_v.h, prev_v)
                for cc in range(NCC):
                    s.dma_start(out=hid[:, 513 * cc:513 * (cc + 1)],
                                in_=D["hid_fm"][cc, :, TAU * tau:TAU * tau + 513]).then_inc(s_ds.h, 16)
            em("sync", fl)
            ld_t = s_ds.bump(16 * NCC)

            def fv(v, ld_t=ld_t):
                v.wait_ge(s_ds.h, ld_t)
                for cc in range(NCC):
                    v.tensor_sub(out=B2[:, 512 * cc:512 * (cc + 1)],
                                 in0=hid[:, 513 * cc:513 * cc + 512],
                                 in1=hid[:, 513 * cc + 1:513 * cc + 513])
                v.drain()
                for cc in range(NCC):
                    v.scalar_tensor_tensor(
                        out=B2[:, 8192 + 512 * cc:8192 + 512 * (cc + 1)],
                        in0=B2[:, 512 * cc:512 * (cc + 1)], scalar=vcol("maa_x", cc),
                        in1=hid[:, 513 * cc + 1:513 * cc + 513], op0=ALU.mult, op1=ALU.add)
                v.drain()
                v.nop().then_inc(s_v.h, 1)
            em("vector", fv)
            v_t = s_v.bump(1)

            def ft(t, v_t=v_t):
                t.wait_ge(s_v.h, v_t)
                for cc in range(NCC):
                    t.matmul(PB[0][:, :], SB["w1"][:, cc, 0:128],
                             B2[:, 8192 + 512 * cc:8192 + 512 * (cc + 1)],
                             start=(cc == 0), stop=(cc == 15))
                for cc in range(NCC):
                    ins = t.matmul(PB[1][0:32, :], SB["w1"][:, cc, 128:160],
                                   B2[:, 8192 + 512 * cc:8192 + 512 * (cc + 1)],
                                   start=(cc == 0), stop=(cc == 15))
                ins.then_inc(s_t.h, 1)
            em("tensor", ft)
            t_t = s_t.bump(1)

            def fs(sc, t_t=t_t):
                sc.wait_ge(s_t.h, t_t)
                sc.activation(out=BW[:, 0:512], in_=PB[0][:, :], func=AF.Tanh)
                sc.activation(out=BW[0:32, 512:1024], in_=PB[1][0:32, :],
                              func=AF.Tanh).then_inc(s_sc.h, 1)
            em("scalar", fs)
            sc_t = s_sc.bump(1)

            vmix0 = s_v.v
            for f in range(5):
                for cc in range(NCC):
                    idx = f * NCC + cc
                    bank = PB[2 + (idx % 2)]
                    wait_v = vmix0 + idx - 1 if idx >= 2 else None

                    def ftm(t, f=f, cc=cc, bank=bank, sc_t=sc_t, wait_v=wait_v):
                        if f == 0 and cc == 0:
                            t.wait_ge(s_sc.h, sc_t)
                        if wait_v is not None:
                            t.wait_ge(s_v.h, wait_v)
                        zsl = (BW[32 * f:32 * (f + 1), 0:512] if f < 4
                               else BW[0:32, 512:1024])
                        w2sl = (SB["w2"][32 * f:32 * (f + 1), cc, :] if f < 4
                                else SB["w2g"][:, cc, :])
                        tp = (32 * f, 0) if f < 4 else (0, 0)
                        t.matmul(bank[:, :], w2sl, zsl,
                                 start=True, stop=True,
                                 tile_position=tp).then_inc(s_t.h, 1)
                    em("tensor", ftm)
                    mm_t = s_t.bump(1)

                    def fvm(v, f=f, cc=cc, bank=bank, mm_t=mm_t):
                        v.wait_ge(s_t.h, mm_t)
                        v.scalar_tensor_tensor(
                            out=F3[:, 0:512], in0=bank[:, :],
                            scalar=vcol("maa_" + "wkvrg"[f], cc),
                            in1=B2[:, 512 * cc:512 * (cc + 1)],
                            op0=ALU.add, op1=ALU.mult)
                        v.drain()
                        v.tensor_add(out=B3[:, 512 * cc:512 * (cc + 1)],
                                     in0=F3[:, 0:512],
                                     in1=hid[:, 513 * cc + 1:513 * cc + 513])
                        v.drain()
                        v.nop().then_inc(s_v.h, 1)
                    em("vector", fvm)
                    s_v.bump(1)
                v_done = s_v.v

                def fst(s, f=f, tau=tau, v_done=v_done):
                    s.wait_ge(s_v.h, v_done)
                    s.dma_start(out=spl("xf_sp", TAU * tau, TAU * (tau + 1), f),
                                in_=s3(B3[:, 0:8192])).then_inc(s_ds.h, 16)
                em("sync", fst)
                st_done = s_ds.bump(16)

                if f < 4:
                    def fvw(v, st_done=st_done):
                        v.wait_ge(s_ds.h, st_done)
                    em("vector", fvw)
                else:
                    prev_st = st_done

    def ph_B():
        for wname, fidx, spname, act in [
                ("wr_t", 3, "r_sp", None), ("wk_t", 1, "k_sp", None),
                ("wg_t", 4, "g_sp", "silu")]:
            prev_t = s_t.v

            def flw(g, wname=wname, prev_t=prev_t):
                g.wait_ge(s_t.h, prev_t)
                for cc in range(NCC):
                    g.dma_start(out=BW[:, 2048 * cc:2048 * (cc + 1)],
                                in_=D[wname][128 * cc:128 * (cc + 1), :]).then_inc(s_dg.h, 16)
            em("gpsimd", flw)
            w_t = s_dg.bump(16 * NCC)

            for tau in range(ntau):
                prev_mm = s_t.v

                def flx(s, fidx=fidx, tau=tau, prev_mm=prev_mm):
                    if tau > 0:
                        s.wait_ge(s_t.h, prev_mm)
                    s.dma_start(out=s3(B2[:, 0:8192]),
                                in_=spl("xf_sp", TAU * tau, TAU * (tau + 1), fidx)).then_inc(s_ds.h, 16)
                em("sync", flx)
                x_t = s_ds.bump(16)

                sc0 = s_sc.v
                prev_fso = s_ds.v
                for mt in range(NCC):
                    bank = PB[mt % 4]
                    wait_sc = sc0 + mt - 3 if mt >= 4 else None

                    def ftp(t, mt=mt, bank=bank, w_t=w_t, x_t=x_t, wait_sc=wait_sc):
                        if mt == 0:
                            t.wait_ge(s_dg.h, w_t)
                            t.wait_ge(s_ds.h, x_t)
                        if wait_sc is not None:
                            t.wait_ge(s_sc.h, wait_sc)
                        for cc in range(NCC):
                            ins = t.matmul(bank[:, :],
                                           BW[:, 2048 * cc + 128 * mt:2048 * cc + 128 * (mt + 1)],
                                           B2[:, 512 * cc:512 * (cc + 1)],
                                           start=(cc == 0), stop=(cc == 15))
                        ins.then_inc(s_t.h, 1)
                    em("tensor", ftp)
                    mm_t = s_t.bump(1)

                    def fse(sc, mt=mt, bank=bank, mm_t=mm_t, act=act, x_t=x_t):
                        sc.wait_ge(s_t.h, mm_t)
                        if mt == 0:
                            sc.wait_ge(s_ds.h, x_t)
                        fn = AF.Silu if act == "silu" else AF.Copy
                        sc.activation(out=B2[:, 8192 + 512 * mt:8192 + 512 * (mt + 1)],
                                      in_=bank[:, :], func=fn).then_inc(s_sc.h, 1)
                    em("scalar", fse)
                    s_sc.bump(1)
                e_t = s_sc.v

                def fso(s, spname=spname, tau=tau, e_t=e_t):
                    s.wait_ge(s_sc.h, e_t)
                    s.dma_start(out=spl(spname, TAU * tau, TAU * (tau + 1)),
                                in_=s3(B2[:, 8192:16384])).then_inc(s_ds.h, 16)
                em("sync", fso)
                s_ds.bump(16)

        prev_t = s_t.v

        def flw2(g, prev_t=prev_t):
            g.wait_ge(s_t.h, prev_t)
            for cc in range(NCC):
                g.dma_start(out=BW[:, 2048 * cc:2048 * (cc + 1)],
                            in_=D["wv_t"][128 * cc:128 * (cc + 1), :]).then_inc(s_dg.h, 16)
        em("gpsimd", flw2)
        w_t = s_dg.bump(16 * NCC)

        for tau in range(ntau):
            prev_mm = s_t.v

            def flx(s, tau=tau, prev_mm=prev_mm):
                if tau > 0:
                    s.wait_ge(s_t.h, prev_mm)
                s.dma_start(out=s3(B2[:, 0:8192]),
                            in_=spl("xf_sp", TAU * tau, TAU * (tau + 1), 2)).then_inc(s_ds.h, 16)
            em("sync", flx)
            x_t = s_ds.bump(16)

            v0 = s_v.v
            prev_fso2 = s_ds.v
            for tt in range(4):
                for ac in range(4):
                    idx = tt * 4 + ac
                    bank = PB[idx % 4]
                    wait_v = v0 + idx - 3 if idx >= 4 else None

                    def ftv(t, tt=tt, ac=ac, bank=bank, w_t=w_t, x_t=x_t, wait_v=wait_v):
                        if tt == 0 and ac == 0:
                            t.wait_ge(s_dg.h, w_t)
                            t.wait_ge(s_ds.h, x_t)
                        if wait_v is not None:
                            t.wait_ge(s_v.h, wait_v)
                        for cc in range(NCC):
                            ins = t.matmul(bank[:, :],
                                           B2[:, 512 * cc + 128 * tt:512 * cc + 128 * (tt + 1)],
                                           BW[:, 2048 * cc + 512 * ac:2048 * cc + 512 * (ac + 1)],
                                           start=(cc == 0), stop=(cc == 15))
                        ins.then_inc(s_t.h, 1)
                    em("tensor", ftv)
                    mm_t = s_t.bump(1)

                    def fve(v, tt=tt, ac=ac, bank=bank, mm_t=mm_t, x_t=x_t):
                        v.wait_ge(s_t.h, mm_t)
                        if tt == 0 and ac == 0:
                            v.wait_ge(s_ds.h, x_t)
                        v.tensor_copy(out=B3[:, 2048 * tt + 512 * ac:2048 * tt + 512 * (ac + 1)],
                                      in_=bank[:, :])
                        v.drain()
                        v.nop().then_inc(s_v.h, 1)
                    em("vector", fve)
                    s_v.bump(1)
            v_done = s_v.v

            def fso2(s, tau=tau, v_done=v_done):
                s.wait_ge(s_v.h, v_done)
                for tt in range(4):
                    s.dma_start(out=D["vtm_sp"][4 * tau + tt],
                                in_=B3[:, 2048 * tt:2048 * (tt + 1)]).then_inc(s_ds.h, 16)
                for tt in range(4):
                    s.dma_start(out=D["vtm2_sp"][4 * tau + tt][0:64, :],
                                in_=B3[64:128, 2048 * tt:2048 * (tt + 1)]).then_inc(s_ds.h, 16)
                    s.dma_start(out=D["vtm2_sp"][4 * tau + tt][64:128, :],
                                in_=B3[0:64, 2048 * tt:2048 * (tt + 1)]).then_inc(s_ds.h, 16)
            em("sync", fso2)
            s_ds.bump(16 * 12)

        for tau in range(ntau):
            prev_t = s_t.v
            prev_b3 = s_ds.v
            prev_fsn = s_ds.v

            def flx(s, tau=tau, prev_t=prev_t):
                if tau > 0:
                    s.wait_ge(s_t.h, prev_t)
                s.dma_start(out=s3(B2[:, 0:8192]),
                            in_=spl("xf_sp", TAU * tau, TAU * (tau + 1), 0)).then_inc(s_ds.h, 16)
            em("sync", flx)
            x_t = s_ds.bump(16)

            def ftz(t, x_t=x_t):
                t.wait_ge(s_ds.h, x_t)
                for cc in range(NCC):
                    t.matmul(PB[4][0:64, :], SB["dw1"][:, cc, :],
                             B2[:, 512 * cc:512 * (cc + 1)],
                             start=(cc == 0), stop=(cc == 15))
                for cc in range(NCC):
                    ins = t.matmul(PB[4][64:128, :], SB["dw1"][:, cc, :],
                                   B2[:, 512 * cc:512 * (cc + 1)],
                                   start=(cc == 0), stop=(cc == 15),
                                   tile_position=(0, 64))
                ins.then_inc(s_t.h, 1)
            em("tensor", ftz)
            tz = s_t.bump(1)

            def fsz(sc, tz=tz, x_t=x_t):
                sc.wait_ge(s_t.h, tz)
                sc.wait_ge(s_ds.h, x_t)
                sc.activation(out=B3[:, 0:512], in_=PB[4][:, :],
                              func=AF.Tanh).then_inc(s_sc.h, 1)
            em("scalar", fsz)
            scz = s_sc.bump(1)

            v0 = s_v.v
            for cc in range(NCC):
                bank = PB[5 + (cc % 2)]
                wait_v = v0 + cc - 1 if cc >= 2 else None

                def ftd(t, cc=cc, bank=bank, scz=scz, wait_v=wait_v):
                    if cc == 0:
                        t.wait_ge(s_sc.h, scz)
                    if wait_v is not None:
                        t.wait_ge(s_v.h, wait_v)
                    zrow = 64 * (cc % 2)
                    t.matmul(bank[:, :], SB["dw2"][zrow:zrow + 64, cc // 2, :],
                             B3[zrow:zrow + 64, 0:512],
                             start=True, stop=True, tile_position=(zrow, 0)).then_inc(s_t.h, 1)
                em("tensor", ftd)
                td = s_t.bump(1)

                def fsd(sc, cc=cc, bank=bank, td=td, x_t=x_t):
                    sc.wait_ge(s_t.h, td)
                    if cc == 0:
                        sc.wait_ge(s_ds.h, x_t)
                    sc.activation(out=F1[:, 512 * cc:512 * (cc + 1)], in_=bank[:, :],
                                  func=AF.Exp, bias=vcol("td", cc)).then_inc(s_sc.h, 1)
                em("scalar", fsd)
                sd = s_sc.bump(1)

                def fvd(v, cc=cc, sd=sd):
                    v.wait_ge(s_sc.h, sd)
                    v.tensor_scalar_min(out=F1[:, 512 * cc:512 * (cc + 1)],
                                        in0=F1[:, 512 * cc:512 * (cc + 1)],
                                        scalar1=NU_CLAMP)
                    v.drain()
                    v.nop().then_inc(s_v.h, 1)
                em("vector", fvd)
                s_v.bump(1)
            vnu = s_v.v

            def fsn(s, tau=tau, vnu=vnu):
                s.wait_ge(s_v.h, vnu)
                s.dma_start(out=spl("nu_sp", TAU * tau, TAU * (tau + 1)),
                            in_=s3(F1[:, 0:8192])).then_inc(s_ds.h, 16)
            em("sync", fsn)
            s_ds.bump(16)

    FK8, FKA, FKF2, FKF3, FK32, FR32, FR8, FK0 = range(8)

    def bwsl(slot, cc, a=0, b=TAUC):
        return BW[:, 4096 * slot + 256 * cc + a:4096 * slot + 256 * cc + b]

    def ph_C():
        prev_oso = 0
        prev_sO_last = None
        for tc in range(ntc):
            prev_t = s_t.v
            prev_v = s_v.v

            def fl(s, tc=tc, prev_t=prev_t, prev_v=prev_v):
                if tc > 0:
                    s.wait_ge(s_t.h, prev_t)
                    s.wait_ge(s_v.h, prev_v)
                s.dma_start(out=s3(B2[:, 0:4096]),
                            in_=spl("k_sp", TAUC * tc, TAUC * (tc + 1))).then_inc(s_ds.h, 16)
                s.dma_start(out=s3(B2[:, 4096:8192]),
                            in_=spl("r_sp", TAUC * tc, TAUC * (tc + 1))).then_inc(s_ds.h, 16)
                s.dma_start(out=s3(F1[:, 0:4096]),
                            in_=spl("nu_sp", TAUC * tc, TAUC * (tc + 1))).then_inc(s_ds.h, 16)
                s.dma_start(out=B3[:, 0:2048], in_=D["vtm_sp"][2 * tc]).then_inc(s_ds.h, 16)
                s.dma_start(out=B3[:, 2048:4096], in_=D["vtm_sp"][2 * tc + 1]).then_inc(s_ds.h, 16)
                for b in range(2):
                    s.dma_start(out=B2[:, 8192 + 2048 * b:8192 + 2048 * (b + 1)],
                                in_=D["vtm2_sp"][2 * tc + b]).then_inc(s_ds.h, 16)
            em("sync", fl)
            ld_t = s_ds.bump(16 * 7)

            def scn_(cc):
                return F1[:, 4096 + 256 * cc:4096 + 256 * (cc + 1)]

            def nu_(cc):
                return F1[:, 256 * cc:256 * (cc + 1)]

            def k_(cc):
                return B2[:, 256 * cc:256 * (cc + 1)]

            def r_(cc):
                return B2[:, 4096 + 256 * cc:4096 + 256 * (cc + 1)]

            def scr_(cc):
                return F1[:, 8192 + 256 * cc:8192 + 256 * (cc + 1)]

            def k64_(cc, a=0, b=TAUC):
                return B2[:, 12288 + 256 * cc + a:12288 + 256 * cc + b]

            def sV1(v, ld_t=ld_t):
                v.wait_ge(s_ds.h, ld_t)
                for cc in range(NCC):
                    v.tensor_tensor_scan(out=scn_(cc), data0=SB["m8"][:, 0:TAUC],
                                         data1=nu_(cc), initial=0.0,
                                         op0=ALU.mult, op1=ALU.add)
                v.drain()
                v.nop().then_inc(s_v.h, 1)
            em("vector", sV1)
            s_v.bump(1)

            def sS1(sc, vw=s_v.v):
                sc.wait_ge(s_v.h, vw)
                for cc in range(NCC):
                    sc.activation(out=bwsl(FK8, cc), in_=scn_(cc), func=AF.Exp)
                    sc.activation(out=bwsl(FR8, cc, 1, TAUC), in_=scn_(cc)[:, 0:TAUC - 1],
                                  func=AF.Exp, scale=-1.0)
                    g8 = bass.AP(tensor=F1, offset=4096 + 256 * cc + 7,
                                 ap=[[12288, 128], [8, 32]])
                    ins = sc.activation(out=F3[:, 34 * cc:34 * cc + 32], in_=g8,
                                        func=AF.Exp, scale=-1.0)
                ins.then_inc(s_sc.h, 1)
            em("scalar", sS1)
            s_sc.bump(1)

            def sV23(v, sw=s_sc.v):
                v.wait_ge(s_sc.h, sw)
                for cc in range(NCC):
                    fix = bass.AP(tensor=BW, offset=4096 * FR8 + 256 * cc,
                                  ap=[[32768, 128], [8, 32]])
                    v.memset(fix, 1.0)
                    v.memset(F3[:, 34 * cc + 32:34 * cc + 34], 0.0)
                v.drain()
                for cc in range(NCC):
                    v.tensor_mul(out=bwsl(FK8, cc), in0=bwsl(FK8, cc), in1=k_(cc))
                    v.tensor_mul(out=bwsl(FR8, cc), in0=bwsl(FR8, cc), in1=r_(cc))
                v.drain()
                for cc in range(NCC):
                    e8b = bass.AP(tensor=F3, offset=34 * cc,
                                  ap=[[2048, 128], [1, 32], [0, 8]])
                    v.tensor_tensor(out=bwsl(FKA, cc), in0=bwsl(FK8, cc), in1=e8b,
                                    op=ALU.mult)
                v.drain()
                for cc in range(NCC):
                    e8b1 = bass.AP(tensor=F3, offset=34 * cc + 1,
                                   ap=[[2048, 128], [1, 32], [0, 8]])
                    v.tensor_tensor(out=bwsl(FKF2, cc), in0=bwsl(FKA, cc), in1=e8b1,
                                    op=ALU.mult)
                v.drain()
                for cc in range(NCC):
                    e8b2 = bass.AP(tensor=F3, offset=34 * cc + 2,
                                   ap=[[2048, 128], [1, 32], [0, 8]])
                    v.tensor_tensor(out=bwsl(FKF3, cc), in0=bwsl(FKF2, cc), in1=e8b2,
                                    op=ALU.mult)
                v.drain()
                for cc in range(NCC):
                    v.tensor_tensor_scan(out=scn_(cc), data0=SB["m32"][:, 0:TAUC],
                                         data1=nu_(cc), initial=0.0,
                                         op0=ALU.mult, op1=ALU.add)
                v.drain()
                v.nop().then_inc(s_v.h, 1)
            em("vector", sV23)
            s_v.bump(1)

            def sS2(sc, vw=s_v.v):
                sc.wait_ge(s_v.h, vw)
                for cc in range(NCC):
                    sc.activation(out=bwsl(FR32, cc, 1, TAUC), in_=scn_(cc)[:, 0:TAUC - 1],
                                  func=AF.Exp, scale=-1.0)
                    g32 = bass.AP(tensor=F1, offset=4096 + 256 * cc + 31,
                                  ap=[[12288, 128], [32, 8]])
                    ins = sc.activation(out=F3[:, 544 + 8 * cc:544 + 8 * cc + 8], in_=g32,
                                        func=AF.Copy)
                ins.then_inc(s_sc.h, 1)
            em("scalar", sS2)
            s_sc.bump(1)

            def sV4(v, sw=s_sc.v):
                v.wait_ge(s_sc.h, sw)
                for cc in range(NCC):
                    fix = bass.AP(tensor=BW, offset=4096 * FR32 + 256 * cc,
                                  ap=[[32768, 128], [32, 8]])
                    v.memset(fix, 1.0)
                v.drain()
                for cc in range(NCC):
                    v.tensor_mul(out=bwsl(FR32, cc), in0=bwsl(FR32, cc), in1=r_(cc))
                for cc in range(NCC):
                    b32b = bass.AP(tensor=F3, offset=544 + 8 * cc,
                                   ap=[[2048, 128], [1, 8], [0, 32]])
                    v.tensor_tensor(out=scr_(cc), in0=scn_(cc), in1=b32b,
                                    op=ALU.subtract)
                v.drain()
                v.nop().then_inc(s_v.h, 1)
            em("vector", sV4)
            s_v.bump(1)

            def sS3(sc, vw=s_v.v):
                sc.wait_ge(s_v.h, vw)
                for cc in range(NCC):
                    ins = sc.activation(out=bwsl(FK32, cc), in_=scr_(cc), func=AF.Exp)
                ins.then_inc(s_sc.h, 1)
            em("scalar", sS3)
            s_sc.bump(1)

            def sV5(v, sw=s_sc.v):
                v.wait_ge(s_sc.h, sw)
                for cc in range(NCC):
                    v.tensor_mul(out=bwsl(FK32, cc), in0=bwsl(FK32, cc), in1=k_(cc))
                for cc in range(NCC):
                    v.tensor_tensor_scan(out=scn_(cc), data0=SB["m64"][:, 0:TAUC],
                                         data1=nu_(cc), initial=0.0,
                                         op0=ALU.mult, op1=ALU.add)
                v.drain()
                v.nop().then_inc(s_v.h, 1)
            em("vector", sV5)
            s_v.bump(1)

            def sS4(sc, vw=s_v.v):
                sc.wait_ge(s_v.h, vw)
                for cc in range(NCC):
                    g64 = bass.AP(tensor=F1, offset=4096 + 256 * cc + 63,
                                  ap=[[12288, 128], [64, 4]])
                    sc.activation(out=F3[:, 736 + 4 * cc:736 + 4 * cc + 4], in_=g64,
                                  func=AF.Copy)
                    ins = sc.activation(out=F3[:, 672 + 4 * cc:672 + 4 * cc + 4], in_=g64,
                                        func=AF.Exp, scale=-1.0)
                ins.then_inc(s_sc.h, 1)
            em("scalar", sS4)
            s_sc.bump(1)

            def sV6(v, sw=s_sc.v):
                v.wait_ge(s_sc.h, sw)
                for cc in range(NCC):
                    b64b = bass.AP(tensor=F3, offset=736 + 4 * cc,
                                   ap=[[2048, 128], [1, 4], [0, 64]])
                    v.tensor_tensor(out=scr_(cc), in0=scn_(cc), in1=b64b,
                                    op=ALU.subtract)
                for cc in range(NCC):
                    v.tensor_scalar_mul(out=bwsl(FK0, cc), in0=k_(cc),
                                        scalar1=vcol("u", cc))
                v.drain()
                v.nop().then_inc(s_v.h, 1)
            em("vector", sV6)
            s_v.bump(1)

            def sS5(sc, vw=s_v.v):
                sc.wait_ge(s_v.h, vw)
                for cc in range(NCC):
                    ins = sc.activation(out=k64_(cc), in_=scr_(cc), func=AF.Exp)
                ins.then_inc(s_sc.h, 1)
            em("scalar", sS5)
            s_sc.bump(1)

            def sV7(v, sw=s_sc.v):
                v.wait_ge(s_sc.h, sw)
                for cc in range(NCC):
                    v.tensor_mul(out=k64_(cc), in0=k64_(cc), in1=k_(cc))
                v.drain()
                v.nop().then_inc(s_v.h, 1)
            em("vector", sV7)
            s_v.bump(1)

            def sT1(t, vw=s_v.v):
                t.wait_ge(s_v.h, vw)
                for cc in range(NCC):
                    bank, boff = PB[cc // 4], 128 * (cc % 4)
                    for b in range(2):
                        ins = t.transpose(
                            bank[:, boff + 64 * b:boff + 64 * (b + 1)].bitcast(BF16),
                            B2[:, 12288 + 256 * cc + 128 * b:12288 + 256 * cc + 128 * (b + 1)],
                            SB["ident"][:])
                ins.then_inc(s_t.h, 1)
            em("tensor", sT1)
            tT1 = s_t.bump(1)

            def sV8(v, tT1=tT1):
                v.wait_ge(s_t.h, tT1)
                for cc in range(NCC):
                    bank, boff = PB[cc // 4], 128 * (cc % 4)
                    v.tensor_copy(out=B3[:, 4096 + 256 * cc:4096 + 256 * (cc + 1)],
                                  in_=bank[:, boff:boff + 128].bitcast(BF16))
                v.drain()
                v.nop().then_inc(s_v.h, 1)
            em("vector", sV8)
            s_v.bump(1)

            def sS6(sc, tT1=tT1):
                sc.wait_ge(s_t.h, tT1)
                for cc in range(NCC):
                    ins = sc.activation(out=k64_(cc, 1, TAUC), in_=scn_(cc)[:, 0:TAUC - 1],
                                        func=AF.Exp, scale=-1.0)
                ins.then_inc(s_sc.h, 1)
            em("scalar", sS6)
            s_sc.bump(1)

            def sV9(v, sw=s_sc.v):
                v.wait_ge(s_sc.h, sw)
                for cc in range(NCC):
                    fix = bass.AP(tensor=B2, offset=12288 + 256 * cc,
                                  ap=[[16384, 128], [64, 4]])
                    v.memset(fix, 1.0)
                v.drain()
                for cc in range(NCC):
                    v.tensor_mul(out=k64_(cc), in0=k64_(cc), in1=r_(cc))
                v.drain()
                v.nop().then_inc(s_v.h, 1)
            em("vector", sV9)
            fac_done = s_v.bump(1)

            nit = 2 * nch_tc

            def pcl_(i, po=0, fo=0, p=128, w=512):
                if i % 2 == 0:
                    return SB["Pcl"][po:po + p, fo:fo + w]
                return F3[po:po + p, 1536 + fo // 2:1536 + (fo + w) // 2].bitcast(BF16)

            def mk_ftp(i, wv):
                ch, half = divmod(i, 2)
                t0 = LC * ch

                def ftp(t, t0=t0, half=half, wv=wv):
                    t.wait_ge(s_v.h, wv)
                    for ti, cc in enumerate(range(8 * half, 8 * (half + 1))):
                        for hh in range(2):
                            po, fo = 64 * hh, 64 * ti
                            tp = (64 * hh, 64 * hh)
                            for bank, slot in [(PB[0], FK8), (PB[1], FKA),
                                               (PB[2], FKF2), (PB[3], FKF3),
                                               (PB[4], FK32)]:
                                rslot = FR32 if slot == FK32 else FR8
                                t.matmul(bank[po:po + 64, fo:fo + 64],
                                         bwsl(slot, cc, t0, t0 + LC)[64 * hh:64 * hh + 64, :],
                                         bwsl(rslot, cc, t0, t0 + LC)[64 * hh:64 * hh + 64, :],
                                         start=True, stop=True, tile_position=tp)
                            ins = t.matmul(
                                PB[5][po:po + 64, fo:fo + 64],
                                bwsl(FK0, cc, t0, t0 + LC)[64 * hh:64 * hh + 64, :],
                                B2[64 * hh:64 * hh + 64, 4096 + 256 * cc + t0:4096 + 256 * cc + t0 + LC],
                                start=True, stop=True, tile_position=tp)
                    ins.then_inc(s_t.h, 1)
                return ftp

            def mk_fvc(i, wt):
                def fvc(v, i=i, wt=wt):
                    v.wait_ge(s_t.h, wt)
                    v.tensor_tensor(out=pcl_(i), in0=PB[0][:, :], in1=SB["MA"][:],
                                    op=ALU.mult)
                    v.drain()
                    for mask, bank in [("MB", PB[1]), ("MC", PB[2]), ("MD", PB[3]),
                                       ("ME", PB[4]), ("MF", PB[5])]:
                        v.select(out=pcl_(i), mask=SB[mask][:], on_true=bank[:, :],
                                 on_false=pcl_(i), add_drain=True)
                        v.drain()
                    v.nop().then_inc(s_v.h, 1)
                return fvc

            def mk_fti(i, wv, wsc):
                ch, half = divmod(i, 2)
                t0 = LC * ch

                def fti(t, i=i, t0=t0, half=half, wv=wv, wsc=wsc):
                    t.wait_ge(s_v.h, wv)
                    if wsc is not None:
                        t.wait_ge(s_sc.h, wsc)
                    vb = t0 // 128
                    vrow = 64 * ((t0 // 64) % 2)
                    for ti, cc in enumerate(range(8 * half, 8 * (half + 1))):
                        for hh in range(2):
                            po, fo = 64 * hh, 64 * ti
                            hd = 2 * cc + hh
                            t.matmul(PB[6][po:po + 64, fo:fo + 64],
                                     SB["Sb"][64 * hh:64 * hh + 64, cc, :],
                                     B2[64 * hh:64 * hh + 64, 12288 + 256 * cc + t0:12288 + 256 * cc + t0 + LC],
                                     start=True, stop=False, tile_position=(64 * hh, po))
                            if po == vrow:
                                vsrc = B3[po:po + 64, 2048 * vb + 64 * hd:2048 * vb + 64 * hd + 64]
                            else:
                                vsrc = B2[po:po + 64, 8192 + 2048 * vb + 64 * hd:8192 + 2048 * vb + 64 * hd + 64]
                            t.matmul(PB[6][po:po + 64, fo:fo + 64], vsrc,
                                     pcl_(i, po, fo, 64, 64),
                                     start=False, stop=True, tile_position=(po, po))
                            ins = t.matmul(
                                PB[7][po:po + 64, fo:fo + 64],
                                B3[vrow:vrow + 64, 4096 + 256 * cc + 128 * vb + 64 * hh:4096 + 256 * cc + 128 * vb + 64 * hh + 64],
                                B3[vrow:vrow + 64, 2048 * vb + 64 * hd:2048 * vb + 64 * hd + 64],
                                start=True, stop=True, tile_position=(vrow, po))
                    ins.then_inc(s_t.h, 1)
                return fti

            def mk_fve(i, wt):
                ch, half = divmod(i, 2)

                def fve(v, half=half, ch=ch, wt=wt):
                    v.wait_ge(s_t.h, wt)
                    sap = bass.AP(tensor=SB["S"], offset=64 * 8 * half,
                                  ap=[[1024, 128], [64, 8], [1, 64]])
                    d64 = bass.AP(tensor=F3, offset=672 + 4 * 8 * half + ch,
                                  ap=[[2048, 128], [4, 8], [0, 64]])
                    v.tensor_tensor(out=F3[:, 1024:1536], in0=sap, in1=d64, op=ALU.mult)
                    v.drain()
                    v.tensor_add(out=sap, in0=F3[:, 1024:1536], in1=PB[7][:, :])
                    v.drain()
                    sbap = bass.AP(tensor=SB["Sb"], offset=64 * 8 * half,
                                   ap=[[1024, 128], [64, 8], [1, 64]])
                    v.tensor_copy(out=sbap, in_=sap)
                    v.drain()
                    v.nop().then_inc(s_v.h, 1)
                return fve

            def mk_foe(i, wt):
                ch, half = divmod(i, 2)
                t0 = LC * ch

                def foe(sc, half=half, t0=t0, wt=wt):
                    sc.wait_ge(s_t.h, wt)
                    oap = bass.AP(tensor=F1, offset=8192 + 256 * 8 * half + t0,
                                  ap=[[12288, 128], [256, 8], [1, 64]])
                    sc.activation(out=oap, in_=PB[6][:, :],
                                  func=AF.Copy).then_inc(s_sc.h, 1)
                return foe

            tPl = [None] * nit
            vCl = [None] * nit
            tIl = [None] * nit
            vEl = [None] * nit
            sOl = [None] * nit
            for i in range(nit):
                em("tensor", mk_ftp(i, vCl[i - 1] if i > 0 else fac_done))
                tPl[i] = s_t.bump(1)
                em("vector", mk_fvc(i, tPl[i]))
                vCl[i] = s_v.bump(1)
                if i >= 1:
                    j = i - 1
                    wv = vEl[j - 1] if j >= 1 else vCl[0]
                    wsc = sOl[j - 1] if j >= 1 else prev_sO_last
                    em("tensor", mk_fti(j, wv, wsc))
                    tIl[j] = s_t.bump(1)
                    em("vector", mk_fve(j, tIl[j]))
                    vEl[j] = s_v.bump(1)
                    em("scalar", mk_foe(j, tIl[j]))
                    sOl[j] = s_sc.bump(1)
            j = nit - 1
            em("tensor", mk_fti(j, vEl[j - 1], sOl[j - 1]))
            tIl[j] = s_t.bump(1)
            em("vector", mk_fve(j, tIl[j]))
            vEl[j] = s_v.bump(1)
            em("scalar", mk_foe(j, tIl[j]))
            sOl[j] = s_sc.bump(1)
            prev_sO_last = sOl[j]

            def fso(s, tc=tc, wsc=sOl[nit - 1]):
                s.wait_ge(s_sc.h, wsc)
                s.dma_start(out=spl("o_sp", TAUC * tc, TAUC * (tc + 1)),
                            in_=s3(F1[:, 8192:12288])).then_inc(s_ds.h, 16)
            em("sync", fso)
            prev_oso = s_ds.bump(16)

            def fvg(v):
                for ch in range(nch_tc):
                    d64 = bass.AP(tensor=F3, offset=672 + ch, ap=[[2048, 128], [4, NCC]])
                    v.tensor_tensor(out=SB["Gfin"][:], in0=SB["Gfin"][:], in1=d64,
                                    op=ALU.mult)
                    v.drain()
                v.nop().then_inc(s_v.h, 1)
            em("vector", fvg)
            s_v.bump(1)

    def ph_AG():
        vS = s_v.v
        dg0 = s_dg.v

        def fg(g, vS=vS, dg0=dg0):
            g.wait_ge(s_v.h, vS)
            g.dma_start(out=D["cc_in"][:, 0:1024],
                        in_=SB["S"][:].rearrange("p c m -> p (c m)")).then_inc(s_dg.h, 16)
            g.dma_start(out=D["cc_in"][:, 1024:1040], in_=SB["Gfin"][:]).then_inc(s_dg.h, 16)
            g.wait_ge(s_dg.h, dg0 + 32)
            g.collective_compute(
                "AllGather", ALU.bypass, replica_groups=[list(range(NCORES))],
                ins=[D["cc_in"]], outs=[D["cc_out"]],
            ).then_inc(s_g.h, 1)
        em("gpsimd", fg)
        s_dg.bump(32)
        gT = s_g.bump(1)

        def fl(s, gT=gT):
            s.wait_ge(s_g.h, gT)
            s.dma_start(out=F1[:, 0:8320].rearrange("p (r f) -> p r f", r=NCORES),
                        in_=D["cc_out"].transpose([1, 0, 2])).then_inc(s_ds.h, 16)
        em("sync", fl)
        ld = s_ds.bump(16)

        def fv(v, ld=ld):
            v.wait_ge(s_ds.h, ld)
            v.memset(F1[:, 8320:9344], 0.0)
            v.memset(F3[:, 0:NCC], 1.0)
            v.drain()
            for j in range(NCORES - 1, -1, -1):
                v.tensor_scalar_mul(out=F3[:, 16:32], in0=F3[:, 0:NCC],
                                    scalar1=SB["rankmask"][:, j:j + 1])
                v.drain()
                facb = bass.AP(tensor=F3, offset=16, ap=[[2048, 128], [1, NCC], [0, HS]])
                v.tensor_tensor(out=F3[:, 1024:2048],
                                in0=F1[:, 1040 * j:1040 * j + 1024], in1=facb, op=ALU.mult)
                v.drain()
                v.tensor_add(out=F1[:, 8320:9344], in0=F1[:, 8320:9344],
                             in1=F3[:, 1024:2048])
                v.drain()
                v.tensor_single_scalar(out=F3[:, 32:48],
                                       in_=F1[:, 1040 * j + 1024:1040 * j + 1040],
                                       scalar=1.0, op=ALU.subtract)
                v.drain()
                v.tensor_scalar(out=F3[:, 32:48], in0=F3[:, 32:48],
                                scalar1=SB["rankmask"][:, j:j + 1], scalar2=1.0,
                                op0=ALU.mult, op1=ALU.add)
                v.drain()
                v.tensor_mul(out=F3[:, 0:NCC], in0=F3[:, 0:NCC], in1=F3[:, 32:48])
                v.drain()
            v.tensor_copy(out=SB["Sinb"][:].rearrange("p c m -> p (c m)"),
                          in_=F1[:, 8320:9344])
            v.drain()
            v.memset(F3[:, 2044:2048], 0.0)
            v.drain()
            v.nop().then_inc(s_v.h, 1)
        em("vector", fv)
        vcomb = s_v.bump(1)

        if "ag_dbg" in dbg:
            D["ag_dbg"] = nc.dram_tensor("ag_dbg", [NCORES, 128, 1040], F32,
                                         kind="ExternalOutput").ap()
            D["sin_dbg"] = nc.dram_tensor("sin_dbg", [128, NCC, HS], BF16,
                                          kind="ExternalOutput").ap()

            def fdbg(s, vcomb=vcomb):
                s.wait_ge(s_v.h, vcomb)
                s.dma_start(out=D["ag_dbg"], in_=D["cc_out"]).then_inc(s_ds.h, 16)
                s.dma_start(out=D["sin_dbg"], in_=SB["Sinb"][:]).then_inc(s_ds.h, 16)
            em("sync", fdbg)
            s_ds.bump(32)

    def ph_C2():
        prev_fsg = 0
        for tc in range(ntc):
            prev_t = s_t.v
            prev_v = s_v.v

            def fl(s, tc=tc, prev_t=prev_t, prev_v=prev_v):
                if tc > 0:
                    s.wait_ge(s_t.h, prev_t)
                    s.wait_ge(s_v.h, prev_v)
                s.dma_start(out=s3(F1[:, 8192:12288]),
                            in_=spl("o_sp", TAUC * tc, TAUC * (tc + 1))).then_inc(s_ds.h, 16)
                s.dma_start(out=s3(F1[:, 0:4096]),
                            in_=spl("nu_sp", TAUC * tc, TAUC * (tc + 1))).then_inc(s_ds.h, 16)
                s.dma_start(out=s3(B2[:, 4096:8192]),
                            in_=spl("r_sp", TAUC * tc, TAUC * (tc + 1))).then_inc(s_ds.h, 16)
                s.dma_start(out=s3(B2[:, 0:4096]),
                            in_=spl("g_sp", TAUC * tc, TAUC * (tc + 1))).then_inc(s_ds.h, 16)
            em("sync", fl)
            ld = s_ds.bump(16 * 4)

            def scn2_(cc):
                return F1[:, 4096 + 256 * cc:4096 + 256 * (cc + 1)]

            def nu2_(cc):
                return F1[:, 256 * cc:256 * (cc + 1)]

            def cV1(v, ld=ld):
                v.wait_ge(s_ds.h, ld)
                for cc in range(NCC):
                    v.tensor_tensor_scan(out=scn2_(cc), data0=SB["m64"][:, 0:TAUC],
                                         data1=nu2_(cc), initial=0.0,
                                         op0=ALU.mult, op1=ALU.add)
                v.drain()
                v.nop().then_inc(s_v.h, 1)
            em("vector", cV1)
            s_v.bump(1)

            def cS1(sc, vw=s_v.v):
                sc.wait_ge(s_v.h, vw)
                for cc in range(NCC):
                    sc.activation(out=B3[:, 256 * cc + 1:256 * (cc + 1)],
                                  in_=scn2_(cc)[:, 0:TAUC - 1], func=AF.Exp, scale=-1.0)
                    g64 = bass.AP(tensor=F1, offset=4096 + 256 * cc + 63,
                                  ap=[[12288, 128], [64, 4]])
                    ins = sc.activation(out=F3[:, 672 + 4 * cc:672 + 4 * cc + 4], in_=g64,
                                        func=AF.Exp, scale=-1.0)
                ins.then_inc(s_sc.h, 1)
            em("scalar", cS1)
            s_sc.bump(1)

            def cV2(v, sw=s_sc.v, tc=tc):
                v.wait_ge(s_sc.h, sw)
                for cc in range(NCC):
                    fix = bass.AP(tensor=B3, offset=256 * cc, ap=[[8192, 128], [64, 4]])
                    v.memset(fix, 1.0)
                    if tc == 0:
                        v.memset(F3[:, 1792 + cc:1792 + cc + 1], 1.0)
                v.drain()
                for cc in range(NCC):
                    v.tensor_tensor_scan(out=F3[:, 832 + 4 * cc:832 + 4 * cc + 4],
                                         data0=F3[:, 672 + 4 * cc:672 + 4 * cc + 4],
                                         data1=F3[:, 2044:2048],
                                         initial=F3[:, 1792 + cc:1792 + cc + 1],
                                         op0=ALU.mult, op1=ALU.add)
                v.drain()
                for cc in range(NCC):
                    v.tensor_copy(out=F3[:, 896 + 4 * cc + 1:896 + 4 * cc + 4],
                                  in_=F3[:, 832 + 4 * cc:832 + 4 * cc + 3])
                    v.tensor_copy(out=F3[:, 896 + 4 * cc:896 + 4 * cc + 1],
                                  in_=F3[:, 1792 + cc:1792 + cc + 1])
                v.drain()
                for cc in range(NCC):
                    v.tensor_copy(out=F3[:, 1792 + cc:1792 + cc + 1],
                                  in_=F3[:, 832 + 4 * cc + 3:832 + 4 * cc + 4])
                v.drain()
                for cc in range(NCC):
                    v.tensor_mul(out=B3[:, 256 * cc:256 * (cc + 1)],
                                 in0=B3[:, 256 * cc:256 * (cc + 1)],
                                 in1=B2[:, 4096 + 256 * cc:4096 + 256 * (cc + 1)])
                v.drain()
                for cc in range(NCC):
                    gpb = bass.AP(tensor=F3, offset=896 + 4 * cc,
                                  ap=[[2048, 128], [1, 4], [0, 64]])
                    v.tensor_tensor(out=B3[:, 256 * cc:256 * (cc + 1)],
                                    in0=B3[:, 256 * cc:256 * (cc + 1)], in1=gpb,
                                    op=ALU.mult)
                v.drain()
                v.nop().then_inc(s_v.h, 1)
            em("vector", cV2)
            vR = s_v.bump(1)

            def cT1(t, vw=vR):
                t.wait_ge(s_v.h, vw)
                for cc in range(NCC):
                    bank, off = PB[cc // 2], 256 * (cc % 2)
                    for hh in range(2):
                        ins = t.matmul(bank[64 * hh:64 * hh + 64, off:off + 256],
                                       SB["Sinb"][64 * hh:64 * hh + 64, cc, :],
                                       B3[64 * hh:64 * hh + 64, 256 * cc:256 * (cc + 1)],
                                       start=True, stop=True,
                                       tile_position=(64 * hh, 64 * hh))
                ins.then_inc(s_t.h, 1)
            em("tensor", cT1)
            tC = s_t.bump(1)

            def cVc(v, tC=tC):
                v.wait_ge(s_t.h, tC)
                for cc in range(NCC):
                    off = 256 * (cc % 2)
                    v.tensor_add(out=F1[:, 8192 + 256 * cc:8192 + 256 * (cc + 1)],
                                 in0=F1[:, 8192 + 256 * cc:8192 + 256 * (cc + 1)],
                                 in1=PB[cc // 2][:, off:off + 256])
                v.drain()
                v.nop().then_inc(s_v.h, 1)
            em("vector", cVc)
            vO = s_v.bump(1)

            def fsq(sc, vO=vO):
                sc.wait_ge(s_v.h, vO)
                for cc in range(NCC):
                    sc.activation(out=B2[:, 8192 + 256 * cc:8192 + 256 * (cc + 1)],
                                  in_=F1[:, 8192 + 256 * cc:8192 + 256 * (cc + 1)],
                                  func=AF.Square)
                for cc in range(NCC):
                    ins = sc.activation(out=B2[:, 4096 + 256 * cc:4096 + 256 * (cc + 1)],
                                        in_=F1[:, 8192 + 256 * cc:8192 + 256 * (cc + 1)],
                                        func=AF.Copy)
                ins.then_inc(s_sc.h, 1)
            em("scalar", fsq)
            sq = s_sc.bump(1)

            def fts(t, sq=sq):
                t.wait_ge(s_sc.h, sq)
                for cc in range(NCC):
                    t.matmul(PB[2][0:32, 0:256], SB["gn_ones_bf"][:, cc, :],
                             B2[:, 4096 + 256 * cc:4096 + 256 * (cc + 1)],
                             start=(cc == 0), stop=(cc == 15))
                for cc in range(NCC):
                    ins = t.matmul(PB[3][0:32, 0:256], SB["gn_ones_bf"][:, cc, :],
                                   B2[:, 8192 + 256 * cc:8192 + 256 * (cc + 1)],
                                   start=(cc == 0), stop=(cc == 15))
                ins.then_inc(s_t.h, 1)
            em("tensor", fts)
            tS = s_t.bump(1)

            def fvs(v, tS=tS):
                v.wait_ge(s_t.h, tS)
                v.tensor_scalar_mul(out=F3[0:32, 1024:1280], in0=PB[2][0:32, 0:256],
                                    scalar1=1.0 / HS)
                v.drain()
                v.tensor_scalar_mul(out=F3[0:32, 1280:1536], in0=PB[3][0:32, 0:256],
                                    scalar1=1.0 / HS)
                v.drain()
                v.tensor_mul(out=F3[0:32, 1536:1792], in0=F3[0:32, 1024:1280],
                             in1=F3[0:32, 1024:1280])
                v.drain()
                v.tensor_sub(out=F3[0:32, 1280:1536], in0=F3[0:32, 1280:1536],
                             in1=F3[0:32, 1536:1792])
                v.drain()
                v.tensor_copy(out=SB["Pcl"][0:32, 0:256], in_=F3[0:32, 1024:1280])
                v.drain()
                v.nop().then_inc(s_v.h, 1)
            em("vector", fvs)
            vs = s_v.bump(1)

            def fss(sc, vs=vs):
                sc.wait_ge(s_v.h, vs)
                sc.activation(out=F3[0:32, 1536:1792], in_=F3[0:32, 1280:1536],
                              func=AF.Ln, bias=SB["eps"][0:32, 0:1])
                sc.drain()
                sc.activation(out=SB["Pcl"][0:32, 256:512], in_=F3[0:32, 1536:1792],
                              func=AF.Exp, scale=-0.5).then_inc(s_sc.h, 1)
            em("scalar", fss)
            ss = s_sc.bump(1)

            for half in range(2):
                def gT(t, half=half, ss=ss, pv=s_v.v):
                    if half == 0:
                        t.wait_ge(s_sc.h, ss)
                    else:
                        t.wait_ge(s_v.h, pv)
                    for i in range(8):
                        cc = 8 * half + i
                        gsl = SB["gn_sel"][:, cc, :]
                        off = 256 * (i % 2)
                        t.matmul(PB[i // 2][:, off:off + 256], gsl,
                                 SB["Pcl"][0:32, 0:256], start=True, stop=True)
                        ins = t.matmul(PB[4 + i // 2][:, off:off + 256], gsl,
                                       SB["Pcl"][0:32, 256:512],
                                       start=True, stop=True)
                    ins.then_inc(s_t.h, 1)
                em("tensor", gT)
                tb = s_t.bump(1)

                def gV(v, half=half, tb=tb, prev_fsg=prev_fsg):
                    v.wait_ge(s_t.h, tb)
                    if half == 0 and prev_fsg > 0:
                        v.wait_ge(s_dx.h, prev_fsg)
                    for i in range(8):
                        cc = 8 * half + i
                        off = 256 * (i % 2)
                        v.tensor_sub(out=F1[:, 4096 + 256 * cc:4096 + 256 * (cc + 1)],
                                     in0=F1[:, 8192 + 256 * cc:8192 + 256 * (cc + 1)],
                                     in1=PB[i // 2][:, off:off + 256])
                    v.drain()
                    for i in range(8):
                        cc = 8 * half + i
                        off = 256 * (i % 2)
                        v.scalar_tensor_tensor(out=F1[:, 4096 + 256 * cc:4096 + 256 * (cc + 1)],
                                               in0=F1[:, 4096 + 256 * cc:4096 + 256 * (cc + 1)],
                                               scalar=vcol("ln_g", cc),
                                               in1=PB[4 + i // 2][:, off:off + 256],
                                               op0=ALU.mult, op1=ALU.mult)
                    v.drain()
                    for i in range(8):
                        cc = 8 * half + i
                        v.scalar_tensor_tensor(out=B3[:, 4096 + 256 * cc:4096 + 256 * (cc + 1)],
                                               in0=F1[:, 4096 + 256 * cc:4096 + 256 * (cc + 1)],
                                               scalar=vcol("ln_b", cc),
                                               in1=B2[:, 256 * cc:256 * (cc + 1)],
                                               op0=ALU.add, op1=ALU.mult)
                    v.drain()
                    v.nop().then_inc(s_v.h, 1)
                em("vector", gV)
                s_v.bump(1)
            vgn = s_v.v

            def fsg(s, tc=tc, vgn=vgn):
                s.wait_ge(s_v.h, vgn)
                s.dma_start(out=spl("ogn_sp", TAUC * tc, TAUC * (tc + 1)),
                            in_=s3(B3[:, 4096:8192])).then_inc(s_dx.h, 16)
            em("sync", fsg)
            prev_fsg = s_dx.bump(16)

    def ph_D():
        def flw(g):
            for cc in range(NCC):
                g.dma_start(out=BW[:, 2048 * cc:2048 * (cc + 1)],
                            in_=D["wo_t"][128 * cc:128 * (cc + 1), :]).then_inc(s_dg.h, 16)
        em("gpsimd", flw)
        w_t = s_dg.bump(16 * NCC)

        wd_idx = [0]
        qsl = [None] * 128
        spl_l = [None] * 128
        wd_lw = [None] * 128

        def emit_wd(n):
            for _ in range(n):
                idx = wd_idx[0]
                if idx >= 128:
                    return
                wd_idx[0] += 1
                sl, hf = divmod(idx, 2)
                ldsem = s_dg2 if idx % 2 == 0 else s_dg3

                def flwq(g, sl=sl, hf=hf, idx=idx, ldsem=ldsem,
                         pq=(qsl[idx - 2] if idx >= 2 else None)):
                    if pq is not None:
                        g.wait_ge(s_sc.h, pq)
                    g.dma_start(out=F3[:, 1024 * (idx % 2):1024 * (idx % 2) + 1024],
                                in_=D["wd_t"][128 * sl:128 * (sl + 1), 1024 * hf:1024 * (hf + 1)]).then_inc(ldsem.h, 16)
                em("gpsimd", flwq)
                wd_lw[idx] = ldsem.bump(16)

                def fq(sc, idx=idx, lw=wd_lw[idx], ldsem=ldsem,
                       pspill=(spl_l[idx - 2] if idx >= 2 else None)):
                    sc.wait_ge(ldsem.h, lw)
                    if pspill is not None:
                        sc.wait_ge(s_ds.h if idx % 2 == 0 else s_dx.h, pspill)
                    fsl = F3[:, 1024 * (idx % 2):1024 * (idx % 2) + 1024]
                    sc.sign(out=B3[:, 1024 * (idx % 2):1024 * (idx % 2) + 1024],
                            in_=fsl, bias=SB["hs"][:, 6:7])
                    sc.sign(out=B3[:, 2048 + 1024 * (idx % 2):2048 + 1024 * (idx % 2) + 1024],
                            in_=fsl, bias=SB["hs"][:, 2:3]).then_inc(s_sc.h, 1)
                em("scalar", fq)
                qsl[idx] = s_sc.bump(1)

                def fqv(v, idx=idx, qs=qsl[idx]):
                    v.wait_ge(s_sc.h, qs)
                    v.tensor_add(out=B3[:, 1024 * (idx % 2):1024 * (idx % 2) + 1024],
                                 in0=B3[:, 1024 * (idx % 2):1024 * (idx % 2) + 1024],
                                 in1=B3[:, 2048 + 1024 * (idx % 2):2048 + 1024 * (idx % 2) + 1024])
                    v.drain()
                    v.nop().then_inc(s_v.h, 1)
                em("vector", fqv)
                qv = s_v.bump(1)

                def fsoq(s, sl=sl, hf=hf, idx=idx, qv=qv):
                    s.wait_ge(s_v.h, qv)
                    sem = s_ds.h if idx % 2 == 0 else s_dx.h
                    for d in range(8):
                        dt = 8 * hf + d
                        s.dma_start(out=D["wdq_sp"][dt, :, sl, :],
                                    in_=B3[:, 1024 * (idx % 2) + 128 * d:1024 * (idx % 2) + 128 * (d + 1)]).then_inc(sem, 16)
                em("sync", fsoq)
                spl_l[idx] = (s_ds if idx % 2 == 0 else s_dx).bump(16 * 8)

        vrl = [None] * (ntau * NCC)
        last_mm = None
        for tau in range(ntau):
            prev_v = s_v.v
            prev_st = s_dh.v

            def flx(s, tau=tau, prev_v=prev_v, prev_st=prev_st):
                if tau > 0:
                    s.wait_ge(s_v.h, prev_v)
                    s.wait_ge(s_dh.h, prev_st)
                s.dma_start(out=s3(B2[:, 0:8192]),
                            in_=spl("ogn_sp", TAU * tau, TAU * (tau + 1))).then_inc(s_dh.h, 16)
                for cc in range(NCC):
                    s.dma_start(out=F1[:, 512 * cc:512 * (cc + 1)],
                                in_=D["hid_fm"][cc, :, 1 + TAU * tau:1 + TAU * (tau + 1)]).then_inc(s_dh.h, 16)
            em("sync", flx)
            x_t = s_dh.bump(16 * 17)

            for mt in range(NCC):
                gidx = NCC * tau + mt
                bank = PB[gidx % 4]

                def ftp(t, mt=mt, bank=bank, w_t=w_t, x_t=x_t,
                        wait_v=(vrl[gidx - 4] if gidx >= 4 else None),
                        first=(gidx == 0)):
                    if first:
                        t.wait_ge(s_dg.h, w_t)
                    if mt == 0:
                        t.wait_ge(s_dh.h, x_t)
                    if wait_v is not None:
                        t.wait_ge(s_v.h, wait_v)
                    for cc in range(NCC):
                        ins = t.matmul(bank[:, :],
                                       BW[:, 2048 * cc + 128 * mt:2048 * cc + 128 * (mt + 1)],
                                       B2[:, 512 * cc:512 * (cc + 1)],
                                       start=(cc == 0), stop=(cc == 15))
                    ins.then_inc(s_t.h, 1)
                em("tensor", ftp)
                mm_t = s_t.bump(1)
                last_mm = mm_t

                def fvr(v, mt=mt, bank=bank, mm_t=mm_t):
                    v.wait_ge(s_t.h, mm_t)
                    v.tensor_add(out=F1[:, 512 * mt:512 * (mt + 1)],
                                 in0=F1[:, 512 * mt:512 * (mt + 1)], in1=bank[:, :])
                    v.drain()
                    v.nop().then_inc(s_v.h, 1)
                em("vector", fvr)
                vrl[gidx] = s_v.bump(1)
                emit_wd(2)
            vr = s_v.v

            def fsq(sc, vr=vr):
                sc.wait_ge(s_v.h, vr)
                for cc in range(NCC):
                    ins = sc.activation(out=B2[:, 8192 + 512 * cc:8192 + 512 * (cc + 1)],
                                        in_=F1[:, 512 * cc:512 * (cc + 1)], func=AF.Square)
                ins.then_inc(s_sc.h, 1)
            em("scalar", fsq)
            sq = s_sc.bump(1)

            def fts(t, sq=sq):
                t.wait_ge(s_sc.h, sq)
                for cc in range(NCC):
                    t.matmul(PB[4][0:1, :], SB["ones_col"][:], F1[:, 512 * cc:512 * (cc + 1)],
                             start=(cc == 0), stop=(cc == 15))
                for cc in range(NCC):
                    ins = t.matmul(PB[5][0:1, :], SB["ones_col_bf"][:],
                                   B2[:, 8192 + 512 * cc:8192 + 512 * (cc + 1)],
                                   start=(cc == 0), stop=(cc == 15))
                ins.then_inc(s_t.h, 1)
            em("tensor", fts)
            tS = s_t.bump(1)

            def fvs(v, tS=tS):
                v.wait_ge(s_t.h, tS)
                v.tensor_scalar_mul(out=F1[0:1, 8192:8704], in0=PB[4][0:1, :],
                                    scalar1=1.0 / C)
                v.drain()
                v.tensor_scalar_mul(out=F1[0:1, 8704:9216], in0=PB[5][0:1, :],
                                    scalar1=1.0 / C)
                v.drain()
                v.tensor_mul(out=F1[0:1, 9216:9728], in0=F1[0:1, 8192:8704],
                             in1=F1[0:1, 8192:8704])
                v.drain()
                v.tensor_sub(out=F1[0:1, 8704:9216], in0=F1[0:1, 8704:9216],
                             in1=F1[0:1, 9216:9728])
                v.drain()
                v.nop().then_inc(s_v.h, 1)
            em("vector", fvs)
            vs = s_v.bump(1)

            def fss(sc, vs=vs):
                sc.wait_ge(s_v.h, vs)
                sc.activation(out=F1[0:1, 9216:9728], in_=F1[0:1, 8704:9216],
                              func=AF.Ln, bias=SB["eps"][0:1, 1:2])
                sc.drain()
                sc.activation(out=F1[0:1, 8704:9216], in_=F1[0:1, 9216:9728],
                              func=AF.Exp, scale=-0.5).then_inc(s_sc.h, 1)
            em("scalar", fss)
            ss = s_sc.bump(1)

            def ftb(t, ss=ss):
                t.wait_ge(s_sc.h, ss)
                t.matmul(PB[6][:, :], SB["ones_row"][:], F1[0:1, 8192:8704],
                         start=True, stop=True)
                t.matmul(PB[7][:, :], SB["ones_row"][:], F1[0:1, 8704:9216],
                         start=True, stop=True).then_inc(s_t.h, 1)
            em("tensor", ftb)
            tb = s_t.bump(1)

            for cc in range(NCC):
                def fva(v, cc=cc, tb=tb):
                    if cc == 0:
                        v.wait_ge(s_t.h, tb)
                    v.tensor_sub(out=F1[:, 512 * cc:512 * (cc + 1)],
                                 in0=F1[:, 512 * cc:512 * (cc + 1)], in1=PB[6][:, :])
                    v.drain()
                    v.tensor_mul(out=F1[:, 512 * cc:512 * (cc + 1)],
                                 in0=F1[:, 512 * cc:512 * (cc + 1)], in1=PB[7][:, :])
                    v.drain()
                    v.tensor_copy(out=B2[:, 512 * cc:512 * (cc + 1)],
                                  in_=F1[:, 512 * cc:512 * (cc + 1)])
                    v.drain()
                    v.nop().then_inc(s_v.h, 1)
                em("vector", fva)
                s_v.bump(1)
            va = s_v.v

            def fso(s, tau=tau, va=va):
                s.wait_ge(s_v.h, va)
                s.dma_start(out=spl("x1f_sp", TAU * tau, TAU * (tau + 1)),
                            in_=s3(F1[:, 0:8192])).then_inc(s_dh.h, 16)
                s.dma_start(out=spl("x1b_sp", TAU * tau, TAU * (tau + 1)),
                            in_=s3(B2[:, 0:8192])).then_inc(s_dh.h, 16)
            em("sync", fso)
            s_dh.bump(32)

        emit_wd(128)
        ph_D._mm_end = last_mm
        ph_D._sp_end = s_dh.v
        ph_D._wd_end = (s_ds.v, s_dx.v)

    def ph_E():
        def flx(s):
            s.wait_ge(s_t.h, ph_D._mm_end)
            s.wait_ge(s_dh.h, ph_D._sp_end)
            ins = s.dma_start(
                out=BW[:, 0:NCC * t_core].rearrange("p (c t) -> p c t", c=NCC),
                in_=D["x1b_sp"].transpose([1, 0, 2]))
            ins.then_inc(s_dh.h, 16)
        em("sync", flx)
        x_t = s_dh.bump(16)

        hs_ready = s_g.v

        prev_eg_mm = 0
        for eg in range(16):
            e0 = 512 * eg
            for wi, wname in [(0, "wu_t"), (1, "wgate_t")]:
                prev_sc = s_sc.v
                prev_fqv = s_v.v

                def flw(s, wname=wname, e0=e0, prev_sc=prev_sc, first=(eg == 0 and wi == 0)):
                    if first:
                        s.wait_ge(s_dh.h, ph_D._sp_end)
                        s.wait_ge(s_ds.h, ph_D._wd_end[0])
                        s.wait_ge(s_dx.h, ph_D._wd_end[1])
                    else:
                        s.wait_ge(s_sc.h, prev_sc)
                    for cc in range(NCC):
                        s.dma_start(out=F1[:, 512 * cc:512 * (cc + 1)],
                                    in_=D[wname][128 * cc:128 * (cc + 1), e0:e0 + 512]).then_inc(s_ds.h, 16)
                em("sync", flw)
                lw = s_ds.bump(16 * NCC)

                scr = F1[:, 8192:12288].bitcast(BF16)

                def fq(sc, wi=wi, lw=lw, scr=scr, hs_ready=hs_ready,
                       prev_fqv=prev_fqv, prev_eg_mm=prev_eg_mm,
                       first=(eg == 0 and wi == 0)):
                    sc.wait_ge(s_ds.h, lw)
                    if first:
                        sc.wait_ge(s_g.h, hs_ready)
                    if prev_fqv > 0:
                        sc.wait_ge(s_v.h, prev_fqv)
                    if prev_eg_mm > 0:
                        sc.wait_ge(s_t.h, prev_eg_mm)
                    sc.sign(out=scr, in_=F1[:, 0:8192],
                            bias=SB["hs"][:, 4 + wi:5 + wi])
                    sc.drain()
                    sc.sign(out=B2[:, wi * 8192:wi * 8192 + 8192], in_=F1[:, 0:8192],
                            bias=SB["hs"][:, wi:wi + 1]).then_inc(s_sc.h, 1)
                em("scalar", fq)
                qs = s_sc.bump(1)

                def fqv(v, wi=wi, qs=qs, scr=scr):
                    v.wait_ge(s_sc.h, qs)
                    v.tensor_add(out=B2[:, wi * 8192:wi * 8192 + 8192],
                                 in0=B2[:, wi * 8192:wi * 8192 + 8192], in1=scr)
                    v.drain()
                    v.nop().then_inc(s_v.h, 1)
                em("vector", fqv)
                s_v.bump(1)
            q_done = s_v.v

            sc0 = s_sc.v
            vE0 = s_v.v
            prev_fsh = getattr(ph_E, "_pf", 0)
            nslot = 0
            for et in range(4):
                for tau in range(ntau):
                    upb = PB[(nslot % 2) * 2]
                    gtb = PB[(nslot % 2) * 2 + 1]
                    wait_sc = sc0 + (nslot - 1) if nslot >= 2 else None
                    wait_ve = vE0 + (nslot - 1) if nslot >= 2 else None

                    def ftm(t, et=et, tau=tau, upb=upb, gtb=gtb, q_done=q_done,
                            x_t=x_t, wait_sc=wait_sc, wait_ve=wait_ve,
                            first=(nslot == 0)):
                        if first:
                            t.wait_ge(s_v.h, q_done)
                            t.wait_ge(s_dh.h, x_t)
                        if wait_sc is not None:
                            t.wait_ge(s_sc.h, wait_sc)
                        if wait_ve is not None:
                            t.wait_ge(s_v.h, wait_ve)
                        for cc in range(NCC):
                            t.matmul(upb[:, :],
                                     B2[:, 512 * cc + 128 * et:512 * cc + 128 * (et + 1)],
                                     BW[:, t_core * cc + 512 * tau:t_core * cc + 512 * (tau + 1)],
                                     start=(cc == 0), stop=(cc == 15))
                        for cc in range(NCC):
                            ins = t.matmul(gtb[:, :],
                                           B2[:, 8192 + 512 * cc + 128 * et:8192 + 512 * cc + 128 * (et + 1)],
                                           BW[:, t_core * cc + 512 * tau:t_core * cc + 512 * (tau + 1)],
                                           start=(cc == 0), stop=(cc == 15))
                        ins.then_inc(s_t.h, 1)
                    em("tensor", ftm)
                    mm_t = s_t.bump(1)

                    def fup(sc, upb=upb, mm_t=mm_t,
                            wv=(vE0 + nslot if nslot >= 1 else None)):
                        sc.wait_ge(s_t.h, mm_t)
                        if wv is not None:
                            sc.wait_ge(s_v.h, wv)
                        sc.activation(out=SB["Pcl"][:], in_=upb[:, :], func=AF.Silu,
                                      scale=SB["hs"][:, 0:1]).then_inc(s_sc.h, 1)
                    em("scalar", fup)
                    up_t = s_sc.bump(1)

                    def fh(v, et=et, tau=tau, gtb=gtb, up_t=up_t,
                           first=(eg == 0 and nslot == 0)):
                        v.wait_ge(s_sc.h, up_t)
                        if first:
                            v.wait_ge(s_ds.h, ph_D._wd_end[0])
                            v.wait_ge(s_dx.h, ph_D._wd_end[1])
                        v.scalar_tensor_tensor(
                            out=B3[:, 512 * ntau * et + 512 * tau:512 * ntau * et + 512 * (tau + 1)],
                            in0=gtb[:, :], scalar=SB["hs"][:, 1:2], in1=SB["Pcl"][:],
                            op0=ALU.mult, op1=ALU.mult)
                        v.drain()
                        v.nop().then_inc(s_v.h, 1)
                    em("vector", fh)
                    s_v.bump(1)
                    nslot += 1
            h_done = s_v.v
            prev_eg_mm = s_t.v

            def fsh(s, eg=eg, h_done=h_done):
                s.wait_ge(s_v.h, h_done)
                for et in range(4):
                    s.dma_start(out=D["h_sp"][4 * eg + et],
                                in_=B3[:, 512 * ntau * et:512 * ntau * (et + 1)]).then_inc(s_ds.h, 16)
            em("sync", fsh)
            ph_E._pf = s_ds.bump(16 * 4)

        h_stores = s_ds.v
        ph_E._mm_end = s_t.v
        ph_E._h_sp_end = h_stores

    def ph_F():
        mm_end = ph_E._mm_end
        h_sp_end = ph_E._h_sp_end
        wd_end = ph_D._wd_end
        prev_out = 0
        NG = ntau * NCC
        lwg = [None] * NG
        mmg = [None] * NG
        vrg = [None] * NG
        sqg = [None] * NG
        for tau in range(ntau):
            prev_t = s_t.v

            def flh(s, tau=tau, prev_t=prev_t):
                if tau > 0:
                    s.wait_ge(s_t.h, prev_t)
                else:
                    s.wait_ge(s_t.h, mm_end)
                    s.wait_ge(s_ds.h, h_sp_end)
                for q in range(4):
                    s.dma_start(
                        out=BW[:, 8192 * q:8192 * (q + 1)].rearrange("p (e t) -> p e t", e=16),
                        in_=D["h_sp"][16 * q:16 * (q + 1), :, TAU * tau:TAU * (tau + 1)].transpose([1, 0, 2])).then_inc(s_dh.h, 16)
            em("sync", flh)
            h_t = s_dh.bump(16 * 4)

            for dt in range(NCC):
                g = NCC * tau + dt

                def flw(s, tau=tau, dt=dt, g=g,
                        wt=(mmg[g - 2] if g >= 2 else mm_end),
                        wv=(vrg[g - 2] if g >= 2 else None),
                        first=(g < 2)):
                    s.wait_ge(s_t.h, wt)
                    if wv is not None:
                        s.wait_ge(s_v.h, wv)
                    if first:
                        s.wait_ge(s_ds.h, wd_end[0])
                        s.wait_ge(s_dx.h, wd_end[1])
                    sem = s_ds.h if g % 2 == 0 else s_dx.h
                    s.dma_start(out=B2[:, 8192 * (dt % 2):8192 * (dt % 2) + 8192],
                                in_=D["wdq_sp"][dt].rearrange("p ec d -> p (ec d)")).then_inc(sem, 16)
                    s.dma_start(out=F3[:, 512 * (dt % 2):512 * (dt % 2) + 512],
                                in_=D["x1f_sp"][dt, :, TAU * tau:TAU * (tau + 1)]).then_inc(sem, 16)
                em("sync", flw)
                lwg[g] = (s_ds if g % 2 == 0 else s_dx).bump(32)

                def ftm(t, dt=dt, g=g, lw=lwg[g], h_t=h_t,
                        wv=(vrg[g - 2] if g >= 2 else None), hwait=(dt < 2)):
                    t.wait_ge(s_ds.h if g % 2 == 0 else s_dx.h, lw)
                    if hwait:
                        t.wait_ge(s_dh.h, h_t)
                    if wv is not None:
                        t.wait_ge(s_v.h, wv)
                    bank = PB[dt % 2]
                    for ec in range(NET):
                        ins = t.matmul(bank[:, :],
                                       B2[:, 8192 * (dt % 2) + 128 * ec:8192 * (dt % 2) + 128 * (ec + 1)],
                                       BW[:, 512 * ec:512 * (ec + 1)],
                                       start=(ec == 0), stop=(ec == NET - 1))
                    ins.then_inc(s_t.h, 1)
                em("tensor", ftm)
                mmg[g] = s_t.bump(1)

                def fvr(v, dt=dt, g=g, mm_t=mmg[g], prev_out=prev_out,
                        first=(dt == 0)):
                    v.wait_ge(s_t.h, mm_t)
                    if first and prev_out > 0:
                        v.wait_ge(s_do.h, prev_out)
                    v.scalar_tensor_tensor(out=F1[:, 512 * dt:512 * (dt + 1)],
                                           in0=PB[dt % 2][:, :], scalar=SB["hs"][:, 2:3],
                                           in1=F3[:, 512 * (dt % 2):512 * (dt % 2) + 512],
                                           op0=ALU.mult, op1=ALU.add)
                    v.drain()
                    v.nop().then_inc(s_v.h, 1)
                em("vector", fvr)
                vrg[g] = s_v.bump(1)

                def fsq(sc, dt=dt, g=g, vr=vrg[g]):
                    sc.wait_ge(s_v.h, vr)
                    sc.activation(out=B3[:, 512 * dt:512 * (dt + 1)],
                                  in_=F1[:, 512 * dt:512 * (dt + 1)],
                                  func=AF.Square).then_inc(s_sc.h, 1)
                em("scalar", fsq)
                sqg[g] = s_sc.bump(1)

            def fts(t, sq=sqg[NCC * tau + NCC - 1]):
                t.wait_ge(s_sc.h, sq)
                for cc in range(NCC):
                    t.matmul(PB[4][0:1, :], SB["ones_col"][:], F1[:, 512 * cc:512 * (cc + 1)],
                             start=(cc == 0), stop=(cc == 15))
                for cc in range(NCC):
                    ins = t.matmul(PB[5][0:1, :], SB["ones_col_bf"][:],
                                   B3[:, 512 * cc:512 * (cc + 1)],
                                   start=(cc == 0), stop=(cc == 15))
                ins.then_inc(s_t.h, 1)
            em("tensor", fts)
            tS = s_t.bump(1)

            def fvs(v, tS=tS):
                v.wait_ge(s_t.h, tS)
                v.tensor_scalar_mul(out=F1[0:1, 8192:8704], in0=PB[4][0:1, :],
                                    scalar1=1.0 / C)
                v.drain()
                v.tensor_scalar_mul(out=F1[0:1, 8704:9216], in0=PB[5][0:1, :],
                                    scalar1=1.0 / C)
                v.drain()
                v.tensor_mul(out=F1[0:1, 9216:9728], in0=F1[0:1, 8192:8704],
                             in1=F1[0:1, 8192:8704])
                v.drain()
                v.tensor_sub(out=F1[0:1, 8704:9216], in0=F1[0:1, 8704:9216],
                             in1=F1[0:1, 9216:9728])
                v.drain()
                v.nop().then_inc(s_v.h, 1)
            em("vector", fvs)
            vs = s_v.bump(1)

            def fss(sc, vs=vs):
                sc.wait_ge(s_v.h, vs)
                sc.activation(out=F1[0:1, 9216:9728], in_=F1[0:1, 8704:9216],
                              func=AF.Ln, bias=SB["eps"][0:1, 1:2])
                sc.drain()
                sc.activation(out=F1[0:1, 8704:9216], in_=F1[0:1, 9216:9728],
                              func=AF.Exp, scale=-0.5).then_inc(s_sc.h, 1)
            em("scalar", fss)
            ss = s_sc.bump(1)

            def ftb(t, ss=ss):
                t.wait_ge(s_sc.h, ss)
                t.matmul(PB[6][:, :], SB["ones_row"][:], F1[0:1, 8192:8704],
                         start=True, stop=True)
                t.matmul(PB[7][:, :], SB["ones_row"][:], F1[0:1, 8704:9216],
                         start=True, stop=True).then_inc(s_t.h, 1)
            em("tensor", ftb)
            tb = s_t.bump(1)

            def fva(v, tb=tb):
                v.wait_ge(s_t.h, tb)
                for cc in range(NCC):
                    v.tensor_sub(out=F1[:, 512 * cc:512 * (cc + 1)],
                                 in0=F1[:, 512 * cc:512 * (cc + 1)], in1=PB[6][:, :])
                v.drain()
                for cc in range(NCC):
                    v.tensor_mul(out=F1[:, 512 * cc:512 * (cc + 1)],
                                 in0=F1[:, 512 * cc:512 * (cc + 1)], in1=PB[7][:, :])
                v.drain()
                v.nop().then_inc(s_v.h, 1)
            em("vector", fva)
            va = s_v.bump(1)

            def fso(s, tau=tau, va=va):
                s.wait_ge(s_v.h, va)
                s.dma_start(out=spl("out_fm", TAU * tau, TAU * (tau + 1)),
                            in_=s3(F1[:, 0:8192])).then_inc(s_do.h, 16)
            em("sync", fso)
            prev_out = s_do.bump(16)

    allph = [("setup", ph_setup), ("scale", ph_scale), ("A", ph_A), ("B", ph_B),
             ("C", ph_C), ("AG", ph_AG), ("C2", ph_C2), ("D", ph_D), ("E", ph_E),
             ("F", ph_F)]
    NOFENCE = {"D", "E"}
    for name, fn in allph:
        fn()
        if name not in NOFENCE or name == upto:
            fence()
        if upto is not None and name == upto:
            break

    blk = ctx.enter_context(nc.Block())

    def runner(engine_name):
        def run(eng):
            for fn in steps[engine_name]:
                fn(eng)
        return run

    blk.sync(runner("sync"))
    blk.gpsimd(runner("gpsimd"))
    blk.vector(runner("vector"))
    blk.scalar(runner("scalar"))
    blk.tensor(runner("tensor"))
    ctx.close()
    return nc, D


def kernel(**inputs):
    in_maps, t_core = prep_inputs(inputs)
    nc, D = build_program(t_core)
    res = run_bass_kernel_spmd(nc, in_maps, core_ids=list(range(NCORES)))
    outs = []
    for r in range(NCORES):
        o = np.asarray(res.results[r]["out_fm"])
        outs.append(o.reshape(C, t_core).T)
    full = np.concatenate(outs, axis=0)
    return np.ascontiguousarray(full.reshape(1, NCORES * t_core, C)).astype(np.float32)
